# version 11
# baseline (speedup 1.0000x reference)
import numpy as np
import ml_dtypes

import concourse.bass as bass
import concourse.mybir as mybir
from concourse.ap import AP
from concourse.bass_utils import run_bass_kernel_spmd

F32 = mybir.dt.float32
BF16 = mybir.dt.bfloat16
AF = mybir.ActivationFunctionType
ALU = mybir.AluOpType
AX = mybir.AxisListType

T = 2048
D = 1024
NT = T // 128
NC = D // 128
DCD = 4096
NFC = DCD // 128
H = 16
HD = 64
G = 16
NB = 128
FH = 4
NK = 64
NEXP = 4096
BIGM = 32768.0
EPS_SEL = 1e-6

STAGE = 3


class Plan:
    ENGINES = ("sync", "gpsimd", "vector", "scalar", "tensor")

    def __init__(self, nc):
        self.nc = nc
        self.q = {e: [] for e in self.ENGINES}
        self.counts = {}
        self.selfsems = {}

    def op(self, eng, fn, inc=None, inc_by=1):
        ss = self.selfsems.get(eng)
        if ss is not None:
            prior = self.cur(ss)
            if prior:
                self.q[eng].append(
                    lambda e, s=ss, v=prior: e.wait_ge(s, v))
            if inc is None:
                inc, inc_by = ss, 1
            else:
                assert inc is ss, f"{eng} ops must inc their self-sem"
        tgt = None
        if inc is not None:
            key = id(inc)
            tgt = self.counts.get(key, 0) + inc_by
            self.counts[key] = tgt
            self.q[eng].append(
                lambda e, f=fn, s=inc, v=inc_by: f(e).then_inc(s, v))
        else:
            self.q[eng].append(fn)
        return tgt

    def cur(self, sem):
        return self.counts.get(id(sem), 0)

    def wait(self, eng, sem, val):
        if val:
            self.q[eng].append(lambda e, s=sem, v=val: e.wait_ge(s, v))

    def barrier(self, sems):
        vals = [(s, self.cur(s)) for s in sems]
        for eng in self.ENGINES:
            for s, v in vals:
                self.wait(eng, s, v)

    def emit(self):
        with self.nc.Block() as block:
            for name in self.ENGINES:
                qlist = self.q[name]

                def body(eng, qlist=qlist):
                    for fn in qlist:
                        fn(eng)
                getattr(block, name)(body)


def ts(i, n=128):
    return slice(i * n, (i + 1) * n)


def build_nc(stage=STAGE, debug=0):
    nc = bass.Bass("TRN2", target_bir_lowering=False)

    def din(name, shape, dtype=F32):
        return nc.dram_tensor(name, list(shape), dtype, kind="ExternalInput").ap()

    x_in = din("x", (T, D))
    Wfg = din("Wfg", (D, DCD))
    Wfu = din("Wfu", (D, DCD))
    bfg_c = din("bfg_c", (128, NFC))
    bfu_c = din("bfu_c", (128, NFC))
    Wfd = din("Wfd", (DCD, D))
    bd_rep = din("bd_rep", (128, D))
    ident_bf = din("ident_bf", (128, 128), BF16)
    if stage >= 2:
        Wfq = din("Wfq", (D, 512))
        keys_in = din("keys", (FH * NK * 2, NK))
        d_emb = din("down_embed", (NEXP, D))
        u_emb = din("up_embed", (NEXP, D))
    if stage >= 3:
        Wq_d = din("Wq", (D, D)); Wk_d = din("Wk", (D, D))
        Wv_d = din("Wv", (D, D)); Wo_d = din("Wo", (D, D))
        Wck_d = din("Wck", (D, HD)); Wcv_d = din("Wcv", (D, HD))
        Wg_att = din("Wg_att", (HD, 3))
        bg_att_rep = din("bg_att_rep", (128, 3))
        cok_tm = din("cok_tm", (T, NB), BF16)
        cokT_in = din("cokT", (NB, T), BF16)
        bandmask = din("bandmask", (2, 128, 256), BF16)
        causal64 = din("causal64", (64, 64), BF16)
        bigexp = din("bigexp", (16, 128, 128), BF16)
        ameanT = din("ameanT", (HD, D), BF16)

    out_t = nc.dram_tensor("out", [T, D], F32, kind="ExternalOutput").ap()
    xn_dram = nc.dram_tensor("xn_spill", [128, NC * T], BF16,
                             kind="ExternalOutput" if debug else "Internal").ap()
    if stage >= 2:
        deT_dram = nc.dram_tensor("deT_spill", [D, NEXP], BF16,
                                  kind="Internal").ap()
    if stage >= 3:
        dbg_kind = "ExternalOutput" if debug >= 2 else "Internal"
        q_dram = nc.dram_tensor("q_spill", [128, NC * T], BF16,
                                kind=dbg_kind).ap()
        k_dram = nc.dram_tensor("k_spill", [128, NC * T], BF16,
                                kind=dbg_kind).ap()
        v_dram = nc.dram_tensor("v_spill", [128, NC * T], BF16,
                                kind=dbg_kind).ap()
        mixT_dram = (nc.dram_tensor("mixT_dump", [128, NC * T], BF16,
                                    kind="ExternalOutput").ap()
                     if debug >= 2 else None)
        gates_dram = (nc.dram_tensor("gates_dump", [128, NT * 4], F32,
                                     kind="ExternalOutput").ap()
                      if debug >= 2 else None)
        awg_dram = (nc.dram_tensor("awg_dump", [128, 32], BF16,
                                   kind="ExternalOutput").ap()
                    if debug >= 2 else None)
        bg_dram = (nc.dram_tensor("bg_dump", [128, 16], F32,
                                  kind="ExternalOutput").ap()
                   if debug >= 2 else None)
        gprobe = (nc.dram_tensor("gates_probe", [128, 32], F32,
                                 kind="ExternalOutput").ap()
                  if debug >= 2 else None)
        gprobe2 = (nc.dram_tensor("gates_probe2", [128, 64], F32,
                                  kind="ExternalOutput").ap()
                   if debug >= 2 else None)
        gprobe3 = (nc.dram_tensor("gates_probe3", [128, 32], BF16,
                                  kind="ExternalOutput").ap()
                   if debug >= 2 else None)

    p = Plan(nc)

    def sbuf(name, shape, dtype):
        return nc.alloc_sbuf_tensor(name, list(shape), dtype)

    def psum(name, shape, dtype=F32):
        return nc.alloc_psum_tensor(name, list(shape), dtype)

    identity = sbuf("identity", (128, 128), BF16)
    bdrep_sb = sbuf("bdrep_sb", (128, D), F32)
    biasg_sb = sbuf("biasg_sb", (128, NFC), F32)
    biasu_sb = sbuf("biasu_sb", (128, NFC), F32)
    expacc = sbuf("expacc", (128, NT * D), BF16)
    mixT = sbuf("mixT", (128, NC * T), BF16)
    gates_tm = sbuf("gates_tm", (128, NT * 4), F32)
    thp_cols = sbuf("thp_cols", (128, NT * FH), F32)
    mzl_cols = sbuf("mzl_cols", (128, NT * FH), F32)
    tcol_cols = sbuf("tcol_cols", (128, NT * FH), F32)
    constcol = sbuf("constcol", (128, 4), F32)
    AR16 = 57 * 1024 + 192
    AR32 = 5 * 1024
    arena16 = sbuf("arena16", (128, AR16), BF16)
    arena32 = sbuf("arena32", (128, AR32), F32)

    class Carver:
        def __init__(self):
            self.o16 = 0
            self.o32 = 0

        def c16(self, n):
            v = arena16[:, self.o16:self.o16 + n]
            self.o16 += n
            assert self.o16 <= AR16, f"arena16 overflow {self.o16}"
            return v

        def c32(self, n):
            v = arena32[:, self.o32:self.o32 + n]
            self.o32 += n
            assert self.o32 <= AR32, f"arena32 overflow {self.o32}"
            return v

    pf = [psum(f"pf{i}", (128, 512), F32) for i in range(6)]
    pb16 = [psum(f"pb{i}", (128, 1024), BF16) for i in range(2)]

    sPE = nc.alloc_semaphore("sPE")
    sV = nc.alloc_semaphore("sV")
    sA = nc.alloc_semaphore("sA")
    sG = nc.alloc_semaphore("sG")
    ALLSEMS = [sPE, sV, sA, sG]
    p.selfsems = {"vector": sV, "scalar": sA}

    _streams = []

    def mkstream(name):
        s = nc.alloc_semaphore(name)
        ALLSEMS.append(s)
        _streams.append(s)
        return s

    def dma(eng, stream, outap, inap, **kw):
        return p.op(eng, lambda e: e.dma_start(out=outap, in_=inap, **kw),
                    inc=stream, inc_by=16)

    def mm(ps, lhsT, rhs, start, stop, inc=None):
        return p.op("tensor", lambda e: e.matmul(ps, lhsT, rhs, start=start,
                                                 stop=stop), inc=inc)

    def mm_chain(ps, pairs, inc=None):
        n = len(pairs)
        t = None
        for i, (l, r) in enumerate(pairs):
            t = mm(ps, l, r, i == 0, i == n - 1,
                   inc=inc if i == n - 1 else None)
        return t

    p.op("vector", lambda e: e.memset(constcol[:, 0:1], -4096.0))
    stC = mkstream("stC")
    dma("sync", stC, identity[:], ident_bf[:, :])
    dma("sync", stC, bdrep_sb[:], bd_rep[:, :])
    dma("sync", stC, biasg_sb[:], bfg_c[:, :])
    t_const = dma("sync", stC, biasu_sb[:], bfu_c[:, :])

    cv = Carver()
    x_f32 = [cv.c32(D) for _ in range(2)]
    ln_sq = cv.c32(D)
    ln_st = cv.c32(3 * NT)
    ln_rs = cv.c32(2 * NT)
    ln_xn2 = [cv.c16(D), cv.c16(D)]
    xn_stage = [cv.c16(D) for _ in range(2)]

    stXL = [mkstream("stXL0"), mkstream("stXL1")]
    stSp = [mkstream("stSp0"), mkstream("stSp1")]
    spill_hist = {}
    sum_hist = {}
    for i in range(NT):
        buf = x_f32[i % 2]
        if i >= 2:
            p.wait("sync", sV, sum_hist[i - 2])
        tL = dma("sync", stXL[i % 2], buf, x_in[ts(i), :])
        p.wait("vector", stXL[i % 2], tL)
        p.op("vector", lambda e, b=buf, i=i: e.reduce_sum(
            out=ln_st[:, i:i + 1], in_=b, axis=AX.X))
        p.op("vector", lambda e, b=buf: e.tensor_tensor(
            out=ln_sq, in0=b, in1=b, op=ALU.mult))
        sum_hist[i] = p.op("vector", lambda e, i=i: e.reduce_sum(
            out=ln_st[:, NT + i:NT + i + 1], in_=ln_sq, axis=AX.X))
    p.op("vector", lambda e: e.tensor_scalar(
        out=ln_st[:, 0:2 * NT], in0=ln_st[:, 0:2 * NT], scalar1=1.0 / D,
        scalar2=None, op0=ALU.mult))
    p.op("vector", lambda e: e.tensor_tensor(
        out=ln_st[:, 2 * NT:3 * NT], in0=ln_st[:, 0:NT],
        in1=ln_st[:, 0:NT], op=ALU.mult))
    p.op("vector", lambda e: e.tensor_tensor(
        out=ln_st[:, NT:2 * NT], in0=ln_st[:, NT:2 * NT],
        in1=ln_st[:, 2 * NT:3 * NT], op=ALU.subtract))
    tVar = p.op("vector", lambda e: e.tensor_scalar(
        out=ln_st[:, NT:2 * NT], in0=ln_st[:, NT:2 * NT], scalar1=1e-5,
        scalar2=None, op0=ALU.add), inc=sV)
    p.wait("scalar", sV, tVar)
    tQ = p.op("scalar", lambda e: e.activation(
        out=ln_st[:, 2 * NT:3 * NT], in_=ln_st[:, NT:2 * NT],
        func=AF.Sqrt), inc=sA)
    p.wait("vector", sA, tQ)
    p.op("vector", lambda e: e.reciprocal(
        out=ln_rs[:, 0:NT], in_=ln_st[:, 2 * NT:3 * NT]))
    p.op("vector", lambda e: e.tensor_scalar(
        out=ln_rs[:, NT:2 * NT], in0=ln_st[:, 0:NT], scalar1=-1.0,
        scalar2=None, op0=ALU.mult))
    p.op("vector", lambda e: e.tensor_tensor(
        out=ln_rs[:, NT:2 * NT], in0=ln_rs[:, NT:2 * NT],
        in1=ln_rs[:, 0:NT], op=ALU.mult))
    e0_ev = {}
    if stage >= 2:
        de_in = [cv.c16(1024) for _ in range(2)]
        de_stage = [cv.c16(1024) for _ in range(2)]
        stDE = [mkstream("stDE0"), mkstream("stDE1")]
        stDEo = [mkstream("stDEo0"), mkstream("stDEo1")]
        ev_hist = {}
        st_hist = {}
        p.wait("tensor", stC, t_const)
        for ec in range(32):
            sl = ec % 2
            if ec >= 2:
                p.wait("gpsimd", sPE, ev_hist[ec - 2][0])
            tDE = dma("gpsimd", stDE[sl], de_in[sl], d_emb[ts(ec), :])
            p.wait("tensor", stDE[sl], tDE)
            pbk = pb16[sl]
            if ec >= 2:
                p.wait("tensor", sV, ev_hist[ec - 2][1])
            tT = None
            for c in range(NC):
                tT = p.op("tensor", lambda e, c=c, pbk=pbk, src=de_in[sl]:
                          e.transpose(pbk[:, ts(c)], src[:, ts(c)],
                                      identity[:]),
                          inc=sPE if c == NC - 1 else None)
            p.wait("vector", sPE, tT)
            if ec >= 2:
                p.wait("vector", stDEo[sl], st_hist[ec - 2])
            tEv = p.op("vector", lambda e, d=de_stage[sl], pbk=pbk:
                       e.tensor_copy(out=d, in_=pbk[:, :]))
            ev_hist[ec] = (tT, tEv)
            p.wait("gpsimd", sV, tEv)
            dst = AP(deT_dram.tensor, ec * 128,
                     [[NEXP, 128], [128 * NEXP, NC], [1, 128]])
            st_hist[ec] = dma("gpsimd", stDEo[sl], dst,
                              de_stage[sl].rearrange("p (c f) -> p c f", c=NC))
        e0_ev = {0: ev_hist[30][1], 1: ev_hist[31][1]}

    act_hist = {}
    tr_hist = {}
    p.wait("sync", sV, sum_hist[NT - 1])
    for i in range(NT):
        buf = x_f32[i % 2]
        if i >= 2:
            p.wait("sync", sV, act_hist[i - 2])
        tL = dma("sync", stXL[i % 2], buf, x_in[ts(i), :])
        p.wait("vector", stXL[i % 2], tL)
        if i >= 2:
            p.wait("vector", sPE, tr_hist[i - 2])
        if i < 2 and i in e0_ev:
            p.wait("tensor", sV, e0_ev[i])
        lxn = ln_xn2[i % 2]
        tXN = p.op("vector", lambda e, b=buf, i=i, lxn=lxn: e.tensor_scalar(
            out=lxn, in0=b, scalar1=ln_rs[:, i:i + 1],
            scalar2=ln_rs[:, NT + i:NT + i + 1],
            op0=ALU.mult, op1=ALU.add), inc=sV)
        act_hist[i] = tXN
        pbk = pb16[i % 2]
        p.wait("tensor", sV, tXN)
        p.wait("tensor", stC, t_const)
        if i >= 2:
            p.wait("tensor", sV, spill_hist[i - 2][0])
        tT = None
        for c in range(NC):
            tT = p.op("tensor", lambda e, c=c, pbk=pbk, lxn=lxn: e.transpose(
                pbk[:, ts(c)], lxn[:, ts(c)], identity[:]),
                inc=sPE if c == NC - 1 else None)
        p.wait("vector", sPE, tT)
        stg = xn_stage[i % 2]
        if i >= 2:
            p.wait("vector", stSp[i % 2], spill_hist[i - 2][1])
        tEv = p.op("vector", lambda e, stg=stg, pbk=pbk: e.tensor_copy(
            out=stg, in_=pbk[:, :]), inc=sV)
        p.wait("sync", sV, tEv)
        dstap = AP(xn_dram.tensor, i * 128,
                   [[NC * T, 128], [T, NC], [1, 128]])
        tSp = dma("sync", stSp[i % 2], dstap,
                  stg.rearrange("p (c f) -> p c f", c=NC))
        spill_hist[i] = (tEv, tSp)
        tr_hist[i] = tT

    p.barrier(ALLSEMS)

    if stage >= 2:
        cv = Carver()
        xn_sb = cv.c16(NC * T)
        sims = cv.c16(NT * 512)
        sxm = cv.c16(32)
        _mark16 = cv.o16
        qfst = [cv.c16(T) for _ in range(8)]
        keysq = cv.c16(8 * 64)
        keysT = cv.c16(8 * 64)
        wq_buf = cv.c16(8 * 128)
        c_v8x = cv.c32(8); c_v8y = cv.c32(8); c_osc = cv.c32(64)
        c_v8 = cv.c32(8); c_negm = cv.c32(1); c_ex8 = cv.c32(8)
        c_z = cv.c32(1); c_lnz = cv.c32(1); c_targ = cv.c32(1)
        stXN2 = mkstream("stXN2")
        stK = mkstream("stK")
        stWQ = mkstream("stWQ")
        t_xn = dma("sync", stXN2, xn_sb.rearrange("p (c t) -> p c t", c=NC),
                   AP(xn_dram.tensor, 0, [[NC * T, 128], [T, NC], [1, T]]))
        for ph in range(8):
            pp, hh = ph // 4, ph % 4
            src = AP(keys_in.tensor, (hh * 128 + pp) * NK,
                     [[2 * NK, NK], [1, NK]])
            dma("gpsimd", stK, keysq[0:64, ph * 64:(ph + 1) * 64], src)
        tK = p.cur(stK)
        p.wait("tensor", stK, tK)
        tKT = None
        for ph in range(8):
            tKT = p.op("tensor", lambda e, ph=ph: e.transpose(
                pb16[0][0:64, ph * 64:(ph + 1) * 64],
                keysq[0:64, ph * 64:(ph + 1) * 64],
                identity[0:64, 0:64]), inc=sPE if ph == 7 else None)
        p.wait("vector", sPE, tKT)
        tKE = p.op("vector", lambda e: e.tensor_copy(
            out=keysT[0:64, 0:512], in_=pb16[0][0:64, 0:512]))
        p.wait("tensor", sV, tKE)
        p.wait("tensor", stXN2, t_xn)
        for qc in range(4):
            wsrc = AP(Wfq.tensor, qc * 128,
                      [[512, 128], [128 * 512, NC], [1, 128]])
            tW = dma("gpsimd", stWQ,
                     wq_buf.rearrange("p (k f) -> p k f", k=NC), wsrc)
            p.wait("tensor", stWQ, tW)
            tQ4 = None
            for ph4 in range(4):
                tQ4 = mm_chain(pf[ph4][:, :],
                               [(wq_buf[:, ts(k)],
                                 xn_sb[:, k * T + ph4 * 512:
                                       k * T + ph4 * 512 + 512])
                                for k in range(NC)],
                               inc=sPE if ph4 == 3 else None)
            p.wait("scalar", sPE, tQ4)
            tEq = None
            for hf in range(2):
                for ph4 in range(4):
                    tEq = p.op("scalar", lambda e, qc=qc, hf=hf, ph4=ph4:
                               e.activation(
                                   out=qfst[2 * qc + hf][0:64,
                                       ph4 * 512:ph4 * 512 + 512],
                                   in_=pf[ph4][hf * 64:hf * 64 + 64, :],
                                   func=AF.Copy, scale=1.0))
            p.wait("tensor", sA, tEq)
            p.wait("gpsimd", sPE, tQ4)
        for i in range(NT):
            tSm = None
            for ph in range(8):
                tSm = mm(pf[4 + (i % 2)][:, ph * 64:(ph + 1) * 64],
                         qfst[ph][0:64, ts(i)],
                         keysT[0:64, ph * 64:(ph + 1) * 64],
                         True, True, inc=sPE if ph == 7 else None)
            p.wait("scalar", sPE, tSm)
            tSe = p.op("scalar", lambda e, i=i: e.activation(
                out=sims[:, i * 512:(i + 1) * 512],
                in_=pf[4 + (i % 2)][:, :], func=AF.Copy, scale=1.0))
            p.wait("tensor", sA, tSe)
        p.barrier(ALLSEMS)

        v8all = cv.c32(NT * FH * 8)
        negmall = cv.c32(NT * FH)
        vsh = cv.c32(NT * FH * 8)
        zsall = cv.c32(NT * FH)
        lnzall = cv.c32(NT * FH)
        for i in range(NT):
            base = i * 512
            for h in range(4):
                col = i * FH + h
                sx_ap = sims[:, base + h * 64:base + h * 64 + 64]
                sy_ap = sims[:, base + 256 + h * 64:base + 256 + h * 64 + 64]
                p.op("vector", lambda e, a=sx_ap: e.max(out=c_v8x, in_=a))
                p.op("vector", lambda e, a=sy_ap: e.max(out=c_v8y, in_=a))
                p.op("vector", lambda e: e.tensor_tensor(
                    out=c_osc.rearrange("p (i j) -> p i j", i=8),
                    in0=AP(c_v8x.tensor, c_v8x.offset,
                           [[AR32, 128], [1, 8], [0, 8]]),
                    in1=AP(c_v8y.tensor, c_v8y.offset,
                           [[AR32, 128], [0, 8], [1, 8]]),
                    op=ALU.add))
                p.op("vector", lambda e, col=col: e.max(
                    out=v8all[:, col * 8:col * 8 + 8], in_=c_osc))
        p.op("vector", lambda e: e.tensor_scalar(
            out=negmall, in0=AP(v8all.tensor, v8all.offset,
                                [[AR32, 128], [8, 64]]),
            scalar1=-1.0, scalar2=None, op0=ALU.mult))
        p.op("vector", lambda e: e.tensor_scalar(
            out=thp_cols[:, 0:64],
            in0=AP(v8all.tensor, v8all.offset + 7, [[AR32, 128], [8, 64]]),
            scalar1=EPS_SEL, scalar2=None, op0=ALU.subtract))
        eshb = cv.c32(NT * FH * 8)
        tVsh = p.op("vector", lambda e: e.tensor_tensor(
            out=vsh.rearrange("p (c j) -> p c j", c=64),
            in0=v8all.rearrange("p (c j) -> p c j", c=64),
            in1=AP(negmall.tensor, negmall.offset,
                   [[AR32, 128], [1, 64], [0, 8]]),
            op=ALU.add))
        p.wait("scalar", sV, tVsh)
        tEsh = p.op("scalar", lambda e: e.activation(
            out=eshb, in_=vsh, func=AF.Exp))
        p.wait("vector", sA, tEsh)
        tZs = p.op("vector", lambda e: e.reduce_sum(
            out=zsall.rearrange("p (c u) -> p c u", c=64),
            in_=eshb.rearrange("p (c j) -> p c j", c=64), axis=AX.X))
        p.wait("scalar", sV, tZs)
        tLn = p.op("scalar", lambda e: e.activation(
            out=lnzall, in_=zsall, func=AF.Ln))
        p.wait("vector", sA, tLn)
        p.op("vector", lambda e: e.tensor_tensor(
            out=mzl_cols[:, 0:64], in0=negmall, in1=lnzall,
            op=ALU.subtract))
        tTa = p.op("vector", lambda e: e.tensor_tensor(
            out=vsh[:, 0:64], in0=thp_cols[:, 0:64], in1=mzl_cols[:, 0:64],
            op=ALU.add))
        p.wait("scalar", sV, tTa)
        tTc = p.op("scalar", lambda e: e.activation(
            out=tcol_cols[:, 0:64], in_=vsh[:, 0:64], func=AF.Exp))
        p.wait("vector", sA, tTc)
        p.barrier(ALLSEMS)

        EG = 1024
        NKX = EG // 64
        cv.o16 = _mark16
        deTg2 = [cv.c16(NC * EG), mixT[:, 0:NC * EG]]
        ueg2 = [cv.c16(8 * 1024), mixT[:, NC * EG:NC * EG + 8 * 1024]]
        swt2 = [cv.c16(EG), cv.c16(EG)]
        eexp = cv.c16(EG)
        eyall = cv.c16(NT * 256)
        exe2 = [cv.c16(64), cv.c16(64)]
        emsk2 = [[cv.c16(EG) for _ in range(4)] for _ in range(2)]
        epw = cv.c16(EG)
        epwT = cv.c16(EG)
        stDT = [mkstream("stDT0"), mkstream("stDT1")]
        stUE = [mkstream("stUE0"), mkstream("stUE1")]
        NG = NEXP // EG
        NIT = NG * NT
        tDTh = {}; tUEh = {}
        tW_h = {}; tSW_h = {}; tMsk_h = {}; tEPW_h = {}; tEM_h = {}
        tTe_h = {}; tOut_h = {}; tAcc_h = {}

        def e2_load(gh):
            gp = gh % 2
            if gh >= 2:
                p.wait("gpsimd", sPE, tOut_h[(gh - 1) * NT - 1])
            tDTh[gh] = dma("gpsimd", stDT[gp],
                           deTg2[gp].rearrange("p (k t) -> p k t", k=NC),
                           AP(deT_dram.tensor, gh * EG,
                              [[NEXP, 128], [128 * NEXP, NC], [1, EG]]))
            tUEh[gh] = dma("gpsimd", stUE[gp],
                           ueg2[gp].rearrange("p (k t) -> p k t", k=8),
                           AP(u_emb.tensor, gh * EG * D,
                              [[D, 128], [128 * D, 8], [1, D]]))

        def e2_W(it):
            gh, i = it // NT, it % NT
            b = it % 2
            gp = gh % 2
            if i == 0:
                p.wait("tensor", stDT[gp], tDTh[gh])
                p.wait("tensor", stUE[gp], tUEh[gh])
            if it >= 2:
                p.wait("tensor", sV, tEPW_h[it - 2])
            t = None
            for pc in range(2):
                t = mm_chain(
                    pf[2 * b + pc][:, :],
                    [(xn_sb[:, k * T + i * 128:k * T + i * 128 + 128],
                      deTg2[gp][:, k * EG + pc * 512:k * EG + pc * 512 + 512])
                     for k in range(NC)],
                    inc=sPE if pc == 1 else None)
            tW_h[it] = t

        def e2_chain(it):
            gh, i = it // NT, it % NT
            b = it % 2
            if it >= 2:
                p.wait("scalar", sV, tMsk_h[it - 2])
            tE4 = None
            for h in range(4):
                col = i * FH + h
                sxs_ap = sims[:, i * 512 + h * 64 + NKX * gh:
                              i * 512 + h * 64 + NKX * gh + NKX]
                tE4 = p.op("scalar", lambda e, a=sxs_ap, c=col, h=h, b=b:
                           e.activation(
                               out=exe2[b][:, h * 16:h * 16 + 16],
                               in_=a, func=AF.Exp,
                               bias=mzl_cols[:, c:c + 1]))
            p.wait("scalar", sPE, tW_h[it])
            if it >= 2:
                p.wait("scalar", sV, tEPW_h[it - 2])
            tSW = None
            for pc in range(2):
                tSW = p.op("scalar", lambda e, pc=pc, b=b: e.activation(
                    out=swt2[b][:, pc * 512:pc * 512 + 512],
                    in_=pf[2 * b + pc][:, :], func=AF.Silu))
            tSW_h[it] = tSW
            p.wait("vector", sA, tE4)
            tMsk = None
            for h in range(4):
                col = i * FH + h
                ey_off = eyall.offset + i * 256 + h * 64
                tTT = p.op("vector", lambda e, h=h, b=b, eo=ey_off:
                           e.tensor_tensor(
                               out=eexp.rearrange("p (x y) -> p x y", x=NKX),
                               in0=AP(exe2[b].tensor,
                                      exe2[b].offset + h * 16,
                                      [[AR16, 128], [1, NKX], [0, 64]]),
                               in1=AP(eyall.tensor, eo,
                                      [[AR16, 128], [0, NKX], [1, 64]]),
                               op=ALU.mult))
                tMsk = p.op("vector", lambda e, h=h, c=col, b=b:
                            e.scalar_tensor_tensor(
                                out=emsk2[b][h], in0=eexp,
                                scalar=tcol_cols[:, c:c + 1],
                                in1=eexp, op0=ALU.is_ge, op1=ALU.mult))
            tMsk_h[it] = tMsk

        def e2_EM(it):
            gh, i = it // NT, it % NT
            b = it % 2
            p.wait("tensor", sA, tSW_h[it])
            p.wait("tensor", sV, tMsk_h[it])
            tEM = None
            for pc in range(2):
                for h in range(4):
                    tEM = mm(pf[2 * b + pc][:, :], identity[:],
                             emsk2[b][h][:, pc * 512:pc * 512 + 512],
                             h == 0, h == 3,
                             inc=sPE if (pc == 1 and h == 3) else None)
            tEM_h[it] = tEM

        def e2_tailA(it):
            b = it % 2
            tEM = tEM_h[it]
            p.wait("vector", sPE, tEM)
            tEPW = None
            for pc in range(2):
                tEPW = p.op("vector", lambda e, pc=pc, b=b:
                            e.scalar_tensor_tensor(
                                out=epw[:, pc * 512:pc * 512 + 512],
                                in0=pf[2 * b + pc][:, :], scalar=1.0,
                                in1=swt2[b][:, pc * 512:pc * 512 + 512],
                                op0=ALU.mult, op1=ALU.mult))
            tEPW_h[it] = tEPW
            p.wait("tensor", sV, tEPW)
            if it >= 1:
                p.wait("tensor", sV, tTe_h[it - 1])
            tt = None
            for c in range(NC):
                tt = p.op("tensor", lambda e, c=c: e.transpose(
                    pb16[0][:, ts(c)],
                    epw[:, c * 128:c * 128 + 128],
                    identity[:]), inc=sPE if c == NC - 1 else None)
            p.wait("vector", sPE, tt)
            tTe = p.op("vector", lambda e: e.tensor_copy(
                out=epwT[:, :], in_=pb16[0][:, :]))
            tTe_h[it] = tTe

        def e2_tailB(it):
            gh, i = it // NT, it % NT
            gp = gh % 2
            p.wait("tensor", sV, tTe_h[it])
            if it >= 1:
                p.wait("tensor", sV, tAcc_h[it - 1])
            tOut = None
            for dh in range(2):
                for cc in range(8):
                    tOut = mm(pf[4 + dh][:, :],
                              epwT[:, cc * 128:cc * 128 + 128],
                              ueg2[gp][:, cc * 1024 + dh * 512:
                                       cc * 1024 + dh * 512 + 512],
                              cc == 0, cc == 7,
                              inc=sPE if (dh == 1 and cc == 7) else None)
            tOut_h[it] = tOut
            p.wait("vector", sPE, tOut)
            tAcc = None
            for dh in range(2):
                dst = expacc[:, i * D + dh * 512:i * D + dh * 512 + 512]
                if gh == 0:
                    tAcc = p.op("vector", lambda e, dst=dst, dh=dh:
                                e.tensor_copy(out=dst,
                                              in_=pf[4 + dh][:, :]))
                else:
                    tAcc = p.op("vector", lambda e, dst=dst, dh=dh:
                                e.tensor_tensor(out=dst,
                                                in0=pf[4 + dh][:, :],
                                                in1=dst, op=ALU.add))
            tAcc_h[it] = tAcc

        e2_load(0)
        e2_load(1)
        for i in range(NT):
            p.op("scalar", lambda e, i=i: e.activation(
                out=eyall[:, i * 256:(i + 1) * 256],
                in_=sims[:, i * 512 + 256:i * 512 + 512], func=AF.Exp))
        e2_W(0)
        e2_chain(0)
        for it in range(NIT):
            e2_EM(it)
            if it + 1 < NIT:
                e2_W(it + 1)
            e2_tailA(it)
            if it + 1 < NIT:
                e2_chain(it + 1)
            e2_tailB(it)
            if (it + 1) % NT == 0:
                ghn = (it + 1) // NT + 1
                if ghn < NG:
                    e2_load(ghn)
        p.barrier(ALLSEMS)

    if stage >= 3:
        SC = 0.125
        NEGB = -4096.0
        cv = Carver()
        v_aug = cv.c16(16 * 1056)
        kcT = cv.c16(16 * 128)
        vc_aug = cv.c16(16 * 66)
        awg_sb = cv.c16(24)
        bigexp_sb = cv.c16(16 * 128)
        cok_sb = cv.c16(NB)
        causal_sb = cv.c16(64)
        band_sb = cv.c16(256)
        wck_sb = cv.c16(16 * 64)
        wcv_sb = cv.c16(16 * 64)
        _markA = cv.o16

        stA = mkstream("stA")
        stWA = [mkstream("stWA0"), mkstream("stWA1")]
        stSpQ = [mkstream("stSpQ0"), mkstream("stSpQ1")]
        stSpK = [mkstream("stSpK0"), mkstream("stSpK1")]
        stSpV = [mkstream("stSpV0"), mkstream("stSpV1")]
        stXN3 = mkstream("stXN3")
        stQH = mkstream("stQH")
        stKH = mkstream("stKH")
        stQG = mkstream("stQG")
        stBG = mkstream("stBG")
        stDBG = mkstream("stDBG")
        stDBG2 = mkstream("stDBG2")
        stCok = mkstream("stCok")

        xn_at = cv.c16(NC * T)
        wq_at = [cv.c16(8 * 128) for _ in range(2)]
        pstage = [cv.c16(2048) for _ in range(2)]
        vfm = cv.c16(2048)
        wg3_sb = cv.c16(4)
        ameanT_sb = cv.c16(D)

        dma("gpsimd", stA, bigexp_sb.rearrange("p (a b) -> p a b", a=16),
            AP(bigexp.tensor, 0, [[128, 128], [128 * 128, 16], [1, 128]]))
        dma("gpsimd", stA, causal_sb[0:64, :], causal64[:, :])
        dma("gpsimd", stA, band_sb[:, :], bandmask[1, :, :])
        dma("gpsimd", stA, wg3_sb[0:64, 0:3], Wg_att[:, :])
        dma("gpsimd", stA, ameanT_sb[0:64, :], ameanT[:, :])
        wck_src = AP(Wck_d.tensor, 0, [[HD, 64], [64 * HD, 16], [1, HD]])
        dma("gpsimd", stA, wck_sb[0:64, :].rearrange("p (j c) -> p j c", j=16),
            wck_src)
        wcv_src = AP(Wcv_d.tensor, 0, [[HD, 64], [64 * HD, 16], [1, HD]])
        tA0 = dma("gpsimd", stA,
                  wcv_sb[0:64, :].rearrange("p (j c) -> p j c", j=16), wcv_src)
        t_xn3 = dma("sync", stXN3, xn_at.rearrange("p (c t) -> p c t", c=NC),
                    AP(xn_dram.tensor, 0, [[NC * T, 128], [T, NC], [1, T]]))
        p.wait("tensor", stXN3, t_xn3)

        def proj(wdram, spill_dram, stream_sl, kind):
            for c in range(NC):
                wb = wq_at[c % 2]
                if c >= 2:
                    p.wait("gpsimd", sPE, proj_pe.get(c - 2, 0))
                else:
                    p.wait("gpsimd", sPE, p.cur(sPE))
                tW = dma("gpsimd", stWA[c % 2],
                         wb.rearrange("p (k f) -> p k f", k=NC),
                         AP(wdram.tensor, c * 128,
                            [[D, 128], [128 * D, NC], [1, 128]]))
                p.wait("tensor", stWA[c % 2], tW)
                tMM = None
                for pc in range(4):
                    tMM = mm_chain(
                        pf[pc][:, :],
                        [(wb[:, ts(k)],
                          xn_at[:, k * T + pc * 512:k * T + pc * 512 + 512])
                         for k in range(NC)],
                        inc=sPE if pc == 3 else None)
                proj_pe[c] = tMM
                p.wait("scalar", sPE, tMM)
                if kind == "spill":
                    dst_sb = pstage[c % 2]
                    if (c % 2) in spill_guard:
                        gs, gv = spill_guard[c % 2]
                        p.wait("scalar", gs, gv)
                    tEv = None
                    for pc in range(4):
                        tEv = p.op("scalar", lambda e, pc=pc, d=dst_sb:
                                   e.activation(
                                       out=d[:, pc * 512:pc * 512 + 512],
                                       in_=pf[pc][:, :], func=AF.Copy,
                                       scale=1.0))
                    p.wait("sync", sA, tEv)
                    proj_sp[c] = dma("sync", stream_sl[c % 2],
                                     spill_dram[:, c * T:(c + 1) * T], dst_sb)
                    spill_guard[c % 2] = (stream_sl[c % 2], proj_sp[c])
                    p.wait("tensor", sA, tEv)
                else:
                    tEv = None
                    for pc in range(4):
                        tEv = p.op("scalar", lambda e, pc=pc: e.activation(
                            out=vfm[:, pc * 512:pc * 512 + 512],
                            in_=pf[pc][:, :], func=AF.Copy, scale=1.0))
                    p.wait("tensor", sA, tEv)
                    p.wait("sync", sA, tEv)
                    if c >= 2:
                        p.wait("sync", sV, vau_hist.get(c - 2, 0))
                    proj_sp[c] = dma("sync", stream_sl[c % 2],
                                     spill_dram[:, c * T:(c + 1) * T], vfm)
                    for half8 in range(2):
                        tt = None
                        for kc8 in range(8):
                            kc = half8 * 8 + kc8
                            tt = p.op("tensor", lambda e, kc=kc, kc8=kc8,
                                      half8=half8: e.transpose(
                                          pb16[half8][:, ts(kc8)],
                                          vfm[:, kc * 128:kc * 128 + 128],
                                          identity[:]),
                                      inc=sPE if kc8 == 7 else None)
                        p.wait("vector", sPE, tt)
                        dstv = AP(v_aug.tensor,
                                  v_aug.offset + half8 * 8 * 1056 + c * 2 * 66,
                                  [[AR16, 128], [1056, 8], [66, 2], [1, 64]])
                        tVe = p.op("vector", lambda e, dstv=dstv, half8=half8:
                                   e.tensor_copy(
                                       out=dstv,
                                       in_=pb16[half8].rearrange(
                                           "p (a h f) -> p a h f", a=8, h=2)))
                        p.wait("tensor", sV, tVe)
                    vau_hist[c] = tVe
                    p.wait("scalar", stream_sl[c % 2], proj_sp[c])
                    p.wait("scalar", sPE, proj_pe[c])

        proj_pe = {}
        proj_sp = {}
        spill_guard = {}
        proj(Wq_d, q_dram, stSpQ, "spill")
        proj_pe = {}
        proj_sp = {}
        proj(Wk_d, k_dram, stSpK, "spill")
        proj_pe = {}
        proj_sp = {}
        vau_hist = {}
        proj(Wv_d, v_dram, stSpV, "v")
        tOnes = p.op("vector", lambda e: e.memset(
            AP(v_aug.tensor, v_aug.offset + 64,
               [[AR16, 128], [1056, 16], [66, 16]]), 1.0))

        p.barrier(ALLSEMS)
        bgatt_sb = cv.c32(4)
        glog = cv.c32(4)
        gsc = cv.c32(8)
        qg_buf = cv.c16(8 * 128)
        tBg = dma("sync", stBG, bgatt_sb[:, 0:3], bg_att_rep[:, :])
        tAW = None
        for c in range(NC):
            tAW = mm(pf[0][:, c * 3:c * 3 + 3],
                     ameanT_sb[0:64, ts(c)], wg3_sb[0:64, 0:3],
                     True, True, inc=sPE if c == NC - 1 else None)
        p.wait("scalar", sPE, tAW)
        tAWe = p.op("scalar", lambda e: e.activation(
            out=awg_sb[:, 0:24], in_=pf[0][:, 0:24], func=AF.Copy, scale=1.0))
        p.wait("tensor", sA, tAWe)
        if debug >= 2:
            p.wait("sync", sA, tAWe)
            dma("sync", stDBG, awg_dram[:, 0:24], awg_sb[:, 0:24])
            p.wait("sync", stBG, tBg)
            dma("sync", stDBG, bg_dram[:, 0:3], bgatt_sb[:, 0:3])
        gm_hist = 0
        for i in range(NT):
            p.wait("sync", sPE, gm_hist)
            tQG = dma("sync", stQG, qg_buf.rearrange("p (c f) -> p c f", c=NC),
                      AP(q_dram.tensor, i * 128,
                         [[NC * T, 128], [T, NC], [1, 128]]))
            p.wait("tensor", stQG, tQG)
            tGm = None
            for c in range(NC):
                tGm = mm(pf[1][:, 0:3], qg_buf[:, ts(c)],
                         awg_sb[:, c * 3:c * 3 + 3], c == 0, c == NC - 1,
                         inc=sPE if c == NC - 1 else None)
            gm_hist = tGm
            p.wait("vector", sPE, tGm)
            p.wait("vector", stBG, tBg)
            p.op("vector", lambda e: e.tensor_tensor(
                out=glog[:, 0:3], in0=pf[1][:, 0:3], in1=bgatt_sb[:, 0:3],
                op=ALU.add))
            p.op("vector", lambda e: e.reduce_max(
                out=gsc[:, 0:1], in_=glog[:, 0:3], axis=AX.X))
            tGn = p.op("vector", lambda e: e.tensor_scalar(
                out=gsc[:, 1:2], in0=gsc[:, 0:1], scalar1=-1.0, scalar2=None,
                op0=ALU.mult))
            p.wait("scalar", sV, tGn)
            tGe = p.op("scalar", lambda e: e.activation(
                out=gsc[:, 2:5], in_=glog[:, 0:3], func=AF.Exp,
                bias=gsc[:, 1:2], accum_out=gsc[:, 5:6]))
            p.wait("vector", sA, tGe)
            p.op("vector", lambda e: e.reciprocal(
                out=gsc[:, 6:7], in_=gsc[:, 5:6]))
            tGf = p.op("vector", lambda e, i=i: e.tensor_scalar(
                out=gates_tm[:, i * 4:i * 4 + 3], in0=gsc[:, 2:5],
                scalar1=gsc[:, 6:7], scalar2=None, op0=ALU.mult))
            if debug >= 2:
                if i % 2 == 0:
                    tGf = p.op("vector", lambda e, i=i: e.tensor_copy(
                        out=gates_tm[:, i * 4 + 3:i * 4 + 4],
                        in_=pf[1][:, 0:1]))
                else:
                    tGf = p.op("vector", lambda e, i=i: e.tensor_copy(
                        out=gates_tm[:, i * 4 + 3:i * 4 + 4],
                        in_=bgatt_sb[:, 0:1]))
            p.wait("tensor", sV, tGf)
            if debug >= 2 and i == 0:
                p.wait("sync", sV, p.cur(sV))
                p.wait("sync", sA, p.cur(sA))
                dma("sync", stDBG, bg_dram[:, 4:8], glog[:, 0:4],
                    allow_slow_non_contiguous=True)
                dma("sync", stDBG, bg_dram[:, 8:16], gsc[:, 0:8],
                    allow_slow_non_contiguous=True)
                p.wait("vector", stDBG, p.cur(stDBG))
        if debug >= 2:
            p.wait("sync", sV, p.cur(sV))
            dma("sync", stDBG, bg_dram[:, 3:4], bgatt_sb[:, 0:1],
                allow_slow_non_contiguous=True)
            dma("sync", stDBG, gprobe[:, 0:4], gates_tm[:, 0:4])

        p.barrier(ALLSEMS)
        cv.o16 = _markA
        cv.o32 = 0
        qh = cv.c16(T)
        kh = cv.c16(T)
        khblk = cv.c16(T)
        vct = cv.c16(128)
        kc_hist = 0
        vc_hist = 0
        for h in range(H):
            p.wait("sync", sPE, kc_hist)
            tKH = dma("sync", stKH, kh[0:64, :],
                      k_dram[(h % 2) * 64:(h % 2) * 64 + 64,
                             (h // 2) * T:(h // 2) * T + T])
            p.wait("vector", stKH, tKH)
            p.wait("vector", sPE, kc_hist)
            tGk = p.op("vector", lambda e: e.tensor_copy(
                out=khblk[0:64, :].rearrange("p (j n) -> p j n", j=16),
                in_=AP(kh.tensor, kh.offset, [[AR16, 64], [1, 16], [16, 128]])))
            p.wait("tensor", sV, tGk)
            p.wait("tensor", stA, tA0)
            tKc = None
            for j in range(16):
                tKc = mm(pf[0][0:64, 0:128],
                         wck_sb[0:64, j * 64:j * 64 + 64],
                         khblk[0:64, j * 128:j * 128 + 128],
                         j == 0, j == 15, inc=sPE if j == 15 else None)
            p.wait("scalar", sPE, tKc)
            tKce = p.op("scalar", lambda e, h=h: e.activation(
                out=kcT[0:64, h * 128:h * 128 + 128], in_=pf[0][0:64, 0:128],
                func=AF.Copy, scale=1.0))
            p.wait("tensor", sA, tKce)
            kc_hist = tKc
            p.wait("sync", sPE, vc_hist)
            tVH = dma("sync", stQH, qh[0:64, :],
                      v_dram[(h % 2) * 64:(h % 2) * 64 + 64,
                             (h // 2) * T:(h // 2) * T + T])
            p.wait("tensor", stQH, tVH)
            tVc = None
            for j in range(16):
                tVc = mm(pf[1][0:64, 0:128],
                         wcv_sb[0:64, j * 64:j * 64 + 64],
                         AP(qh.tensor, qh.offset + j, [[AR16, 64], [16, 128]]),
                         j == 0, j == 15, inc=sPE if j == 15 else None)
            vc_hist = tVc
            p.wait("scalar", sPE, tVc)
            tVe = p.op("scalar", lambda e: e.activation(
                out=vct[0:64, :], in_=pf[1][0:64, 0:128], func=AF.Copy,
                scale=1.0))
            p.wait("tensor", sA, tVe)
            tVt = p.op("tensor", lambda e: e.transpose(
                pb16[0][:, 0:64], vct[0:64, :], identity[0:64, 0:64]),
                inc=sPE)
            p.wait("vector", sPE, tVt)
            tVce = p.op("vector", lambda e, h=h: e.tensor_copy(
                out=vc_aug[:, h * 66:h * 66 + 64], in_=pb16[0][:, 0:64]))
            p.wait("tensor", sV, tVce)
        tOne2 = p.op("vector", lambda e: e.memset(
            AP(vc_aug.tensor, vc_aug.offset + 64,
               [[AR16, 128], [66, 16]]), 1.0))
        p.barrier(ALLSEMS)
        if debug >= 2:
            dma("sync", stDBG, gprobe[:, 4:8], gates_tm[:, 0:4])
            p.wait("sync", stDBG, p.cur(stDBG))

        qe2 = [cv.c16(768), cv.c16(768)]
        kh2 = [kh, cv.c16(T)]
        cok_sb = cv.c16(NT * NB)
        cokT_sb = cv.c16(T)
        mtile2 = [cv.c16(128), cv.c16(128)]
        mT_sl = cv.c16(512)
        ecT_sl = cv.c16(512)
        elb = cv.c16(5 * 256)
        euT = cv.c16(16 * 512)
        mixq = cv.c16(4 * 1024)
        a_th4 = cv.c32(1)
        a_v8 = cv.c32(8)
        outc = cv.c32(200)
        z3s = cv.c32(8)
        tmp64 = cv.c32(64)
        if debug >= 2:
            dma("sync", stDBG, gprobe3[:, 30:32], bigexp_sb[:, 0:2])
        tCk = dma("gpsimd", stCok, cok_sb.rearrange("p (i n) -> p i n", i=NT),
                  AP(cok_tm.tensor, 0, [[NB, 128], [128 * NB, NT], [1, NB]]))
        tCkT = dma("gpsimd", stCok, cokT_sb[:, :], cokT_in[:, :])
        p.wait("tensor", stCok, tCkT)
        stKA = [mkstream("stKA0"), mkstream("stKA1")]
        stQA = [mkstream("stQA0"), mkstream("stQA1")]
        NGH = 4 * H
        tKH_h = {}; tQH_h = {}; tSlab_h = {}; tOc_h = {}
        tEc_h = {}; tMe_h = {}; tMix_h = {}
        tMTe_hist = []
        slab_ctr = [0]
        slab_ev = {}
        slab_mm = {}

        def a2_load(g):
            if g >= NGH:
                return
            tq, h = g // H, g % H
            gp = g % 2
            t0 = tq * 512
            if g >= 2:
                p.wait("sync", sPE, tSlab_h[g - 2])
            tKH_h[g] = dma("sync", stKA[gp], kh2[gp][0:64, :],
                           k_dram[(h % 2) * 64:(h % 2) * 64 + 64,
                                  (h // 2) * T:(h // 2) * T + T])
            qlo = max(t0 - 128, 0)
            qhi = min(t0 + 640, T)
            tQH_h[g] = dma("sync", stQA[gp],
                           qe2[gp][0:64, 128 - (t0 - qlo):
                                   128 - (t0 - qlo) + (qhi - qlo)],
                           q_dram[(h % 2) * 64:(h % 2) * 64 + 64,
                                  (h // 2) * T + qlo:(h // 2) * T + qhi])

        def a2_head(g):
            tq, h = g // H, g % H
            hp = g % 2
            t0 = tq * 512
            qe = qe2[hp]
            kh_ = kh2[hp]
            p.wait("tensor", stKA[hp], tKH_h[g])
            p.wait("tensor", stQA[hp], tQH_h[g])
            if g >= 1:
                p.wait("tensor", sV, tOc_h[g - 1][3])
            tSb_h = {}
            tMb = {}
            tMt = {}

            def emit_pair(ii):
                i = tq * 4 + ii
                bank = pf[2 + ii % 2]
                if ii >= 2:
                    p.wait("tensor", sV, tMb[ii - 2])
                mm(bank[:, 0:128],
                   qe[0:64, 128 + ii * 128:128 + ii * 128 + 128],
                   kcT[0:64, h * 128:h * 128 + 128], True, False)
                tSb_h[ii] = mm(bank[:, 0:128],
                               identity[:], cok_sb[:, i * NB:i * NB + NB],
                               False, True, inc=sPE)

            def emit_chain(ii):
                i = tq * 4 + ii
                bank = pf[2 + ii % 2]
                p.wait("vector", sPE, tSb_h[ii])
                p.op("vector", lambda e, bank=bank: e.max(
                    out=a_v8, in_=bank[:, 0:128]))
                p.op("vector", lambda e: e.tensor_scalar(
                    out=a_th4, in0=a_v8[:, 3:4], scalar1=1e-5,
                    scalar2=None, op0=ALU.subtract))
                if ii >= 2:
                    p.wait("vector", sPE, tMt[ii - 2])
                tMb[ii] = p.op("vector", lambda e, ii=ii, bank=bank:
                               e.tensor_scalar(
                                   out=mtile2[ii % 2], in0=bank[:, 0:128],
                                   scalar1=a_th4, scalar2=None,
                                   op0=ALU.is_ge))
                if i == 0:
                    p.op("vector", lambda e: e.memset(
                        mtile2[0][0:64, 0:4], 1.0))
                    tMb[ii] = p.op("vector", lambda e: e.memset(
                        mtile2[0][0:64, 4:128], 0.0))

            def emit_tr(j):
                p.wait("tensor", sV, tMb[j])
                if j == 0 and g >= 1:
                    p.wait("tensor", sV, tMe_h[g - 1])
                if j == 0 and h == 0 and tq > 0:
                    for tk in tMTe_hist[-2:]:
                        p.wait("tensor", sV, tk)
                tMt[j] = p.op("tensor", lambda e, j=j: e.transpose(
                    pb16[0][:, j * 128:j * 128 + 128], mtile2[j % 2],
                    identity[:]), inc=sPE)

            emit_pair(0)
            emit_pair(1)
            emit_chain(0)
            emit_chain(1)
            emit_tr(0)
            emit_pair(2)
            emit_chain(2)
            emit_tr(1)
            emit_pair(3)
            emit_chain(3)
            emit_tr(2)
            emit_tr(3)
            p.wait("vector", sPE, tMt[3])
            tMe = p.op("vector", lambda e: e.tensor_copy(
                out=mT_sl, in_=pb16[0][:, 0:512]))
            tMe_h[g] = tMe
            bB = pf[4 + g % 2]
            if g >= 2:
                p.wait("tensor", sA, tEc_h[g - 2])
            tSbT = mm(bB[:, :], kcT[0:64, h * 128:h * 128 + 128],
                      qe[0:64, 128:640], True, False)
            tSbT = mm(bB[:, :], identity[:],
                      cokT_sb[:, t0:t0 + 512], False, True, inc=sPE)
            p.wait("scalar", sPE, tSbT)
            tEc = p.op("scalar", lambda e, bB=bB: e.activation(
                out=ecT_sl, in_=bB[:, :], func=AF.Exp, scale=SC))
            tEc_h[g] = tEc
            p.wait("tensor", sV, tMe)
            slabs = [("eu", kc) for kc in range(4 * tq + 4)]
            slabs += [("lb", kc) for kc in range(max(0, 4 * tq - 1),
                                                 4 * tq + 4)]
            tCorner = None
            tEvLast = None
            tSlab = None
            for kind, kc in slabs:
                sc_i = slab_ctr[0]
                slab_ctr[0] += 1
                bank = pf[sc_i % 2]
                if sc_i >= 2:
                    p.wait("tensor", sA, slab_ev[sc_i - 2])
                if kind == "eu":
                    mm(bank[:, :], kh_[0:64, kc * 128:kc * 128 + 128],
                       qe[0:64, 128:640], True, False)
                    tSlab = mm(bank[:, :],
                               bigexp_sb[:, kc * 128:kc * 128 + 128],
                               mT_sl, False, True, inc=sPE)
                    p.wait("scalar", sPE, tSlab)
                    tEv2 = p.op("scalar", lambda e, kc=kc, bank=bank:
                                e.activation(
                                    out=euT[:, kc * 512:kc * 512 + 512],
                                    in_=bank[:, :], func=AF.Exp, scale=SC,
                                    bias=constcol[:, 0:1]))
                    if kc == 0 and tq == 0:
                        p.wait("vector", sA, tEv2)
                        tCorner = p.op("vector", lambda e: e.tensor_tensor(
                            out=euT[0:64, 0:64], in0=euT[0:64, 0:64],
                            in1=causal_sb[0:64, 0:64], op=ALU.mult))
                        p.wait("scalar", sV, tCorner)
                else:
                    wlo = kc * 128
                    whi = min(kc * 128 + 256, T)
                    wid = whi - wlo
                    qcol = 128 + (wlo - t0)
                    slot = kc % 5
                    mm(bank[:, 0:wid], kh_[0:64, kc * 128:kc * 128 + 128],
                       qe[0:64, qcol:qcol + wid], True, False)
                    tSlab = mm(bank[:, 0:wid], identity[:],
                               band_sb[:, 0:wid], False, True, inc=sPE)
                    p.wait("scalar", sPE, tSlab)
                    tEv2 = p.op("scalar", lambda e, slot=slot, wid=wid,
                                bank=bank: e.activation(
                                    out=elb[:, slot * 256:slot * 256 + wid],
                                    in_=bank[:, 0:wid], func=AF.Exp,
                                    scale=SC))
                slab_ev[sc_i] = tEv2
                tEvLast = tEv2
            tSlab_h[g] = tSlab
            a2_load(g + 2)
            p.wait("tensor", sA, tEvLast)
            if tCorner is not None:
                p.wait("tensor", sV, tCorner)
            tOc_h[g] = {}
            for ii in range(4):
                i = tq * 4 + ii
                bS = pf[2 + ii % 2]
                bC = pf[4 + ii % 2]
                if ii >= 2:
                    p.wait("tensor", sV, tOc_h[g][ii - 2])
                tSel = None
                for kc in range(i + 1):
                    tSel = mm(bS[:, 0:65],
                              euT[:, kc * 512 + ii * 128:
                                  kc * 512 + ii * 128 + 128],
                              v_aug[:, kc * 1056 + h * 66:
                                    kc * 1056 + h * 66 + 65],
                              kc == 0, kc == i, inc=sPE if kc == i else None)
                tCmp = mm(bC[:, 0:65],
                          ecT_sl[:, ii * 128:ii * 128 + 128],
                          vc_aug[:, h * 66:h * 66 + 65],
                          True, True, inc=sPE)
                tLoc = mm(bC[:, 128:193],
                          elb[:, (i % 5) * 256:(i % 5) * 256 + 128],
                          v_aug[:, i * 1056 + h * 66:
                                i * 1056 + h * 66 + 65],
                          True, i == 0, inc=sPE if i == 0 else None)
                if i > 0:
                    tLoc = mm(bC[:, 128:193],
                              elb[:, ((i - 1) % 5) * 256 + 128:
                                  ((i - 1) % 5) * 256 + 256],
                              v_aug[:, (i - 1) * 1056 + h * 66:
                                    (i - 1) * 1056 + h * 66 + 65],
                              False, True, inc=sPE)
                p.wait("vector", sPE, tLoc)
                p.op("vector", lambda e, bC=bC: e.tensor_copy(
                    out=outc[:, 0:65], in_=bC[:, 0:65]))
                p.op("vector", lambda e, bS=bS: e.tensor_copy(
                    out=outc[:, 65:130], in_=bS[:, 0:65]))
                tOc = p.op("vector", lambda e, bC=bC: e.tensor_copy(
                    out=outc[:, 130:195], in_=bC[:, 128:193]))
                zview = AP(outc.tensor, outc.offset + 64,
                           [[AR32, 128], [65, 3]])
                p.op("vector", lambda e, z=zview: e.tensor_scalar(
                    out=z3s[:, 0:3], in0=z, scalar1=1e-30, scalar2=None,
                    op0=ALU.add))
                p.op("vector", lambda e: e.reciprocal(
                    out=z3s[:, 3:6], in_=z3s[:, 0:3]))
                p.op("vector", lambda e, i=i: e.tensor_tensor(
                    out=z3s[:, 0:3], in0=z3s[:, 3:6],
                    in1=gates_tm[:, i * 4:i * 4 + 3], op=ALU.mult))
                p.op("vector", lambda e: e.tensor_scalar(
                    out=tmp64, in0=outc[:, 0:64], scalar1=z3s[:, 0:1],
                    scalar2=None, op0=ALU.mult))
                p.op("vector", lambda e: e.scalar_tensor_tensor(
                    out=tmp64, in0=outc[:, 65:129], scalar=z3s[:, 1:2],
                    in1=tmp64, op0=ALU.mult, op1=ALU.add))
                tMix = p.op("vector", lambda e, ii=ii, h=h:
                            e.scalar_tensor_tensor(
                                out=mixq[:, ii * 1024 + h * 64:
                                         ii * 1024 + h * 64 + 64],
                                in0=outc[:, 130:194], scalar=z3s[:, 2:3],
                                in1=tmp64, op0=ALU.mult, op1=ALU.add))
                tOc_h[g][ii] = tOc
                tMix_h[g] = tMix

        a2_load(0)
        a2_load(1)
        for g in range(NGH):
            a2_head(g)
            if g % H == H - 1:
                tq = g // H
                p.wait("tensor", sV, tMix_h[g])
                tMTe_q = {}
                for ii in range(4):
                    i = tq * 4 + ii
                    if ii >= 2:
                        p.wait("tensor", sV, tMTe_q[ii - 2])
                    tTr = None
                    for c in range(NC):
                        tTr = p.op("tensor", lambda e, ii=ii, c=c:
                                   e.transpose(
                                       pb16[ii % 2][:, ts(c)],
                                       mixq[:, ii * 1024 + c * 128:
                                            ii * 1024 + c * 128 + 128],
                                       identity[:]),
                                   inc=sPE if c == NC - 1 else None)
                    p.wait("vector", sPE, tTr)
                    dstm = AP(mixT[:].tensor, i * 128,
                              [[NC * T, 128], [T, NC], [1, 128]])
                    tMTe = p.op("vector", lambda e, dstm=dstm, ii=ii:
                                e.tensor_copy(
                                    out=dstm, in_=pb16[ii % 2].rearrange(
                                        "p (c f) -> p c f", c=NC)))
                    tMTe_q[ii] = tMTe
                    tMTe_hist.append(tMTe)
        p.barrier(ALLSEMS)

    cv = Carver()
    hT = cv.c16(NFC * 1024)
    _markM = cv.o16
    xn_half = cv.c16(NC * 1024)
    wbuf = [cv.c16(8 * 128) for _ in range(4)]
    gsil = cv.c16(1024)
    cv.o16 = _markM
    wdres2 = [cv.c16(16 * 512), cv.c16(16 * 512)]
    wores = cv.c16(NC * 512)
    xq_f32 = [cv.c32(512) for _ in range(2)]
    out_sb = [cv.c32(512) for _ in range(4)]

    stXnH = mkstream("stXnH")
    stWo2 = mkstream("stWo2")
    stWg = [mkstream("stWg0"), mkstream("stWg1")]
    stWu = [mkstream("stWu0"), mkstream("stWu1")]
    stWd = mkstream("stWd")
    stXQ = [mkstream("stXQ0"), mkstream("stXQ1")]
    stOut = [mkstream(f"stOut{i}") for i in range(4)]
    store_hist = {}
    add_hist = {}
    for half in range(2):
        p.barrier(ALLSEMS)
        xn_src = AP(xn_dram.tensor, half * 1024,
                    [[NC * T, 128], [T, NC], [1, 1024]])
        t_xnh = dma("sync", stXnH, xn_half.rearrange("p (c t) -> p c t", c=NC),
                    xn_src)
        pe_hist = {}
        ht_hist = {}
        for fc in range(NFC):
            bgw = wbuf[2 * (fc % 2)]
            buw = wbuf[2 * (fc % 2) + 1]
            if fc >= 2:
                p.wait("gpsimd", sPE, pe_hist[fc - 2])
            wsrc_g = AP(Wfg.tensor, fc * 128,
                        [[DCD, 128], [128 * DCD, NC], [1, 128]])
            wsrc_u = AP(Wfu.tensor, fc * 128,
                        [[DCD, 128], [128 * DCD, NC], [1, 128]])
            tWg = dma("gpsimd", stWg[fc % 2],
                      bgw.rearrange("p (k f) -> p k f", k=NC), wsrc_g)
            tWu = dma("gpsimd", stWu[fc % 2],
                      buw.rearrange("p (k f) -> p k f", k=NC), wsrc_u)
            p.wait("tensor", stWg[fc % 2], tWg)
            p.wait("tensor", stWu[fc % 2], tWu)
            if fc == 0:
                p.wait("tensor", stXnH, t_xnh)
                p.wait("tensor", stC, t_const)
            for ph in range(2):
                mm_chain(pf[0 + ph][:, :],
                         [(bgw[:, ts(k)],
                           xn_half[:, k * 1024 + ph * 512:
                                   k * 1024 + ph * 512 + 512])
                          for k in range(NC)])
            tU = None
            for ph in range(2):
                tU = mm_chain(pf[2 + ph][:, :],
                              [(buw[:, ts(k)],
                                xn_half[:, k * 1024 + ph * 512:
                                        k * 1024 + ph * 512 + 512])
                               for k in range(NC)],
                              inc=sPE if ph == 1 else None)
            pe_hist[fc] = tU
            p.wait("scalar", sPE, tU)
            if fc >= 1:
                p.wait("scalar", sV, ht_hist[fc - 1])
            tSg = None
            for ph in range(2):
                tSg = p.op("scalar", lambda e, ph=ph, fc=fc: e.activation(
                    out=gsil[:, ph * 512:ph * 512 + 512], in_=pf[ph][:, :],
                    func=AF.Silu, bias=biasg_sb[:, fc:fc + 1]),
                    inc=sA if ph == 1 else None)
            p.wait("vector", sA, tSg)
            tHT = None
            for ph in range(2):
                tHT = p.op("vector", lambda e, ph=ph, fc=fc:
                           e.scalar_tensor_tensor(
                               out=hT[:, fc * 1024 + ph * 512:
                                      fc * 1024 + ph * 512 + 512],
                               in0=pf[2 + ph][:, :],
                               scalar=biasu_sb[:, fc:fc + 1],
                               in1=gsil[:, ph * 512:ph * 512 + 512],
                               op0=ALU.add, op1=ALU.mult),
                           inc=sV if ph == 1 else None)
            ht_hist[fc] = tHT
            p.wait("tensor", sA, tSg)
            p.wait("tensor", sV, tHT)

        if half == 0:
            t_wdc = {}
            wdA_last = {}
            wdB_last = {}
            stWdc = [mkstream("stWdc0"), mkstream("stWdc1")]

        def load_wd(gdh):
            dh_ = gdh % 2
            p.wait("gpsimd", sV, ht_hist[NFC - 1])
            for c in range(2):
                if gdh >= 1:
                    tok = wdA_last[gdh - 1] if c == 0 else wdB_last[gdh - 1]
                    p.wait("gpsimd", sPE, tok)
                wd_src = AP(Wfd.tensor, c * 16 * 128 * D + dh_ * 512,
                            [[D, 128], [128 * D, 16], [1, 512]])
                t_wdc[(gdh, c)] = dma("gpsimd", stWdc[c],
                                      wdres2[c].rearrange(
                                          "p (k f) -> p k f", k=16), wd_src)

        for dh in range(2):
            gdh = half * 2 + dh
            dcols = slice(dh * 512, dh * 512 + 512)
            if dh == 0:
                load_wd(gdh)
            p.wait("tensor", stWdc[0], t_wdc[(gdh, 0)])
            if stage >= 3:
                wo_src = AP(Wo_d.tensor, dh * 512,
                            [[D, 128], [128 * D, NC], [1, 512]])
                if gdh >= 1:
                    p.wait("gpsimd", sPE, wdB_last[gdh - 1])
                t_wo = dma("gpsimd", stWo2,
                           wores.rearrange("p (k f) -> p k f", k=NC), wo_src)
            for tile in range(8):
                gt = half * 8 + tile
                pq = pf[4 + tile % 2]
                tA = None
                for fc in range(16):
                    tA = mm(pq[:, 0:512],
                            hT[:, fc * 1024 + tile * 128:
                               fc * 1024 + tile * 128 + 128],
                            wdres2[0][:, fc * 512:fc * 512 + 512],
                            fc == 0, False,
                            inc=sPE if fc == 15 else None)
                if tile == 7:
                    wdA_last[gdh] = tA
                if tile == 0:
                    p.wait("tensor", stWdc[1], t_wdc[(gdh, 1)])
                    if stage >= 3:
                        p.wait("tensor", stWo2, t_wo)
                tMM = None
                for fc in range(16, NFC):
                    last = (stage < 3 and fc == NFC - 1)
                    tMM = mm(pq[:, 0:512],
                             hT[:, fc * 1024 + tile * 128:
                                fc * 1024 + tile * 128 + 128],
                             wdres2[1][:, (fc - 16) * 512:
                                       (fc - 16) * 512 + 512],
                             False, last, inc=sPE if last else None)
                if stage >= 3:
                    for c in range(NC):
                        tMM = mm(pq[:, 0:512],
                                 mixT[:, c * T + gt * 128:
                                      c * T + gt * 128 + 128],
                                 wores[:, c * 512:c * 512 + 512],
                                 False, c == NC - 1,
                                 inc=sPE if c == NC - 1 else None)
                if tile == 7:
                    wdB_last[gdh] = tMM
                    if dh == 0:
                        load_wd(gdh + 1)
                xb = xq_f32[tile % 2]
                p.wait("sync", sV, add_hist.get(tile % 2, 0))
                tX = dma("sync", stXQ[tile % 2], xb, x_in[ts(gt), dcols])
                obi = tile % 2 + 2 * (dh % 2)
                ob = out_sb[obi]
                p.wait("vector", sPE, tMM)
                p.wait("vector", stXQ[tile % 2], tX)
                p.wait("vector", stOut[obi], store_hist.get(obi, 0))
                tAdd = p.op("vector", lambda e, pq=pq, xb=xb, ob=ob:
                            e.tensor_tensor(out=ob, in0=pq[:, 0:512], in1=xb,
                                            op=ALU.add), inc=sV)
                add_hist[tile % 2] = tAdd
                if stage >= 2:
                    p.op("vector", lambda e, ob=ob, gt=gt, dh=dh:
                         e.tensor_tensor(
                             out=ob, in0=ob,
                             in1=expacc[:, gt * D + dh * 512:
                                        gt * D + dh * 512 + 512],
                             op=ALU.add))
                bdap = bdrep_sb[:, dcols]
                tO = p.op("vector", lambda e, ob=ob, bdap=bdap:
                          e.tensor_tensor(out=ob, in0=ob, in1=bdap,
                                          op=ALU.add), inc=sV)
                p.wait("tensor", sV, tO)
                p.wait("sync", sV, tO)
                store_hist[obi] = dma("sync", stOut[obi],
                                      out_t[ts(gt), dcols], ob)

    for st in (stOut + [stSp[0], stSp[1]]):
        p.wait("sync", st, p.cur(st))
    p.emit()
    return nc



def _shard_inputs(inputs, core, stage=STAGE):
    d = {}
    d["x"] = np.ascontiguousarray(np.asarray(inputs["L0_attn_x"])[core])
    d["Wfg"] = np.asarray(inputs["L0_ffn_Wg"])
    d["Wfu"] = np.asarray(inputs["L0_ffn_Wu"])
    d["bfg_c"] = np.ascontiguousarray(
        np.asarray(inputs["L0_ffn_bg"]).reshape(NFC, 128).T)
    d["bfu_c"] = np.ascontiguousarray(
        np.asarray(inputs["L0_ffn_bu"]).reshape(NFC, 128).T)
    d["Wfd"] = np.asarray(inputs["L0_ffn_Wd"])
    d["bd_rep"] = np.ascontiguousarray(
        np.broadcast_to(np.asarray(inputs["L0_ffn_bd"])[None, :], (128, D)))
    d["ident_bf"] = np.eye(128, dtype=ml_dtypes.bfloat16)
    if stage >= 2:
        d["Wfq"] = np.asarray(inputs["L0_ffn_Wq"])
        d["keys"] = np.ascontiguousarray(
            np.asarray(inputs["L0_ffn_keys"]).reshape(FH * NK * 2, NK))
        d["down_embed"] = np.asarray(inputs["L0_ffn_down_embed"])
        d["up_embed"] = np.asarray(inputs["L0_ffn_up_embed"])
    if stage >= 3:
        d["Wq"] = np.asarray(inputs["L0_attn_Wq"])
        d["Wk"] = np.asarray(inputs["L0_attn_Wk"])
        d["Wv"] = np.asarray(inputs["L0_attn_Wv"])
        d["Wo"] = np.asarray(inputs["L0_attn_Wo"])
        d["Wck"] = np.asarray(inputs["L0_attn_Wck"])
        d["Wcv"] = np.asarray(inputs["L0_attn_Wcv"])
        d["Wg_att"] = np.asarray(inputs["L0_attn_Wg"])
        d["bg_att_rep"] = np.ascontiguousarray(np.broadcast_to(
            np.asarray(inputs["L0_attn_bg"])[None, :], (128, 3)))
        pos = np.arange(T)
        blk = np.arange(NB)
        cok = np.where(pos[:, None] >= (blk[None, :] + 1) * G, 0.0, -BIGM)
        d["cok_tm"] = cok.astype(ml_dtypes.bfloat16)
        koff = np.arange(128)
        toff = np.arange(256)
        dtk = toff[None, :] - koff[:, None]
        bm = np.where((dtk >= 0) & (dtk <= 32), 0.0, -BIGM).astype(np.float32)
        bmv = np.stack([bm, bm], axis=0)
        d["bandmask"] = bmv.astype(ml_dtypes.bfloat16)
        d["causal64"] = np.triu(np.ones((64, 64), np.float32)).astype(
            ml_dtypes.bfloat16)
        be = np.zeros((16, 128, 128), np.float32)
        for c in range(16):
            for kk in range(128):
                be[c, 8 * c + kk // 16, kk] = BIGM
        d["bigexp"] = be.astype(ml_dtypes.bfloat16)
        am = np.zeros((D, HD), np.float32)
        for h in range(H):
            am[h * HD:(h + 1) * HD, :] = np.eye(HD) / 16.0
        d["ameanT"] = np.ascontiguousarray(am.T).astype(ml_dtypes.bfloat16)
        d["cokT"] = np.ascontiguousarray(
            np.asarray(d["cok_tm"], np.float32).T).astype(ml_dtypes.bfloat16)
    return d


def kernel(**inputs):
    n = 8
    nc = build_nc(STAGE)
    in_maps = [_shard_inputs(inputs, c) for c in range(n)]
    res = run_bass_kernel_spmd(nc, in_maps, core_ids=list(range(n)))
    out = np.stack([np.asarray(res.results[c]["out"]) for c in range(n)],
                   axis=0)
    return out.astype(np.float32)


# version 12
# speedup vs baseline: 1.0245x; 1.0245x over previous
import numpy as np
import ml_dtypes

import concourse.bass as bass
import concourse.mybir as mybir
from concourse.ap import AP
from concourse.bass_utils import run_bass_kernel_spmd

F32 = mybir.dt.float32
BF16 = mybir.dt.bfloat16
AF = mybir.ActivationFunctionType
ALU = mybir.AluOpType
AX = mybir.AxisListType

T = 2048
D = 1024
NT = T // 128
NC = D // 128
DCD = 4096
NFC = DCD // 128
H = 16
HD = 64
G = 16
NB = 128
FH = 4
NK = 64
NEXP = 4096
BIGM = 32768.0
EPS_SEL = 1e-6

STAGE = 3


class Plan:
    ENGINES = ("sync", "gpsimd", "vector", "scalar", "tensor")

    def __init__(self, nc):
        self.nc = nc
        self.q = {e: [] for e in self.ENGINES}
        self.counts = {}
        self.selfsems = {}

    def op(self, eng, fn, inc=None, inc_by=1):
        ss = self.selfsems.get(eng)
        if ss is not None:
            prior = self.cur(ss)
            if prior:
                self.q[eng].append(
                    lambda e, s=ss, v=prior: e.wait_ge(s, v))
            if inc is None:
                inc, inc_by = ss, 1
            else:
                assert inc is ss, f"{eng} ops must inc their self-sem"
        tgt = None
        if inc is not None:
            key = id(inc)
            tgt = self.counts.get(key, 0) + inc_by
            self.counts[key] = tgt
            self.q[eng].append(
                lambda e, f=fn, s=inc, v=inc_by: f(e).then_inc(s, v))
        else:
            self.q[eng].append(fn)
        return tgt

    def cur(self, sem):
        return self.counts.get(id(sem), 0)

    def wait(self, eng, sem, val):
        if val:
            self.q[eng].append(lambda e, s=sem, v=val: e.wait_ge(s, v))

    def barrier(self, sems):
        vals = [(s, self.cur(s)) for s in sems]
        for eng in self.ENGINES:
            for s, v in vals:
                self.wait(eng, s, v)

    def emit(self):
        with self.nc.Block() as block:
            for name in self.ENGINES:
                qlist = self.q[name]

                def body(eng, qlist=qlist):
                    for fn in qlist:
                        fn(eng)
                getattr(block, name)(body)


def ts(i, n=128):
    return slice(i * n, (i + 1) * n)


def build_nc(stage=STAGE, debug=0):
    nc = bass.Bass("TRN2", target_bir_lowering=False)

    def din(name, shape, dtype=F32):
        return nc.dram_tensor(name, list(shape), dtype, kind="ExternalInput").ap()

    x_in = din("x", (T, D))
    Wfg = din("Wfg", (D, DCD))
    Wfu = din("Wfu", (D, DCD))
    bfg_c = din("bfg_c", (128, NFC))
    bfu_c = din("bfu_c", (128, NFC))
    Wfd = din("Wfd", (DCD, D))
    bd_rep = din("bd_rep", (128, D))
    ident_bf = din("ident_bf", (128, 128), BF16)
    if stage >= 2:
        Wfq = din("Wfq", (D, 512))
        keys_in = din("keys", (FH * NK * 2, NK))
        d_emb = din("down_embed", (NEXP, D))
        u_emb = din("up_embed", (NEXP, D))
    if stage >= 3:
        Wq_d = din("Wq", (D, D)); Wk_d = din("Wk", (D, D))
        Wv_d = din("Wv", (D, D)); Wo_d = din("Wo", (D, D))
        Wck_d = din("Wck", (D, HD)); Wcv_d = din("Wcv", (D, HD))
        Wg_att = din("Wg_att", (HD, 3))
        bg_att_rep = din("bg_att_rep", (128, 3))
        cok_tm = din("cok_tm", (T, NB), BF16)
        cokT_in = din("cokT", (NB, T), BF16)
        bandmask = din("bandmask", (2, 128, 256), BF16)
        causal64 = din("causal64", (64, 64), BF16)
        bigexp = din("bigexp", (16, 128, 128), BF16)
        ameanT = din("ameanT", (HD, D), BF16)

    out_t = nc.dram_tensor("out", [T, D], F32, kind="ExternalOutput").ap()
    xn_dram = nc.dram_tensor("xn_spill", [128, NC * T], BF16,
                             kind="ExternalOutput" if debug else "Internal").ap()
    if stage >= 2:
        deT_dram = nc.dram_tensor("deT_spill", [D, NEXP], BF16,
                                  kind="Internal").ap()
    if stage >= 3:
        dbg_kind = "ExternalOutput" if debug >= 2 else "Internal"
        q_dram = nc.dram_tensor("q_spill", [128, NC * T], BF16,
                                kind=dbg_kind).ap()
        k_dram = nc.dram_tensor("k_spill", [128, NC * T], BF16,
                                kind=dbg_kind).ap()
        v_dram = nc.dram_tensor("v_spill", [128, NC * T], BF16,
                                kind=dbg_kind).ap()
        mixT_dram = (nc.dram_tensor("mixT_dump", [128, NC * T], BF16,
                                    kind="ExternalOutput").ap()
                     if debug >= 2 else None)
        gates_dram = (nc.dram_tensor("gates_dump", [128, NT * 4], F32,
                                     kind="ExternalOutput").ap()
                      if debug >= 2 else None)
        awg_dram = (nc.dram_tensor("awg_dump", [128, 32], BF16,
                                   kind="ExternalOutput").ap()
                    if debug >= 2 else None)
        bg_dram = (nc.dram_tensor("bg_dump", [128, 16], F32,
                                  kind="ExternalOutput").ap()
                   if debug >= 2 else None)
        gprobe = (nc.dram_tensor("gates_probe", [128, 32], F32,
                                 kind="ExternalOutput").ap()
                  if debug >= 2 else None)
        gprobe2 = (nc.dram_tensor("gates_probe2", [128, 64], F32,
                                  kind="ExternalOutput").ap()
                   if debug >= 2 else None)
        gprobe3 = (nc.dram_tensor("gates_probe3", [128, 32], BF16,
                                  kind="ExternalOutput").ap()
                   if debug >= 2 else None)

    p = Plan(nc)

    def sbuf(name, shape, dtype):
        return nc.alloc_sbuf_tensor(name, list(shape), dtype)

    def psum(name, shape, dtype=F32):
        return nc.alloc_psum_tensor(name, list(shape), dtype)

    identity = sbuf("identity", (128, 128), BF16)
    bdrep_sb = sbuf("bdrep_sb", (128, D), F32)
    biasg_sb = sbuf("biasg_sb", (128, NFC), F32)
    biasu_sb = sbuf("biasu_sb", (128, NFC), F32)
    expacc = sbuf("expacc", (128, NT * D), BF16)
    mixT = sbuf("mixT", (128, NC * T), BF16)
    gates_tm = sbuf("gates_tm", (128, NT * 4), F32)
    thp_cols = sbuf("thp_cols", (128, NT * FH), F32)
    mzl_cols = sbuf("mzl_cols", (128, NT * FH), F32)
    tcol_cols = sbuf("tcol_cols", (128, NT * FH), F32)
    constcol = sbuf("constcol", (128, 4), F32)
    AR16 = 57 * 1024 + 192
    AR32 = 5 * 1024
    arena16 = sbuf("arena16", (128, AR16), BF16)
    arena32 = sbuf("arena32", (128, AR32), F32)

    class Carver:
        def __init__(self):
            self.o16 = 0
            self.o32 = 0

        def c16(self, n):
            v = arena16[:, self.o16:self.o16 + n]
            self.o16 += n
            assert self.o16 <= AR16, f"arena16 overflow {self.o16}"
            return v

        def c32(self, n):
            v = arena32[:, self.o32:self.o32 + n]
            self.o32 += n
            assert self.o32 <= AR32, f"arena32 overflow {self.o32}"
            return v

    pf = [psum(f"pf{i}", (128, 512), F32) for i in range(6)]
    pb16 = [psum(f"pb{i}", (128, 1024), BF16) for i in range(2)]

    sPE = nc.alloc_semaphore("sPE")
    sV = nc.alloc_semaphore("sV")
    sA = nc.alloc_semaphore("sA")
    sG = nc.alloc_semaphore("sG")
    ALLSEMS = [sPE, sV, sA, sG]
    p.selfsems = {"vector": sV, "scalar": sA}

    _streams = []

    def mkstream(name):
        s = nc.alloc_semaphore(name)
        ALLSEMS.append(s)
        _streams.append(s)
        return s

    def dma(eng, stream, outap, inap, **kw):
        return p.op(eng, lambda e: e.dma_start(out=outap, in_=inap, **kw),
                    inc=stream, inc_by=16)

    def mm(ps, lhsT, rhs, start, stop, inc=None):
        return p.op("tensor", lambda e: e.matmul(ps, lhsT, rhs, start=start,
                                                 stop=stop), inc=inc)

    def mm_chain(ps, pairs, inc=None):
        n = len(pairs)
        t = None
        for i, (l, r) in enumerate(pairs):
            t = mm(ps, l, r, i == 0, i == n - 1,
                   inc=inc if i == n - 1 else None)
        return t

    p.op("vector", lambda e: e.memset(constcol[:, 0:1], -4096.0))
    stC = mkstream("stC")
    dma("sync", stC, identity[:], ident_bf[:, :])
    dma("sync", stC, bdrep_sb[:], bd_rep[:, :])
    dma("sync", stC, biasg_sb[:], bfg_c[:, :])
    t_const = dma("sync", stC, biasu_sb[:], bfu_c[:, :])

    cv = Carver()
    x_f32 = [cv.c32(D) for _ in range(2)]
    ln_sq = cv.c32(D)
    ln_st = cv.c32(3 * NT)
    ln_rs = cv.c32(2 * NT)
    ln_xn2 = [cv.c16(D), cv.c16(D)]
    xn_stage = [cv.c16(D) for _ in range(2)]

    stXL = [mkstream("stXL0"), mkstream("stXL1")]
    stSp = [mkstream("stSp0"), mkstream("stSp1")]
    spill_hist = {}
    e0_ev = {}
    if stage >= 2:
        de_in = [cv.c16(1024) for _ in range(2)]
        de_stage = [cv.c16(1024) for _ in range(2)]
        stDE = [mkstream("stDE0"), mkstream("stDE1")]
        stDEo = [mkstream("stDEo0"), mkstream("stDEo1")]
        ev_hist = {}
        st_hist = {}

        def emit_e0_iter(ec):
            sl = ec % 2
            if ec == 0:
                p.wait("tensor", stC, t_const)
            if ec >= 2:
                p.wait("gpsimd", sPE, ev_hist[ec - 2][0])
            tDE = dma("gpsimd", stDE[sl], de_in[sl], d_emb[ts(ec), :])
            p.wait("tensor", stDE[sl], tDE)
            pbk = pb16[sl]
            if ec >= 2:
                p.wait("tensor", sV, ev_hist[ec - 2][1])
            tT = None
            for c in range(NC):
                tT = p.op("tensor", lambda e, c=c, pbk=pbk, src=de_in[sl]:
                          e.transpose(pbk[:, ts(c)], src[:, ts(c)],
                                      identity[:]),
                          inc=sPE if c == NC - 1 else None)
            p.wait("vector", sPE, tT)
            if ec >= 2:
                p.wait("vector", stDEo[sl], st_hist[ec - 2])
            tEv = p.op("vector", lambda e, d=de_stage[sl], pbk=pbk:
                       e.tensor_copy(out=d, in_=pbk[:, :]))
            ev_hist[ec] = (tT, tEv)
            p.wait("sync", sV, tEv)
            dst = AP(deT_dram.tensor, ec * 128,
                     [[NEXP, 128], [128 * NEXP, NC], [1, 128]])
            st_hist[ec] = dma("sync", stDEo[sl], dst,
                              de_stage[sl].rearrange("p (c f) -> p c f",
                                                     c=NC))
    sum_hist = {}
    for i in range(NT):
        buf = x_f32[i % 2]
        if i >= 2:
            p.wait("sync", sV, sum_hist[i - 2])
        tL = dma("sync", stXL[i % 2], buf, x_in[ts(i), :])
        p.wait("vector", stXL[i % 2], tL)
        p.op("vector", lambda e, b=buf, i=i: e.reduce_sum(
            out=ln_st[:, i:i + 1], in_=b, axis=AX.X))
        p.op("vector", lambda e, b=buf: e.tensor_tensor(
            out=ln_sq, in0=b, in1=b, op=ALU.mult))
        sum_hist[i] = p.op("vector", lambda e, i=i: e.reduce_sum(
            out=ln_st[:, NT + i:NT + i + 1], in_=ln_sq, axis=AX.X))
        if stage >= 2:
            emit_e0_iter(2 * i)
            emit_e0_iter(2 * i + 1)
    if stage >= 2:
        e0_ev = {0: ev_hist[30][1], 1: ev_hist[31][1]}
    p.op("vector", lambda e: e.tensor_scalar(
        out=ln_st[:, 0:2 * NT], in0=ln_st[:, 0:2 * NT], scalar1=1.0 / D,
        scalar2=None, op0=ALU.mult))
    p.op("vector", lambda e: e.tensor_tensor(
        out=ln_st[:, 2 * NT:3 * NT], in0=ln_st[:, 0:NT],
        in1=ln_st[:, 0:NT], op=ALU.mult))
    p.op("vector", lambda e: e.tensor_tensor(
        out=ln_st[:, NT:2 * NT], in0=ln_st[:, NT:2 * NT],
        in1=ln_st[:, 2 * NT:3 * NT], op=ALU.subtract))
    tVar = p.op("vector", lambda e: e.tensor_scalar(
        out=ln_st[:, NT:2 * NT], in0=ln_st[:, NT:2 * NT], scalar1=1e-5,
        scalar2=None, op0=ALU.add), inc=sV)
    p.wait("scalar", sV, tVar)
    tQ = p.op("scalar", lambda e: e.activation(
        out=ln_st[:, 2 * NT:3 * NT], in_=ln_st[:, NT:2 * NT],
        func=AF.Sqrt), inc=sA)
    p.wait("vector", sA, tQ)
    p.op("vector", lambda e: e.reciprocal(
        out=ln_rs[:, 0:NT], in_=ln_st[:, 2 * NT:3 * NT]))
    p.op("vector", lambda e: e.tensor_scalar(
        out=ln_rs[:, NT:2 * NT], in0=ln_st[:, 0:NT], scalar1=-1.0,
        scalar2=None, op0=ALU.mult))
    p.op("vector", lambda e: e.tensor_tensor(
        out=ln_rs[:, NT:2 * NT], in0=ln_rs[:, NT:2 * NT],
        in1=ln_rs[:, 0:NT], op=ALU.mult))
    act_hist = {}
    tr_hist = {}
    p.wait("sync", sV, sum_hist[NT - 1])
    for i in range(NT):
        buf = x_f32[i % 2]
        if i >= 2:
            p.wait("sync", sV, act_hist[i - 2])
        tL = dma("sync", stXL[i % 2], buf, x_in[ts(i), :])
        p.wait("vector", stXL[i % 2], tL)
        if i >= 2:
            p.wait("vector", sPE, tr_hist[i - 2])
        if i < 2 and i in e0_ev:
            p.wait("tensor", sV, e0_ev[i])
        lxn = ln_xn2[i % 2]
        tXN = p.op("vector", lambda e, b=buf, i=i, lxn=lxn: e.tensor_scalar(
            out=lxn, in0=b, scalar1=ln_rs[:, i:i + 1],
            scalar2=ln_rs[:, NT + i:NT + i + 1],
            op0=ALU.mult, op1=ALU.add), inc=sV)
        act_hist[i] = tXN
        pbk = pb16[i % 2]
        p.wait("tensor", sV, tXN)
        p.wait("tensor", stC, t_const)
        if i >= 2:
            p.wait("tensor", sV, spill_hist[i - 2][0])
        tT = None
        for c in range(NC):
            tT = p.op("tensor", lambda e, c=c, pbk=pbk, lxn=lxn: e.transpose(
                pbk[:, ts(c)], lxn[:, ts(c)], identity[:]),
                inc=sPE if c == NC - 1 else None)
        p.wait("vector", sPE, tT)
        stg = xn_stage[i % 2]
        if i >= 2:
            p.wait("vector", stSp[i % 2], spill_hist[i - 2][1])
        tEv = p.op("vector", lambda e, stg=stg, pbk=pbk: e.tensor_copy(
            out=stg, in_=pbk[:, :]), inc=sV)
        p.wait("sync", sV, tEv)
        dstap = AP(xn_dram.tensor, i * 128,
                   [[NC * T, 128], [T, NC], [1, 128]])
        tSp = dma("sync", stSp[i % 2], dstap,
                  stg.rearrange("p (c f) -> p c f", c=NC))
        spill_hist[i] = (tEv, tSp)
        tr_hist[i] = tT

    p.barrier(ALLSEMS)

    if stage >= 2:
        cv = Carver()
        xn_sb = cv.c16(NC * T)
        sims = cv.c16(NT * 512)
        sxm = cv.c16(32)
        _mark16 = cv.o16
        qfst = [cv.c16(T) for _ in range(8)]
        keysq = cv.c16(8 * 64)
        keysT = cv.c16(8 * 64)
        wq_buf = cv.c16(8 * 128)
        c_v8x = cv.c32(8); c_v8y = cv.c32(8); c_osc = cv.c32(64)
        c_v8 = cv.c32(8); c_negm = cv.c32(1); c_ex8 = cv.c32(8)
        c_z = cv.c32(1); c_lnz = cv.c32(1); c_targ = cv.c32(1)
        stXN2 = mkstream("stXN2")
        stK = mkstream("stK")
        stWQ = mkstream("stWQ")
        t_xn = dma("sync", stXN2, xn_sb.rearrange("p (c t) -> p c t", c=NC),
                   AP(xn_dram.tensor, 0, [[NC * T, 128], [T, NC], [1, T]]))
        for ph in range(8):
            pp, hh = ph // 4, ph % 4
            src = AP(keys_in.tensor, (hh * 128 + pp) * NK,
                     [[2 * NK, NK], [1, NK]])
            dma("gpsimd", stK, keysq[0:64, ph * 64:(ph + 1) * 64], src)
        tK = p.cur(stK)
        p.wait("tensor", stK, tK)
        tKT = None
        for ph in range(8):
            tKT = p.op("tensor", lambda e, ph=ph: e.transpose(
                pb16[0][0:64, ph * 64:(ph + 1) * 64],
                keysq[0:64, ph * 64:(ph + 1) * 64],
                identity[0:64, 0:64]), inc=sPE if ph == 7 else None)
        p.wait("vector", sPE, tKT)
        tKE = p.op("vector", lambda e: e.tensor_copy(
            out=keysT[0:64, 0:512], in_=pb16[0][0:64, 0:512]))
        p.wait("tensor", sV, tKE)
        p.wait("tensor", stXN2, t_xn)
        for qc in range(4):
            wsrc = AP(Wfq.tensor, qc * 128,
                      [[512, 128], [128 * 512, NC], [1, 128]])
            tW = dma("gpsimd", stWQ,
                     wq_buf.rearrange("p (k f) -> p k f", k=NC), wsrc)
            p.wait("tensor", stWQ, tW)
            tQ4 = None
            for ph4 in range(4):
                tQ4 = mm_chain(pf[ph4][:, :],
                               [(wq_buf[:, ts(k)],
                                 xn_sb[:, k * T + ph4 * 512:
                                       k * T + ph4 * 512 + 512])
                                for k in range(NC)],
                               inc=sPE if ph4 == 3 else None)
            p.wait("scalar", sPE, tQ4)
            tEq = None
            for hf in range(2):
                for ph4 in range(4):
                    tEq = p.op("scalar", lambda e, qc=qc, hf=hf, ph4=ph4:
                               e.activation(
                                   out=qfst[2 * qc + hf][0:64,
                                       ph4 * 512:ph4 * 512 + 512],
                                   in_=pf[ph4][hf * 64:hf * 64 + 64, :],
                                   func=AF.Copy, scale=1.0))
            p.wait("tensor", sA, tEq)
            p.wait("gpsimd", sPE, tQ4)
        for i in range(NT):
            tSm = None
            for ph in range(8):
                tSm = mm(pf[4 + (i % 2)][:, ph * 64:(ph + 1) * 64],
                         qfst[ph][0:64, ts(i)],
                         keysT[0:64, ph * 64:(ph + 1) * 64],
                         True, True, inc=sPE if ph == 7 else None)
            p.wait("scalar", sPE, tSm)
            tSe = p.op("scalar", lambda e, i=i: e.activation(
                out=sims[:, i * 512:(i + 1) * 512],
                in_=pf[4 + (i % 2)][:, :], func=AF.Copy, scale=1.0))
            p.wait("tensor", sA, tSe)
        p.barrier(ALLSEMS)

        v8all = cv.c32(NT * FH * 8)
        negmall = cv.c32(NT * FH)
        vsh = cv.c32(NT * FH * 8)
        zsall = cv.c32(NT * FH)
        lnzall = cv.c32(NT * FH)
        for i in range(NT):
            base = i * 512
            for h in range(4):
                col = i * FH + h
                sx_ap = sims[:, base + h * 64:base + h * 64 + 64]
                sy_ap = sims[:, base + 256 + h * 64:base + 256 + h * 64 + 64]
                p.op("vector", lambda e, a=sx_ap: e.max(out=c_v8x, in_=a))
                p.op("vector", lambda e, a=sy_ap: e.max(out=c_v8y, in_=a))
                p.op("vector", lambda e: e.tensor_tensor(
                    out=c_osc.rearrange("p (i j) -> p i j", i=8),
                    in0=AP(c_v8x.tensor, c_v8x.offset,
                           [[AR32, 128], [1, 8], [0, 8]]),
                    in1=AP(c_v8y.tensor, c_v8y.offset,
                           [[AR32, 128], [0, 8], [1, 8]]),
                    op=ALU.add))
                p.op("vector", lambda e, col=col: e.max(
                    out=v8all[:, col * 8:col * 8 + 8], in_=c_osc))
        p.op("vector", lambda e: e.tensor_scalar(
            out=negmall, in0=AP(v8all.tensor, v8all.offset,
                                [[AR32, 128], [8, 64]]),
            scalar1=-1.0, scalar2=None, op0=ALU.mult))
        p.op("vector", lambda e: e.tensor_scalar(
            out=thp_cols[:, 0:64],
            in0=AP(v8all.tensor, v8all.offset + 7, [[AR32, 128], [8, 64]]),
            scalar1=EPS_SEL, scalar2=None, op0=ALU.subtract))
        eshb = cv.c32(NT * FH * 8)
        tVsh = p.op("vector", lambda e: e.tensor_tensor(
            out=vsh.rearrange("p (c j) -> p c j", c=64),
            in0=v8all.rearrange("p (c j) -> p c j", c=64),
            in1=AP(negmall.tensor, negmall.offset,
                   [[AR32, 128], [1, 64], [0, 8]]),
            op=ALU.add))
        p.wait("scalar", sV, tVsh)
        tEsh = p.op("scalar", lambda e: e.activation(
            out=eshb, in_=vsh, func=AF.Exp))
        p.wait("vector", sA, tEsh)
        tZs = p.op("vector", lambda e: e.reduce_sum(
            out=zsall.rearrange("p (c u) -> p c u", c=64),
            in_=eshb.rearrange("p (c j) -> p c j", c=64), axis=AX.X))
        p.wait("scalar", sV, tZs)
        tLn = p.op("scalar", lambda e: e.activation(
            out=lnzall, in_=zsall, func=AF.Ln))
        p.wait("vector", sA, tLn)
        p.op("vector", lambda e: e.tensor_tensor(
            out=mzl_cols[:, 0:64], in0=negmall, in1=lnzall,
            op=ALU.subtract))
        tTa = p.op("vector", lambda e: e.tensor_tensor(
            out=vsh[:, 0:64], in0=thp_cols[:, 0:64], in1=mzl_cols[:, 0:64],
            op=ALU.add))
        p.wait("scalar", sV, tTa)
        tTc = p.op("scalar", lambda e: e.activation(
            out=tcol_cols[:, 0:64], in_=vsh[:, 0:64], func=AF.Exp))
        p.wait("vector", sA, tTc)
        p.barrier(ALLSEMS)

        EG = 1024
        NKX = EG // 64
        cv.o16 = _mark16
        deTg2 = [cv.c16(NC * EG), mixT[:, 0:NC * EG]]
        ueg2 = [cv.c16(8 * 1024), mixT[:, NC * EG:NC * EG + 8 * 1024]]
        swt2 = [cv.c16(EG), cv.c16(EG)]
        eexp = cv.c16(EG)
        eyall = cv.c16(NT * 256)
        exe2 = [cv.c16(64), cv.c16(64)]
        emsk2 = [[cv.c16(EG) for _ in range(4)] for _ in range(2)]
        epw = cv.c16(EG)
        epwT = cv.c16(EG)
        stDT = [mkstream("stDT0"), mkstream("stDT1")]
        stUE = [mkstream("stUE0"), mkstream("stUE1")]
        NG = NEXP // EG
        NIT = NG * NT
        tDTh = {}; tUEh = {}
        tW_h = {}; tSW_h = {}; tMsk_h = {}; tEPW_h = {}; tEM_h = {}
        tTe_h = {}; tOut_h = {}; tAcc_h = {}

        def e2_load(gh):
            gp = gh % 2
            if gh >= 2:
                p.wait("gpsimd", sPE, tOut_h[(gh - 1) * NT - 1])
            tDTh[gh] = dma("gpsimd", stDT[gp],
                           deTg2[gp].rearrange("p (k t) -> p k t", k=NC),
                           AP(deT_dram.tensor, gh * EG,
                              [[NEXP, 128], [128 * NEXP, NC], [1, EG]]))
            tUEh[gh] = dma("gpsimd", stUE[gp],
                           ueg2[gp].rearrange("p (k t) -> p k t", k=8),
                           AP(u_emb.tensor, gh * EG * D,
                              [[D, 128], [128 * D, 8], [1, D]]))

        def e2_W(it):
            gh, i = it // NT, it % NT
            b = it % 2
            gp = gh % 2
            if i == 0:
                p.wait("tensor", stDT[gp], tDTh[gh])
                p.wait("tensor", stUE[gp], tUEh[gh])
            if it >= 2:
                p.wait("tensor", sV, tEPW_h[it - 2])
            t = None
            for pc in range(2):
                t = mm_chain(
                    pf[2 * b + pc][:, :],
                    [(xn_sb[:, k * T + i * 128:k * T + i * 128 + 128],
                      deTg2[gp][:, k * EG + pc * 512:k * EG + pc * 512 + 512])
                     for k in range(NC)],
                    inc=sPE if pc == 1 else None)
            tW_h[it] = t

        def e2_chain(it):
            gh, i = it // NT, it % NT
            b = it % 2
            if it >= 2:
                p.wait("scalar", sV, tMsk_h[it - 2])
            tE4 = None
            for h in range(4):
                col = i * FH + h
                sxs_ap = sims[:, i * 512 + h * 64 + NKX * gh:
                              i * 512 + h * 64 + NKX * gh + NKX]
                tE4 = p.op("scalar", lambda e, a=sxs_ap, c=col, h=h, b=b:
                           e.activation(
                               out=exe2[b][:, h * 16:h * 16 + 16],
                               in_=a, func=AF.Exp,
                               bias=mzl_cols[:, c:c + 1]))
            p.wait("scalar", sPE, tW_h[it])
            if it >= 2:
                p.wait("scalar", sV, tEPW_h[it - 2])
            tSW = None
            for pc in range(2):
                tSW = p.op("scalar", lambda e, pc=pc, b=b: e.activation(
                    out=swt2[b][:, pc * 512:pc * 512 + 512],
                    in_=pf[2 * b + pc][:, :], func=AF.Silu))
            tSW_h[it] = tSW
            p.wait("vector", sA, tE4)
            tMsk = None
            for h in range(4):
                col = i * FH + h
                ey_off = eyall.offset + i * 256 + h * 64
                tTT = p.op("vector", lambda e, h=h, b=b, eo=ey_off:
                           e.tensor_tensor(
                               out=eexp.rearrange("p (x y) -> p x y", x=NKX),
                               in0=AP(exe2[b].tensor,
                                      exe2[b].offset + h * 16,
                                      [[AR16, 128], [1, NKX], [0, 64]]),
                               in1=AP(eyall.tensor, eo,
                                      [[AR16, 128], [0, NKX], [1, 64]]),
                               op=ALU.mult))
                tMsk = p.op("vector", lambda e, h=h, c=col, b=b:
                            e.scalar_tensor_tensor(
                                out=emsk2[b][h], in0=eexp,
                                scalar=tcol_cols[:, c:c + 1],
                                in1=eexp, op0=ALU.is_ge, op1=ALU.mult))
            tMsk_h[it] = tMsk

        def e2_EM(it):
            gh, i = it // NT, it % NT
            b = it % 2
            p.wait("tensor", sA, tSW_h[it])
            p.wait("tensor", sV, tMsk_h[it])
            tEM = None
            for pc in range(2):
                for h in range(4):
                    tEM = mm(pf[2 * b + pc][:, :], identity[:],
                             emsk2[b][h][:, pc * 512:pc * 512 + 512],
                             h == 0, h == 3,
                             inc=sPE if (pc == 1 and h == 3) else None)
            tEM_h[it] = tEM

        def e2_tailA(it):
            b = it % 2
            tEM = tEM_h[it]
            p.wait("vector", sPE, tEM)
            tEPW = None
            for pc in range(2):
                tEPW = p.op("vector", lambda e, pc=pc, b=b:
                            e.scalar_tensor_tensor(
                                out=epw[:, pc * 512:pc * 512 + 512],
                                in0=pf[2 * b + pc][:, :], scalar=1.0,
                                in1=swt2[b][:, pc * 512:pc * 512 + 512],
                                op0=ALU.mult, op1=ALU.mult))
            tEPW_h[it] = tEPW
            p.wait("tensor", sV, tEPW)
            if it >= 1:
                p.wait("tensor", sV, tTe_h[it - 1])
            tt = None
            for c in range(NC):
                tt = p.op("tensor", lambda e, c=c: e.transpose(
                    pb16[0][:, ts(c)],
                    epw[:, c * 128:c * 128 + 128],
                    identity[:]), inc=sPE if c == NC - 1 else None)
            p.wait("vector", sPE, tt)
            tTe = p.op("vector", lambda e: e.tensor_copy(
                out=epwT[:, :], in_=pb16[0][:, :]))
            tTe_h[it] = tTe

        def e2_tailB(it):
            gh, i = it // NT, it % NT
            gp = gh % 2
            p.wait("tensor", sV, tTe_h[it])
            if it >= 1:
                p.wait("tensor", sV, tAcc_h[it - 1])
            tOut = None
            for dh in range(2):
                for cc in range(8):
                    tOut = mm(pf[4 + dh][:, :],
                              epwT[:, cc * 128:cc * 128 + 128],
                              ueg2[gp][:, cc * 1024 + dh * 512:
                                       cc * 1024 + dh * 512 + 512],
                              cc == 0, cc == 7,
                              inc=sPE if (dh == 1 and cc == 7) else None)
            tOut_h[it] = tOut
            p.wait("vector", sPE, tOut)
            tAcc = None
            for dh in range(2):
                dst = expacc[:, i * D + dh * 512:i * D + dh * 512 + 512]
                if gh == 0:
                    tAcc = p.op("vector", lambda e, dst=dst, dh=dh:
                                e.tensor_copy(out=dst,
                                              in_=pf[4 + dh][:, :]))
                else:
                    tAcc = p.op("vector", lambda e, dst=dst, dh=dh:
                                e.tensor_tensor(out=dst,
                                                in0=pf[4 + dh][:, :],
                                                in1=dst, op=ALU.add))
            tAcc_h[it] = tAcc

        e2_load(0)
        e2_load(1)
        for i in range(NT):
            p.op("scalar", lambda e, i=i: e.activation(
                out=eyall[:, i * 256:(i + 1) * 256],
                in_=sims[:, i * 512 + 256:i * 512 + 512], func=AF.Exp))
        e2_W(0)
        e2_chain(0)
        for it in range(NIT):
            e2_EM(it)
            if it + 1 < NIT:
                e2_W(it + 1)
            e2_tailA(it)
            if it + 1 < NIT:
                e2_chain(it + 1)
            e2_tailB(it)
            if (it + 1) % NT == 0:
                ghn = (it + 1) // NT + 1
                if ghn < NG:
                    e2_load(ghn)
        p.barrier(ALLSEMS)

    if stage >= 3:
        SC = 0.125
        NEGB = -4096.0
        cv = Carver()
        v_aug = cv.c16(16 * 1056)
        kcT = cv.c16(16 * 128)
        vc_aug = cv.c16(16 * 66)
        awg_sb = cv.c16(24)
        bigexp_sb = cv.c16(16 * 128)
        cok_sb = cv.c16(NB)
        causal_sb = cv.c16(64)
        band_sb = cv.c16(256)
        wck_sb = cv.c16(16 * 64)
        wcv_sb = cv.c16(16 * 64)
        _markA = cv.o16

        stA = mkstream("stA")
        stWA = [mkstream("stWA0"), mkstream("stWA1")]
        stSpQ = [mkstream("stSpQ0"), mkstream("stSpQ1")]
        stSpK = [mkstream("stSpK0"), mkstream("stSpK1")]
        stSpV = [mkstream("stSpV0"), mkstream("stSpV1")]
        stXN3 = mkstream("stXN3")
        stQH = mkstream("stQH")
        stKH = mkstream("stKH")
        stQG = mkstream("stQG")
        stBG = mkstream("stBG")
        stDBG = mkstream("stDBG")
        stDBG2 = mkstream("stDBG2")
        stCok = mkstream("stCok")

        xn_at = cv.c16(NC * T)
        wq_at = [cv.c16(8 * 128) for _ in range(2)]
        pstage = [cv.c16(2048) for _ in range(2)]
        vfm = cv.c16(2048)
        wg3_sb = cv.c16(4)
        ameanT_sb = cv.c16(D)

        dma("gpsimd", stA, bigexp_sb.rearrange("p (a b) -> p a b", a=16),
            AP(bigexp.tensor, 0, [[128, 128], [128 * 128, 16], [1, 128]]))
        dma("gpsimd", stA, causal_sb[0:64, :], causal64[:, :])
        dma("gpsimd", stA, band_sb[:, :], bandmask[1, :, :])
        dma("gpsimd", stA, wg3_sb[0:64, 0:3], Wg_att[:, :])
        dma("gpsimd", stA, ameanT_sb[0:64, :], ameanT[:, :])
        wck_src = AP(Wck_d.tensor, 0, [[HD, 64], [64 * HD, 16], [1, HD]])
        dma("gpsimd", stA, wck_sb[0:64, :].rearrange("p (j c) -> p j c", j=16),
            wck_src)
        wcv_src = AP(Wcv_d.tensor, 0, [[HD, 64], [64 * HD, 16], [1, HD]])
        tA0 = dma("gpsimd", stA,
                  wcv_sb[0:64, :].rearrange("p (j c) -> p j c", j=16), wcv_src)
        t_xn3 = dma("sync", stXN3, xn_at.rearrange("p (c t) -> p c t", c=NC),
                    AP(xn_dram.tensor, 0, [[NC * T, 128], [T, NC], [1, T]]))
        p.wait("tensor", stXN3, t_xn3)

        def proj(wdram, spill_dram, stream_sl, kind):
            for c in range(NC):
                wb = wq_at[c % 2]
                if c >= 2:
                    p.wait("gpsimd", sPE, proj_pe.get(c - 2, 0))
                else:
                    p.wait("gpsimd", sPE, p.cur(sPE))
                tW = dma("gpsimd", stWA[c % 2],
                         wb.rearrange("p (k f) -> p k f", k=NC),
                         AP(wdram.tensor, c * 128,
                            [[D, 128], [128 * D, NC], [1, 128]]))
                p.wait("tensor", stWA[c % 2], tW)
                tMM = None
                for pc in range(4):
                    tMM = mm_chain(
                        pf[pc][:, :],
                        [(wb[:, ts(k)],
                          xn_at[:, k * T + pc * 512:k * T + pc * 512 + 512])
                         for k in range(NC)],
                        inc=sPE if pc == 3 else None)
                proj_pe[c] = tMM
                p.wait("scalar", sPE, tMM)
                if kind == "spill":
                    dst_sb = pstage[c % 2]
                    if (c % 2) in spill_guard:
                        gs, gv = spill_guard[c % 2]
                        p.wait("scalar", gs, gv)
                    tEv = None
                    for pc in range(4):
                        tEv = p.op("scalar", lambda e, pc=pc, d=dst_sb:
                                   e.activation(
                                       out=d[:, pc * 512:pc * 512 + 512],
                                       in_=pf[pc][:, :], func=AF.Copy,
                                       scale=1.0))
                    p.wait("sync", sA, tEv)
                    proj_sp[c] = dma("sync", stream_sl[c % 2],
                                     spill_dram[:, c * T:(c + 1) * T], dst_sb)
                    spill_guard[c % 2] = (stream_sl[c % 2], proj_sp[c])
                    p.wait("tensor", sA, tEv)
                else:
                    tEv = None
                    for pc in range(4):
                        tEv = p.op("scalar", lambda e, pc=pc: e.activation(
                            out=vfm[:, pc * 512:pc * 512 + 512],
                            in_=pf[pc][:, :], func=AF.Copy, scale=1.0))
                    p.wait("tensor", sA, tEv)
                    p.wait("sync", sA, tEv)
                    if c >= 2:
                        p.wait("sync", sV, vau_hist.get(c - 2, 0))
                    proj_sp[c] = dma("sync", stream_sl[c % 2],
                                     spill_dram[:, c * T:(c + 1) * T], vfm)
                    for half8 in range(2):
                        tt = None
                        for kc8 in range(8):
                            kc = half8 * 8 + kc8
                            tt = p.op("tensor", lambda e, kc=kc, kc8=kc8,
                                      half8=half8: e.transpose(
                                          pb16[half8][:, ts(kc8)],
                                          vfm[:, kc * 128:kc * 128 + 128],
                                          identity[:]),
                                      inc=sPE if kc8 == 7 else None)
                        p.wait("vector", sPE, tt)
                        dstv = AP(v_aug.tensor,
                                  v_aug.offset + half8 * 8 * 1056 + c * 2 * 66,
                                  [[AR16, 128], [1056, 8], [66, 2], [1, 64]])
                        tVe = p.op("vector", lambda e, dstv=dstv, half8=half8:
                                   e.tensor_copy(
                                       out=dstv,
                                       in_=pb16[half8].rearrange(
                                           "p (a h f) -> p a h f", a=8, h=2)))
                        p.wait("tensor", sV, tVe)
                    vau_hist[c] = tVe
                    p.wait("scalar", stream_sl[c % 2], proj_sp[c])
                    p.wait("scalar", sPE, proj_pe[c])

        proj_pe = {}
        proj_sp = {}
        spill_guard = {}
        proj(Wq_d, q_dram, stSpQ, "spill")
        proj_pe = {}
        proj_sp = {}
        proj(Wk_d, k_dram, stSpK, "spill")
        proj_pe = {}
        proj_sp = {}
        vau_hist = {}
        proj(Wv_d, v_dram, stSpV, "v")
        tOnes = p.op("vector", lambda e: e.memset(
            AP(v_aug.tensor, v_aug.offset + 64,
               [[AR16, 128], [1056, 16], [66, 16]]), 1.0))

        p.barrier(ALLSEMS)
        bgatt_sb = cv.c32(4)
        glog = cv.c32(4)
        gsc = cv.c32(8)
        qg_buf = cv.c16(8 * 128)
        tBg = dma("sync", stBG, bgatt_sb[:, 0:3], bg_att_rep[:, :])
        tAW = None
        for c in range(NC):
            tAW = mm(pf[0][:, c * 3:c * 3 + 3],
                     ameanT_sb[0:64, ts(c)], wg3_sb[0:64, 0:3],
                     True, True, inc=sPE if c == NC - 1 else None)
        p.wait("scalar", sPE, tAW)
        tAWe = p.op("scalar", lambda e: e.activation(
            out=awg_sb[:, 0:24], in_=pf[0][:, 0:24], func=AF.Copy, scale=1.0))
        p.wait("tensor", sA, tAWe)
        if debug >= 2:
            p.wait("sync", sA, tAWe)
            dma("sync", stDBG, awg_dram[:, 0:24], awg_sb[:, 0:24])
            p.wait("sync", stBG, tBg)
            dma("sync", stDBG, bg_dram[:, 0:3], bgatt_sb[:, 0:3])
        gm_hist = 0
        for i in range(NT):
            p.wait("sync", sPE, gm_hist)
            tQG = dma("sync", stQG, qg_buf.rearrange("p (c f) -> p c f", c=NC),
                      AP(q_dram.tensor, i * 128,
                         [[NC * T, 128], [T, NC], [1, 128]]))
            p.wait("tensor", stQG, tQG)
            tGm = None
            for c in range(NC):
                tGm = mm(pf[1][:, 0:3], qg_buf[:, ts(c)],
                         awg_sb[:, c * 3:c * 3 + 3], c == 0, c == NC - 1,
                         inc=sPE if c == NC - 1 else None)
            gm_hist = tGm
            p.wait("vector", sPE, tGm)
            p.wait("vector", stBG, tBg)
            p.op("vector", lambda e: e.tensor_tensor(
                out=glog[:, 0:3], in0=pf[1][:, 0:3], in1=bgatt_sb[:, 0:3],
                op=ALU.add))
            p.op("vector", lambda e: e.reduce_max(
                out=gsc[:, 0:1], in_=glog[:, 0:3], axis=AX.X))
            tGn = p.op("vector", lambda e: e.tensor_scalar(
                out=gsc[:, 1:2], in0=gsc[:, 0:1], scalar1=-1.0, scalar2=None,
                op0=ALU.mult))
            p.wait("scalar", sV, tGn)
            tGe = p.op("scalar", lambda e: e.activation(
                out=gsc[:, 2:5], in_=glog[:, 0:3], func=AF.Exp,
                bias=gsc[:, 1:2], accum_out=gsc[:, 5:6]))
            p.wait("vector", sA, tGe)
            p.op("vector", lambda e: e.reciprocal(
                out=gsc[:, 6:7], in_=gsc[:, 5:6]))
            tGf = p.op("vector", lambda e, i=i: e.tensor_scalar(
                out=gates_tm[:, i * 4:i * 4 + 3], in0=gsc[:, 2:5],
                scalar1=gsc[:, 6:7], scalar2=None, op0=ALU.mult))
            if debug >= 2:
                if i % 2 == 0:
                    tGf = p.op("vector", lambda e, i=i: e.tensor_copy(
                        out=gates_tm[:, i * 4 + 3:i * 4 + 4],
                        in_=pf[1][:, 0:1]))
                else:
                    tGf = p.op("vector", lambda e, i=i: e.tensor_copy(
                        out=gates_tm[:, i * 4 + 3:i * 4 + 4],
                        in_=bgatt_sb[:, 0:1]))
            p.wait("tensor", sV, tGf)
            if debug >= 2 and i == 0:
                p.wait("sync", sV, p.cur(sV))
                p.wait("sync", sA, p.cur(sA))
                dma("sync", stDBG, bg_dram[:, 4:8], glog[:, 0:4],
                    allow_slow_non_contiguous=True)
                dma("sync", stDBG, bg_dram[:, 8:16], gsc[:, 0:8],
                    allow_slow_non_contiguous=True)
                p.wait("vector", stDBG, p.cur(stDBG))
        if debug >= 2:
            p.wait("sync", sV, p.cur(sV))
            dma("sync", stDBG, bg_dram[:, 3:4], bgatt_sb[:, 0:1],
                allow_slow_non_contiguous=True)
            dma("sync", stDBG, gprobe[:, 0:4], gates_tm[:, 0:4])

        p.barrier(ALLSEMS)
        cv.o16 = _markA
        cv.o32 = 0
        qh = cv.c16(T)
        kh = cv.c16(T)
        khblk = cv.c16(T)
        vct = cv.c16(128)
        kc_hist = 0
        vc_hist = 0
        for h in range(H):
            p.wait("sync", sPE, kc_hist)
            tKH = dma("sync", stKH, kh[0:64, :],
                      k_dram[(h % 2) * 64:(h % 2) * 64 + 64,
                             (h // 2) * T:(h // 2) * T + T])
            p.wait("vector", stKH, tKH)
            p.wait("vector", sPE, kc_hist)
            tGk = p.op("vector", lambda e: e.tensor_copy(
                out=khblk[0:64, :].rearrange("p (j n) -> p j n", j=16),
                in_=AP(kh.tensor, kh.offset, [[AR16, 64], [1, 16], [16, 128]])))
            p.wait("tensor", sV, tGk)
            p.wait("tensor", stA, tA0)
            tKc = None
            for j in range(16):
                tKc = mm(pf[0][0:64, 0:128],
                         wck_sb[0:64, j * 64:j * 64 + 64],
                         khblk[0:64, j * 128:j * 128 + 128],
                         j == 0, j == 15, inc=sPE if j == 15 else None)
            p.wait("scalar", sPE, tKc)
            tKce = p.op("scalar", lambda e, h=h: e.activation(
                out=kcT[0:64, h * 128:h * 128 + 128], in_=pf[0][0:64, 0:128],
                func=AF.Copy, scale=1.0))
            p.wait("tensor", sA, tKce)
            kc_hist = tKc
            p.wait("sync", sPE, vc_hist)
            tVH = dma("sync", stQH, qh[0:64, :],
                      v_dram[(h % 2) * 64:(h % 2) * 64 + 64,
                             (h // 2) * T:(h // 2) * T + T])
            p.wait("tensor", stQH, tVH)
            tVc = None
            for j in range(16):
                tVc = mm(pf[1][0:64, 0:128],
                         wcv_sb[0:64, j * 64:j * 64 + 64],
                         AP(qh.tensor, qh.offset + j, [[AR16, 64], [16, 128]]),
                         j == 0, j == 15, inc=sPE if j == 15 else None)
            vc_hist = tVc
            p.wait("scalar", sPE, tVc)
            tVe = p.op("scalar", lambda e: e.activation(
                out=vct[0:64, :], in_=pf[1][0:64, 0:128], func=AF.Copy,
                scale=1.0))
            p.wait("tensor", sA, tVe)
            tVt = p.op("tensor", lambda e: e.transpose(
                pb16[0][:, 0:64], vct[0:64, :], identity[0:64, 0:64]),
                inc=sPE)
            p.wait("vector", sPE, tVt)
            tVce = p.op("vector", lambda e, h=h: e.tensor_copy(
                out=vc_aug[:, h * 66:h * 66 + 64], in_=pb16[0][:, 0:64]))
            p.wait("tensor", sV, tVce)
        tOne2 = p.op("vector", lambda e: e.memset(
            AP(vc_aug.tensor, vc_aug.offset + 64,
               [[AR16, 128], [66, 16]]), 1.0))
        p.barrier(ALLSEMS)
        if debug >= 2:
            dma("sync", stDBG, gprobe[:, 4:8], gates_tm[:, 0:4])
            p.wait("sync", stDBG, p.cur(stDBG))

        qe2 = [cv.c16(768), cv.c16(768)]
        kh2 = [kh, cv.c16(T)]
        cok_sb = cv.c16(NT * NB)
        cokT_sb = cv.c16(T)
        mtile2 = [cv.c16(128), cv.c16(128)]
        mT_sl = cv.c16(512)
        ecT_sl = cv.c16(512)
        elb = cv.c16(5 * 256)
        euT = cv.c16(16 * 512)
        mixq = cv.c16(4 * 1024)
        a_th4 = cv.c32(1)
        a_v8 = cv.c32(8)
        outc = cv.c32(200)
        z3s = cv.c32(8)
        tmp64 = cv.c32(64)
        if debug >= 2:
            dma("sync", stDBG, gprobe3[:, 30:32], bigexp_sb[:, 0:2])
        tCk = dma("gpsimd", stCok, cok_sb.rearrange("p (i n) -> p i n", i=NT),
                  AP(cok_tm.tensor, 0, [[NB, 128], [128 * NB, NT], [1, NB]]))
        tCkT = dma("gpsimd", stCok, cokT_sb[:, :], cokT_in[:, :])
        p.wait("tensor", stCok, tCkT)
        stKA = [mkstream("stKA0"), mkstream("stKA1")]
        stQA = [mkstream("stQA0"), mkstream("stQA1")]
        NGH = 4 * H
        tKH_h = {}; tQH_h = {}; tSlab_h = {}; tOc_h = {}
        tEc_h = {}; tMe_h = {}; tMix_h = {}
        tMTe_hist = []
        slab_ctr = [0]
        slab_ev = {}
        slab_mm = {}

        def a2_load(g):
            if g >= NGH:
                return
            tq, h = g // H, g % H
            gp = g % 2
            t0 = tq * 512
            if g >= 2:
                p.wait("sync", sPE, tSlab_h[g - 2])
            tKH_h[g] = dma("sync", stKA[gp], kh2[gp][0:64, :],
                           k_dram[(h % 2) * 64:(h % 2) * 64 + 64,
                                  (h // 2) * T:(h // 2) * T + T])
            qlo = max(t0 - 128, 0)
            qhi = min(t0 + 640, T)
            tQH_h[g] = dma("sync", stQA[gp],
                           qe2[gp][0:64, 128 - (t0 - qlo):
                                   128 - (t0 - qlo) + (qhi - qlo)],
                           q_dram[(h % 2) * 64:(h % 2) * 64 + 64,
                                  (h // 2) * T + qlo:(h // 2) * T + qhi])

        def a2_head(g):
            tq, h = g // H, g % H
            hp = g % 2
            t0 = tq * 512
            qe = qe2[hp]
            kh_ = kh2[hp]
            p.wait("tensor", stKA[hp], tKH_h[g])
            p.wait("tensor", stQA[hp], tQH_h[g])
            if g >= 1:
                p.wait("tensor", sV, tOc_h[g - 1][3])
            tSb_h = {}
            tMb = {}
            tMt = {}

            def emit_pair(ii):
                i = tq * 4 + ii
                bank = pf[2 + ii % 2]
                if ii >= 2:
                    p.wait("tensor", sV, tMb[ii - 2])
                mm(bank[:, 0:128],
                   qe[0:64, 128 + ii * 128:128 + ii * 128 + 128],
                   kcT[0:64, h * 128:h * 128 + 128], True, False)
                tSb_h[ii] = mm(bank[:, 0:128],
                               identity[:], cok_sb[:, i * NB:i * NB + NB],
                               False, True, inc=sPE)

            def emit_chain(ii):
                i = tq * 4 + ii
                bank = pf[2 + ii % 2]
                p.wait("vector", sPE, tSb_h[ii])
                p.op("vector", lambda e, bank=bank: e.max(
                    out=a_v8, in_=bank[:, 0:128]))
                p.op("vector", lambda e: e.tensor_scalar(
                    out=a_th4, in0=a_v8[:, 3:4], scalar1=1e-5,
                    scalar2=None, op0=ALU.subtract))
                if ii >= 2:
                    p.wait("vector", sPE, tMt[ii - 2])
                tMb[ii] = p.op("vector", lambda e, ii=ii, bank=bank:
                               e.tensor_scalar(
                                   out=mtile2[ii % 2], in0=bank[:, 0:128],
                                   scalar1=a_th4, scalar2=None,
                                   op0=ALU.is_ge))
                if i == 0:
                    p.op("vector", lambda e: e.memset(
                        mtile2[0][0:64, 0:4], 1.0))
                    tMb[ii] = p.op("vector", lambda e: e.memset(
                        mtile2[0][0:64, 4:128], 0.0))

            def emit_tr(j):
                p.wait("tensor", sV, tMb[j])
                if j == 0 and g >= 1:
                    p.wait("tensor", sV, tMe_h[g - 1])
                if j == 0 and h == 0 and tq > 0:
                    for tk in tMTe_hist[-2:]:
                        p.wait("tensor", sV, tk)
                tMt[j] = p.op("tensor", lambda e, j=j: e.transpose(
                    pb16[0][:, j * 128:j * 128 + 128], mtile2[j % 2],
                    identity[:]), inc=sPE)

            emit_pair(0)
            emit_pair(1)
            emit_chain(0)
            emit_chain(1)
            emit_tr(0)
            emit_pair(2)
            emit_chain(2)
            emit_tr(1)
            emit_pair(3)
            emit_chain(3)
            emit_tr(2)
            emit_tr(3)
            p.wait("vector", sPE, tMt[3])
            tMe = p.op("vector", lambda e: e.tensor_copy(
                out=mT_sl, in_=pb16[0][:, 0:512]))
            tMe_h[g] = tMe
            bB = pf[4 + g % 2]
            if g >= 2:
                p.wait("tensor", sA, tEc_h[g - 2])
            tSbT = mm(bB[:, :], kcT[0:64, h * 128:h * 128 + 128],
                      qe[0:64, 128:640], True, False)
            tSbT = mm(bB[:, :], identity[:],
                      cokT_sb[:, t0:t0 + 512], False, True, inc=sPE)
            p.wait("scalar", sPE, tSbT)
            tEc = p.op("scalar", lambda e, bB=bB: e.activation(
                out=ecT_sl, in_=bB[:, :], func=AF.Exp, scale=SC))
            tEc_h[g] = tEc
            p.wait("tensor", sV, tMe)
            slabs = [("eu", kc) for kc in range(4 * tq + 4)]
            slabs += [("lb", kc) for kc in range(max(0, 4 * tq - 1),
                                                 4 * tq + 4)]
            tCorner = None
            tEvLast = None
            tSlab = None
            for kind, kc in slabs:
                sc_i = slab_ctr[0]
                slab_ctr[0] += 1
                bank = pf[sc_i % 2]
                if sc_i >= 2:
                    p.wait("tensor", sA, slab_ev[sc_i - 2])
                if kind == "eu":
                    mm(bank[:, :], kh_[0:64, kc * 128:kc * 128 + 128],
                       qe[0:64, 128:640], True, False)
                    tSlab = mm(bank[:, :],
                               bigexp_sb[:, kc * 128:kc * 128 + 128],
                               mT_sl, False, True, inc=sPE)
                    p.wait("scalar", sPE, tSlab)
                    tEv2 = p.op("scalar", lambda e, kc=kc, bank=bank:
                                e.activation(
                                    out=euT[:, kc * 512:kc * 512 + 512],
                                    in_=bank[:, :], func=AF.Exp, scale=SC,
                                    bias=constcol[:, 0:1]))
                    if kc == 0 and tq == 0:
                        p.wait("vector", sA, tEv2)
                        tCorner = p.op("vector", lambda e: e.tensor_tensor(
                            out=euT[0:64, 0:64], in0=euT[0:64, 0:64],
                            in1=causal_sb[0:64, 0:64], op=ALU.mult))
                        p.wait("scalar", sV, tCorner)
                else:
                    wlo = kc * 128
                    whi = min(kc * 128 + 256, T)
                    wid = whi - wlo
                    qcol = 128 + (wlo - t0)
                    slot = kc % 5
                    mm(bank[:, 0:wid], kh_[0:64, kc * 128:kc * 128 + 128],
                       qe[0:64, qcol:qcol + wid], True, False)
                    tSlab = mm(bank[:, 0:wid], identity[:],
                               band_sb[:, 0:wid], False, True, inc=sPE)
                    p.wait("scalar", sPE, tSlab)
                    tEv2 = p.op("scalar", lambda e, slot=slot, wid=wid,
                                bank=bank: e.activation(
                                    out=elb[:, slot * 256:slot * 256 + wid],
                                    in_=bank[:, 0:wid], func=AF.Exp,
                                    scale=SC))
                slab_ev[sc_i] = tEv2
                tEvLast = tEv2
            tSlab_h[g] = tSlab
            a2_load(g + 2)
            p.wait("tensor", sA, tEvLast)
            if tCorner is not None:
                p.wait("tensor", sV, tCorner)
            tOc_h[g] = {}
            for ii in range(4):
                i = tq * 4 + ii
                bS = pf[2 + ii % 2]
                bC = pf[4 + ii % 2]
                if ii >= 2:
                    p.wait("tensor", sV, tOc_h[g][ii - 2])
                tSel = None
                for kc in range(i + 1):
                    tSel = mm(bS[:, 0:65],
                              euT[:, kc * 512 + ii * 128:
                                  kc * 512 + ii * 128 + 128],
                              v_aug[:, kc * 1056 + h * 66:
                                    kc * 1056 + h * 66 + 65],
                              kc == 0, kc == i, inc=sPE if kc == i else None)
                tCmp = mm(bC[:, 0:65],
                          ecT_sl[:, ii * 128:ii * 128 + 128],
                          vc_aug[:, h * 66:h * 66 + 65],
                          True, True, inc=sPE)
                tLoc = mm(bC[:, 128:193],
                          elb[:, (i % 5) * 256:(i % 5) * 256 + 128],
                          v_aug[:, i * 1056 + h * 66:
                                i * 1056 + h * 66 + 65],
                          True, i == 0, inc=sPE if i == 0 else None)
                if i > 0:
                    tLoc = mm(bC[:, 128:193],
                              elb[:, ((i - 1) % 5) * 256 + 128:
                                  ((i - 1) % 5) * 256 + 256],
                              v_aug[:, (i - 1) * 1056 + h * 66:
                                    (i - 1) * 1056 + h * 66 + 65],
                              False, True, inc=sPE)
                p.wait("vector", sPE, tLoc)
                p.op("vector", lambda e, bC=bC: e.tensor_copy(
                    out=outc[:, 0:65], in_=bC[:, 0:65]))
                p.op("vector", lambda e, bS=bS: e.tensor_copy(
                    out=outc[:, 65:130], in_=bS[:, 0:65]))
                tOc = p.op("vector", lambda e, bC=bC: e.tensor_copy(
                    out=outc[:, 130:195], in_=bC[:, 128:193]))
                zview = AP(outc.tensor, outc.offset + 64,
                           [[AR32, 128], [65, 3]])
                p.op("vector", lambda e, z=zview: e.tensor_scalar(
                    out=z3s[:, 0:3], in0=z, scalar1=1e-30, scalar2=None,
                    op0=ALU.add))
                p.op("vector", lambda e: e.reciprocal(
                    out=z3s[:, 3:6], in_=z3s[:, 0:3]))
                p.op("vector", lambda e, i=i: e.tensor_tensor(
                    out=z3s[:, 0:3], in0=z3s[:, 3:6],
                    in1=gates_tm[:, i * 4:i * 4 + 3], op=ALU.mult))
                p.op("vector", lambda e: e.tensor_scalar(
                    out=tmp64, in0=outc[:, 0:64], scalar1=z3s[:, 0:1],
                    scalar2=None, op0=ALU.mult))
                p.op("vector", lambda e: e.scalar_tensor_tensor(
                    out=tmp64, in0=outc[:, 65:129], scalar=z3s[:, 1:2],
                    in1=tmp64, op0=ALU.mult, op1=ALU.add))
                tMix = p.op("vector", lambda e, ii=ii, h=h:
                            e.scalar_tensor_tensor(
                                out=mixq[:, ii * 1024 + h * 64:
                                         ii * 1024 + h * 64 + 64],
                                in0=outc[:, 130:194], scalar=z3s[:, 2:3],
                                in1=tmp64, op0=ALU.mult, op1=ALU.add))
                tOc_h[g][ii] = tOc
                tMix_h[g] = tMix

        a2_load(0)
        a2_load(1)
        for g in range(NGH):
            a2_head(g)
            if g % H == H - 1:
                tq = g // H
                p.wait("tensor", sV, tMix_h[g])
                tMTe_q = {}
                for ii in range(4):
                    i = tq * 4 + ii
                    if ii >= 2:
                        p.wait("tensor", sV, tMTe_q[ii - 2])
                    tTr = None
                    for c in range(NC):
                        tTr = p.op("tensor", lambda e, ii=ii, c=c:
                                   e.transpose(
                                       pb16[ii % 2][:, ts(c)],
                                       mixq[:, ii * 1024 + c * 128:
                                            ii * 1024 + c * 128 + 128],
                                       identity[:]),
                                   inc=sPE if c == NC - 1 else None)
                    p.wait("vector", sPE, tTr)
                    dstm = AP(mixT[:].tensor, i * 128,
                              [[NC * T, 128], [T, NC], [1, 128]])
                    tMTe = p.op("vector", lambda e, dstm=dstm, ii=ii:
                                e.tensor_copy(
                                    out=dstm, in_=pb16[ii % 2].rearrange(
                                        "p (c f) -> p c f", c=NC)))
                    tMTe_q[ii] = tMTe
                    tMTe_hist.append(tMTe)
        p.barrier(ALLSEMS)

    cv = Carver()
    hT = cv.c16(NFC * 1024)
    _markM = cv.o16
    xn_half = cv.c16(NC * 1024)
    wbuf = [cv.c16(8 * 128) for _ in range(4)]
    gsil = cv.c16(1024)
    cv.o16 = _markM
    wdres2 = [cv.c16(16 * 512), cv.c16(16 * 512)]
    wores = cv.c16(NC * 512)
    xq_f32 = [cv.c32(512) for _ in range(2)]
    out_sb = [cv.c32(512) for _ in range(4)]

    stXnH = mkstream("stXnH")
    stWo2 = mkstream("stWo2")
    stWg = [mkstream("stWg0"), mkstream("stWg1")]
    stWu = [mkstream("stWu0"), mkstream("stWu1")]
    stWd = mkstream("stWd")
    stXQ = [mkstream("stXQ0"), mkstream("stXQ1")]
    stOut = [mkstream(f"stOut{i}") for i in range(4)]
    store_hist = {}
    add_hist = {}
    for half in range(2):
        p.barrier(ALLSEMS)
        xn_src = AP(xn_dram.tensor, half * 1024,
                    [[NC * T, 128], [T, NC], [1, 1024]])
        t_xnh = dma("sync", stXnH, xn_half.rearrange("p (c t) -> p c t", c=NC),
                    xn_src)
        pe_hist = {}
        ht_hist = {}
        for fc in range(NFC):
            bgw = wbuf[2 * (fc % 2)]
            buw = wbuf[2 * (fc % 2) + 1]
            if fc >= 2:
                p.wait("gpsimd", sPE, pe_hist[fc - 2])
            wsrc_g = AP(Wfg.tensor, fc * 128,
                        [[DCD, 128], [128 * DCD, NC], [1, 128]])
            wsrc_u = AP(Wfu.tensor, fc * 128,
                        [[DCD, 128], [128 * DCD, NC], [1, 128]])
            tWg = dma("gpsimd", stWg[fc % 2],
                      bgw.rearrange("p (k f) -> p k f", k=NC), wsrc_g)
            tWu = dma("gpsimd", stWu[fc % 2],
                      buw.rearrange("p (k f) -> p k f", k=NC), wsrc_u)
            p.wait("tensor", stWg[fc % 2], tWg)
            p.wait("tensor", stWu[fc % 2], tWu)
            if fc == 0:
                p.wait("tensor", stXnH, t_xnh)
                p.wait("tensor", stC, t_const)
            for ph in range(2):
                mm_chain(pf[0 + ph][:, :],
                         [(bgw[:, ts(k)],
                           xn_half[:, k * 1024 + ph * 512:
                                   k * 1024 + ph * 512 + 512])
                          for k in range(NC)])
            tU = None
            for ph in range(2):
                tU = mm_chain(pf[2 + ph][:, :],
                              [(buw[:, ts(k)],
                                xn_half[:, k * 1024 + ph * 512:
                                        k * 1024 + ph * 512 + 512])
                               for k in range(NC)],
                              inc=sPE if ph == 1 else None)
            pe_hist[fc] = tU
            p.wait("scalar", sPE, tU)
            if fc >= 1:
                p.wait("scalar", sV, ht_hist[fc - 1])
            tSg = None
            for ph in range(2):
                tSg = p.op("scalar", lambda e, ph=ph, fc=fc: e.activation(
                    out=gsil[:, ph * 512:ph * 512 + 512], in_=pf[ph][:, :],
                    func=AF.Silu, bias=biasg_sb[:, fc:fc + 1]),
                    inc=sA if ph == 1 else None)
            p.wait("vector", sA, tSg)
            tHT = None
            for ph in range(2):
                tHT = p.op("vector", lambda e, ph=ph, fc=fc:
                           e.scalar_tensor_tensor(
                               out=hT[:, fc * 1024 + ph * 512:
                                      fc * 1024 + ph * 512 + 512],
                               in0=pf[2 + ph][:, :],
                               scalar=biasu_sb[:, fc:fc + 1],
                               in1=gsil[:, ph * 512:ph * 512 + 512],
                               op0=ALU.add, op1=ALU.mult),
                           inc=sV if ph == 1 else None)
            ht_hist[fc] = tHT
            p.wait("tensor", sA, tSg)
            p.wait("tensor", sV, tHT)

        if half == 0:
            t_wdc = {}
            wdA_last = {}
            wdB_last = {}
            stWdc = [mkstream("stWdc0"), mkstream("stWdc1")]

        def load_wd(gdh):
            dh_ = gdh % 2
            p.wait("gpsimd", sV, ht_hist[NFC - 1])
            for c in range(2):
                if gdh >= 1:
                    tok = wdA_last[gdh - 1] if c == 0 else wdB_last[gdh - 1]
                    p.wait("gpsimd", sPE, tok)
                wd_src = AP(Wfd.tensor, c * 16 * 128 * D + dh_ * 512,
                            [[D, 128], [128 * D, 16], [1, 512]])
                t_wdc[(gdh, c)] = dma("gpsimd", stWdc[c],
                                      wdres2[c].rearrange(
                                          "p (k f) -> p k f", k=16), wd_src)

        for dh in range(2):
            gdh = half * 2 + dh
            dcols = slice(dh * 512, dh * 512 + 512)
            if dh == 0:
                load_wd(gdh)
            p.wait("tensor", stWdc[0], t_wdc[(gdh, 0)])
            if stage >= 3:
                wo_src = AP(Wo_d.tensor, dh * 512,
                            [[D, 128], [128 * D, NC], [1, 512]])
                if gdh >= 1:
                    p.wait("gpsimd", sPE, wdB_last[gdh - 1])
                t_wo = dma("gpsimd", stWo2,
                           wores.rearrange("p (k f) -> p k f", k=NC), wo_src)
            for tile in range(8):
                gt = half * 8 + tile
                pq = pf[4 + tile % 2]
                tA = None
                for fc in range(16):
                    tA = mm(pq[:, 0:512],
                            hT[:, fc * 1024 + tile * 128:
                               fc * 1024 + tile * 128 + 128],
                            wdres2[0][:, fc * 512:fc * 512 + 512],
                            fc == 0, False,
                            inc=sPE if fc == 15 else None)
                if tile == 7:
                    wdA_last[gdh] = tA
                if tile == 0:
                    p.wait("tensor", stWdc[1], t_wdc[(gdh, 1)])
                    if stage >= 3:
                        p.wait("tensor", stWo2, t_wo)
                tMM = None
                for fc in range(16, NFC):
                    last = (stage < 3 and fc == NFC - 1)
                    tMM = mm(pq[:, 0:512],
                             hT[:, fc * 1024 + tile * 128:
                                fc * 1024 + tile * 128 + 128],
                             wdres2[1][:, (fc - 16) * 512:
                                       (fc - 16) * 512 + 512],
                             False, last, inc=sPE if last else None)
                if stage >= 3:
                    for c in range(NC):
                        tMM = mm(pq[:, 0:512],
                                 mixT[:, c * T + gt * 128:
                                      c * T + gt * 128 + 128],
                                 wores[:, c * 512:c * 512 + 512],
                                 False, c == NC - 1,
                                 inc=sPE if c == NC - 1 else None)
                if tile == 7:
                    wdB_last[gdh] = tMM
                    if dh == 0:
                        load_wd(gdh + 1)
                xb = xq_f32[tile % 2]
                p.wait("sync", sV, add_hist.get(tile % 2, 0))
                tX = dma("sync", stXQ[tile % 2], xb, x_in[ts(gt), dcols])
                obi = tile % 2 + 2 * (dh % 2)
                ob = out_sb[obi]
                p.wait("vector", sPE, tMM)
                p.wait("vector", stXQ[tile % 2], tX)
                p.wait("vector", stOut[obi], store_hist.get(obi, 0))
                tAdd = p.op("vector", lambda e, pq=pq, xb=xb, ob=ob:
                            e.tensor_tensor(out=ob, in0=pq[:, 0:512], in1=xb,
                                            op=ALU.add), inc=sV)
                add_hist[tile % 2] = tAdd
                if stage >= 2:
                    p.op("vector", lambda e, ob=ob, gt=gt, dh=dh:
                         e.tensor_tensor(
                             out=ob, in0=ob,
                             in1=expacc[:, gt * D + dh * 512:
                                        gt * D + dh * 512 + 512],
                             op=ALU.add))
                bdap = bdrep_sb[:, dcols]
                tO = p.op("vector", lambda e, ob=ob, bdap=bdap:
                          e.tensor_tensor(out=ob, in0=ob, in1=bdap,
                                          op=ALU.add), inc=sV)
                p.wait("tensor", sV, tO)
                p.wait("sync", sV, tO)
                store_hist[obi] = dma("sync", stOut[obi],
                                      out_t[ts(gt), dcols], ob)

    for st in (stOut + [stSp[0], stSp[1]]):
        p.wait("sync", st, p.cur(st))
    p.emit()
    return nc



def _shard_inputs(inputs, core, stage=STAGE):
    d = {}
    d["x"] = np.ascontiguousarray(np.asarray(inputs["L0_attn_x"])[core])
    d["Wfg"] = np.asarray(inputs["L0_ffn_Wg"])
    d["Wfu"] = np.asarray(inputs["L0_ffn_Wu"])
    d["bfg_c"] = np.ascontiguousarray(
        np.asarray(inputs["L0_ffn_bg"]).reshape(NFC, 128).T)
    d["bfu_c"] = np.ascontiguousarray(
        np.asarray(inputs["L0_ffn_bu"]).reshape(NFC, 128).T)
    d["Wfd"] = np.asarray(inputs["L0_ffn_Wd"])
    d["bd_rep"] = np.ascontiguousarray(
        np.broadcast_to(np.asarray(inputs["L0_ffn_bd"])[None, :], (128, D)))
    d["ident_bf"] = np.eye(128, dtype=ml_dtypes.bfloat16)
    if stage >= 2:
        d["Wfq"] = np.asarray(inputs["L0_ffn_Wq"])
        d["keys"] = np.ascontiguousarray(
            np.asarray(inputs["L0_ffn_keys"]).reshape(FH * NK * 2, NK))
        d["down_embed"] = np.asarray(inputs["L0_ffn_down_embed"])
        d["up_embed"] = np.asarray(inputs["L0_ffn_up_embed"])
    if stage >= 3:
        d["Wq"] = np.asarray(inputs["L0_attn_Wq"])
        d["Wk"] = np.asarray(inputs["L0_attn_Wk"])
        d["Wv"] = np.asarray(inputs["L0_attn_Wv"])
        d["Wo"] = np.asarray(inputs["L0_attn_Wo"])
        d["Wck"] = np.asarray(inputs["L0_attn_Wck"])
        d["Wcv"] = np.asarray(inputs["L0_attn_Wcv"])
        d["Wg_att"] = np.asarray(inputs["L0_attn_Wg"])
        d["bg_att_rep"] = np.ascontiguousarray(np.broadcast_to(
            np.asarray(inputs["L0_attn_bg"])[None, :], (128, 3)))
        pos = np.arange(T)
        blk = np.arange(NB)
        cok = np.where(pos[:, None] >= (blk[None, :] + 1) * G, 0.0, -BIGM)
        d["cok_tm"] = cok.astype(ml_dtypes.bfloat16)
        koff = np.arange(128)
        toff = np.arange(256)
        dtk = toff[None, :] - koff[:, None]
        bm = np.where((dtk >= 0) & (dtk <= 32), 0.0, -BIGM).astype(np.float32)
        bmv = np.stack([bm, bm], axis=0)
        d["bandmask"] = bmv.astype(ml_dtypes.bfloat16)
        d["causal64"] = np.triu(np.ones((64, 64), np.float32)).astype(
            ml_dtypes.bfloat16)
        be = np.zeros((16, 128, 128), np.float32)
        for c in range(16):
            for kk in range(128):
                be[c, 8 * c + kk // 16, kk] = BIGM
        d["bigexp"] = be.astype(ml_dtypes.bfloat16)
        am = np.zeros((D, HD), np.float32)
        for h in range(H):
            am[h * HD:(h + 1) * HD, :] = np.eye(HD) / 16.0
        d["ameanT"] = np.ascontiguousarray(am.T).astype(ml_dtypes.bfloat16)
        d["cokT"] = np.ascontiguousarray(
            np.asarray(d["cok_tm"], np.float32).T).astype(ml_dtypes.bfloat16)
    return d


def kernel(**inputs):
    n = 8
    nc = build_nc(STAGE)
    in_maps = [_shard_inputs(inputs, c) for c in range(n)]
    res = run_bass_kernel_spmd(nc, in_maps, core_ids=list(range(n)))
    out = np.stack([np.asarray(res.results[c]["out"]) for c in range(n)],
                   axis=0)
    return out.astype(np.float32)


# version 13
# speedup vs baseline: 1.0331x; 1.0083x over previous
import numpy as np
import ml_dtypes

import concourse.bass as bass
import concourse.mybir as mybir
from concourse.ap import AP
from concourse.bass_utils import run_bass_kernel_spmd

F32 = mybir.dt.float32
BF16 = mybir.dt.bfloat16
AF = mybir.ActivationFunctionType
ALU = mybir.AluOpType
AX = mybir.AxisListType

T = 2048
D = 1024
NT = T // 128
NC = D // 128
DCD = 4096
NFC = DCD // 128
H = 16
HD = 64
G = 16
NB = 128
FH = 4
NK = 64
NEXP = 4096
BIGM = 32768.0
EPS_SEL = 1e-6

STAGE = 3


class Plan:
    ENGINES = ("sync", "gpsimd", "vector", "scalar", "tensor")

    def __init__(self, nc):
        self.nc = nc
        self.q = {e: [] for e in self.ENGINES}
        self.counts = {}
        self.selfsems = {}

    def op(self, eng, fn, inc=None, inc_by=1):
        ss = self.selfsems.get(eng)
        if ss is not None:
            prior = self.cur(ss)
            if prior:
                self.q[eng].append(
                    lambda e, s=ss, v=prior: e.wait_ge(s, v))
            if inc is None:
                inc, inc_by = ss, 1
            else:
                assert inc is ss, f"{eng} ops must inc their self-sem"
        tgt = None
        if inc is not None:
            key = id(inc)
            tgt = self.counts.get(key, 0) + inc_by
            self.counts[key] = tgt
            self.q[eng].append(
                lambda e, f=fn, s=inc, v=inc_by: f(e).then_inc(s, v))
        else:
            self.q[eng].append(fn)
        return tgt

    def cur(self, sem):
        return self.counts.get(id(sem), 0)

    def wait(self, eng, sem, val):
        if val:
            self.q[eng].append(lambda e, s=sem, v=val: e.wait_ge(s, v))

    def barrier(self, sems):
        vals = [(s, self.cur(s)) for s in sems]
        for eng in self.ENGINES:
            for s, v in vals:
                self.wait(eng, s, v)

    def emit(self):
        with self.nc.Block() as block:
            for name in self.ENGINES:
                qlist = self.q[name]

                def body(eng, qlist=qlist):
                    for fn in qlist:
                        fn(eng)
                getattr(block, name)(body)


def ts(i, n=128):
    return slice(i * n, (i + 1) * n)


def build_nc(stage=STAGE, debug=0):
    nc = bass.Bass("TRN2", target_bir_lowering=False)

    def din(name, shape, dtype=F32):
        return nc.dram_tensor(name, list(shape), dtype, kind="ExternalInput").ap()

    x_in = din("x", (T, D))
    Wfg = din("Wfg", (D, DCD))
    Wfu = din("Wfu", (D, DCD))
    bfg_c = din("bfg_c", (128, NFC))
    bfu_c = din("bfu_c", (128, NFC))
    Wfd = din("Wfd", (DCD, D))
    bd_rep = din("bd_rep", (128, D))
    ident_bf = din("ident_bf", (128, 128), BF16)
    if stage >= 2:
        Wfq = din("Wfq", (D, 512))
        keys_in = din("keys", (FH * NK * 2, NK))
        d_emb = din("down_embed", (NEXP, D))
        u_emb = din("up_embed", (NEXP, D))
    if stage >= 3:
        Wq_d = din("Wq", (D, D)); Wk_d = din("Wk", (D, D))
        Wv_d = din("Wv", (D, D)); Wo_d = din("Wo", (D, D))
        Wck_d = din("Wck", (D, HD)); Wcv_d = din("Wcv", (D, HD))
        Wg_att = din("Wg_att", (HD, 3))
        bg_att_rep = din("bg_att_rep", (128, 3))
        cok_tm = din("cok_tm", (T, NB), BF16)
        cokT_in = din("cokT", (NB, T), BF16)
        bandmask = din("bandmask", (2, 128, 256), BF16)
        causal64 = din("causal64", (64, 64), BF16)
        bigexp = din("bigexp", (16, 128, 128), BF16)
        ameanT = din("ameanT", (HD, D), BF16)

    out_t = nc.dram_tensor("out", [T, D], F32, kind="ExternalOutput").ap()
    xn_dram = nc.dram_tensor("xn_spill", [128, NC * T], BF16,
                             kind="ExternalOutput" if debug else "Internal").ap()
    if stage >= 2:
        deT_dram = nc.dram_tensor("deT_spill", [D, NEXP], BF16,
                                  kind="Internal").ap()
    if stage >= 3:
        dbg_kind = "ExternalOutput" if debug >= 2 else "Internal"
        q_dram = nc.dram_tensor("q_spill", [128, NC * T], BF16,
                                kind=dbg_kind).ap()
        k_dram = nc.dram_tensor("k_spill", [128, NC * T], BF16,
                                kind=dbg_kind).ap()
        v_dram = nc.dram_tensor("v_spill", [128, NC * T], BF16,
                                kind=dbg_kind).ap()
        mixT_dram = (nc.dram_tensor("mixT_dump", [128, NC * T], BF16,
                                    kind="ExternalOutput").ap()
                     if debug >= 2 else None)
        gates_dram = (nc.dram_tensor("gates_dump", [128, NT * 4], F32,
                                     kind="ExternalOutput").ap()
                      if debug >= 2 else None)
        awg_dram = (nc.dram_tensor("awg_dump", [128, 32], BF16,
                                   kind="ExternalOutput").ap()
                    if debug >= 2 else None)
        bg_dram = (nc.dram_tensor("bg_dump", [128, 16], F32,
                                  kind="ExternalOutput").ap()
                   if debug >= 2 else None)
        gprobe = (nc.dram_tensor("gates_probe", [128, 32], F32,
                                 kind="ExternalOutput").ap()
                  if debug >= 2 else None)
        gprobe2 = (nc.dram_tensor("gates_probe2", [128, 64], F32,
                                  kind="ExternalOutput").ap()
                   if debug >= 2 else None)
        gprobe3 = (nc.dram_tensor("gates_probe3", [128, 32], BF16,
                                  kind="ExternalOutput").ap()
                   if debug >= 2 else None)

    p = Plan(nc)

    def sbuf(name, shape, dtype):
        return nc.alloc_sbuf_tensor(name, list(shape), dtype)

    def psum(name, shape, dtype=F32):
        return nc.alloc_psum_tensor(name, list(shape), dtype)

    identity = sbuf("identity", (128, 128), BF16)
    bdrep_sb = sbuf("bdrep_sb", (128, D), F32)
    biasg_sb = sbuf("biasg_sb", (128, NFC), F32)
    biasu_sb = sbuf("biasu_sb", (128, NFC), F32)
    expacc = sbuf("expacc", (128, NT * D), BF16)
    mixT = sbuf("mixT", (128, NC * T), BF16)
    gates_tm = sbuf("gates_tm", (128, NT * 4), F32)
    thp_cols = sbuf("thp_cols", (128, NT * FH), F32)
    mzl_cols = sbuf("mzl_cols", (128, NT * FH), F32)
    tcol_cols = sbuf("tcol_cols", (128, NT * FH), F32)
    constcol = sbuf("constcol", (128, 4), F32)
    AR16 = 57 * 1024 + 192
    AR32 = 5 * 1024
    arena16 = sbuf("arena16", (128, AR16), BF16)
    arena32 = sbuf("arena32", (128, AR32), F32)

    class Carver:
        def __init__(self):
            self.o16 = 0
            self.o32 = 0

        def c16(self, n):
            v = arena16[:, self.o16:self.o16 + n]
            self.o16 += n
            assert self.o16 <= AR16, f"arena16 overflow {self.o16}"
            return v

        def c32(self, n):
            v = arena32[:, self.o32:self.o32 + n]
            self.o32 += n
            assert self.o32 <= AR32, f"arena32 overflow {self.o32}"
            return v

    pf = [psum(f"pf{i}", (128, 512), F32) for i in range(6)]
    pb16 = [psum(f"pb{i}", (128, 1024), BF16) for i in range(2)]

    sPE = nc.alloc_semaphore("sPE")
    sV = nc.alloc_semaphore("sV")
    sA = nc.alloc_semaphore("sA")
    sG = nc.alloc_semaphore("sG")
    ALLSEMS = [sPE, sV, sA, sG]
    p.selfsems = {"vector": sV, "scalar": sA}

    _streams = []

    def mkstream(name):
        s = nc.alloc_semaphore(name)
        ALLSEMS.append(s)
        _streams.append(s)
        return s

    def dma(eng, stream, outap, inap, **kw):
        return p.op(eng, lambda e: e.dma_start(out=outap, in_=inap, **kw),
                    inc=stream, inc_by=16)

    def mm(ps, lhsT, rhs, start, stop, inc=None):
        return p.op("tensor", lambda e: e.matmul(ps, lhsT, rhs, start=start,
                                                 stop=stop), inc=inc)

    def mm_chain(ps, pairs, inc=None):
        n = len(pairs)
        t = None
        for i, (l, r) in enumerate(pairs):
            t = mm(ps, l, r, i == 0, i == n - 1,
                   inc=inc if i == n - 1 else None)
        return t

    p.op("vector", lambda e: e.memset(constcol[:, 0:1], -4096.0))
    stC = mkstream("stC")
    dma("sync", stC, identity[:], ident_bf[:, :])
    dma("sync", stC, bdrep_sb[:], bd_rep[:, :])
    dma("sync", stC, biasg_sb[:], bfg_c[:, :])
    t_const = dma("sync", stC, biasu_sb[:], bfu_c[:, :])

    cv = Carver()
    x_f32 = [cv.c32(D) for _ in range(2)]
    ln_sq = cv.c32(D)
    ln_st = cv.c32(3 * NT)
    ln_rs = cv.c32(2 * NT)
    ln_xn2 = [cv.c16(D), cv.c16(D)]
    xn_stage = [cv.c16(D) for _ in range(2)]

    stXL = [mkstream("stXL0"), mkstream("stXL1")]
    stSp = [mkstream("stSp0"), mkstream("stSp1")]
    spill_hist = {}
    e0_ev = {}
    if stage >= 2:
        de_in = [cv.c16(1024) for _ in range(2)]
        de_stage = [cv.c16(1024) for _ in range(2)]
        stDE = [mkstream("stDE0"), mkstream("stDE1")]
        stDEo = [mkstream("stDEo0"), mkstream("stDEo1")]
        ev_hist = {}
        st_hist = {}

        def emit_e0_iter(ec):
            sl = ec % 2
            if ec == 0:
                p.wait("tensor", stC, t_const)
            if ec >= 2:
                p.wait("gpsimd", sPE, ev_hist[ec - 2][0])
            tDE = dma("gpsimd", stDE[sl], de_in[sl], d_emb[ts(ec), :])
            p.wait("tensor", stDE[sl], tDE)
            pbk = pb16[sl]
            if ec >= 2:
                p.wait("tensor", sV, ev_hist[ec - 2][1])
            tT = None
            for c in range(NC):
                tT = p.op("tensor", lambda e, c=c, pbk=pbk, src=de_in[sl]:
                          e.transpose(pbk[:, ts(c)], src[:, ts(c)],
                                      identity[:]),
                          inc=sPE if c == NC - 1 else None)
            p.wait("vector", sPE, tT)
            if ec >= 2:
                p.wait("vector", stDEo[sl], st_hist[ec - 2])
            tEv = p.op("vector", lambda e, d=de_stage[sl], pbk=pbk:
                       e.tensor_copy(out=d, in_=pbk[:, :]))
            ev_hist[ec] = (tT, tEv)
            p.wait("sync", sV, tEv)
            dst = AP(deT_dram.tensor, ec * 128,
                     [[NEXP, 128], [128 * NEXP, NC], [1, 128]])
            st_hist[ec] = dma("sync", stDEo[sl], dst,
                              de_stage[sl].rearrange("p (c f) -> p c f",
                                                     c=NC))
    sum_hist = {}
    for i in range(NT):
        buf = x_f32[i % 2]
        if i >= 2:
            p.wait("sync", sV, sum_hist[i - 2])
        tL = dma("sync", stXL[i % 2], buf, x_in[ts(i), :])
        p.wait("vector", stXL[i % 2], tL)
        p.op("vector", lambda e, b=buf, i=i: e.reduce_sum(
            out=ln_st[:, i:i + 1], in_=b, axis=AX.X))
        p.op("vector", lambda e, b=buf: e.tensor_tensor(
            out=ln_sq, in0=b, in1=b, op=ALU.mult))
        sum_hist[i] = p.op("vector", lambda e, i=i: e.reduce_sum(
            out=ln_st[:, NT + i:NT + i + 1], in_=ln_sq, axis=AX.X))
        if stage >= 2:
            emit_e0_iter(2 * i)
            emit_e0_iter(2 * i + 1)
    if stage >= 2:
        e0_ev = {0: ev_hist[30][1], 1: ev_hist[31][1]}
    p.op("vector", lambda e: e.tensor_scalar(
        out=ln_st[:, 0:2 * NT], in0=ln_st[:, 0:2 * NT], scalar1=1.0 / D,
        scalar2=None, op0=ALU.mult))
    p.op("vector", lambda e: e.tensor_tensor(
        out=ln_st[:, 2 * NT:3 * NT], in0=ln_st[:, 0:NT],
        in1=ln_st[:, 0:NT], op=ALU.mult))
    p.op("vector", lambda e: e.tensor_tensor(
        out=ln_st[:, NT:2 * NT], in0=ln_st[:, NT:2 * NT],
        in1=ln_st[:, 2 * NT:3 * NT], op=ALU.subtract))
    tVar = p.op("vector", lambda e: e.tensor_scalar(
        out=ln_st[:, NT:2 * NT], in0=ln_st[:, NT:2 * NT], scalar1=1e-5,
        scalar2=None, op0=ALU.add), inc=sV)
    p.wait("scalar", sV, tVar)
    tQ = p.op("scalar", lambda e: e.activation(
        out=ln_st[:, 2 * NT:3 * NT], in_=ln_st[:, NT:2 * NT],
        func=AF.Sqrt), inc=sA)
    p.wait("vector", sA, tQ)
    p.op("vector", lambda e: e.reciprocal(
        out=ln_rs[:, 0:NT], in_=ln_st[:, 2 * NT:3 * NT]))
    p.op("vector", lambda e: e.tensor_scalar(
        out=ln_rs[:, NT:2 * NT], in0=ln_st[:, 0:NT], scalar1=-1.0,
        scalar2=None, op0=ALU.mult))
    p.op("vector", lambda e: e.tensor_tensor(
        out=ln_rs[:, NT:2 * NT], in0=ln_rs[:, NT:2 * NT],
        in1=ln_rs[:, 0:NT], op=ALU.mult))
    act_hist = {}
    tr_hist = {}
    p.wait("sync", sV, sum_hist[NT - 1])
    for i in range(NT):
        buf = x_f32[i % 2]
        if i >= 2:
            p.wait("sync", sV, act_hist[i - 2])
        tL = dma("sync", stXL[i % 2], buf, x_in[ts(i), :])
        p.wait("vector", stXL[i % 2], tL)
        if i >= 2:
            p.wait("vector", sPE, tr_hist[i - 2])
        if i < 2 and i in e0_ev:
            p.wait("tensor", sV, e0_ev[i])
        lxn = ln_xn2[i % 2]
        tXN = p.op("vector", lambda e, b=buf, i=i, lxn=lxn: e.tensor_scalar(
            out=lxn, in0=b, scalar1=ln_rs[:, i:i + 1],
            scalar2=ln_rs[:, NT + i:NT + i + 1],
            op0=ALU.mult, op1=ALU.add), inc=sV)
        act_hist[i] = tXN
        pbk = pb16[i % 2]
        p.wait("tensor", sV, tXN)
        p.wait("tensor", stC, t_const)
        if i >= 2:
            p.wait("tensor", sV, spill_hist[i - 2][0])
        tT = None
        for c in range(NC):
            tT = p.op("tensor", lambda e, c=c, pbk=pbk, lxn=lxn: e.transpose(
                pbk[:, ts(c)], lxn[:, ts(c)], identity[:]),
                inc=sPE if c == NC - 1 else None)
        p.wait("vector", sPE, tT)
        stg = xn_stage[i % 2]
        if i >= 2:
            p.wait("vector", stSp[i % 2], spill_hist[i - 2][1])
        tEv = p.op("vector", lambda e, stg=stg, pbk=pbk: e.tensor_copy(
            out=stg, in_=pbk[:, :]), inc=sV)
        p.wait("sync", sV, tEv)
        dstap = AP(xn_dram.tensor, i * 128,
                   [[NC * T, 128], [T, NC], [1, 128]])
        tSp = dma("sync", stSp[i % 2], dstap,
                  stg.rearrange("p (c f) -> p c f", c=NC))
        spill_hist[i] = (tEv, tSp)
        tr_hist[i] = tT

    p.barrier(ALLSEMS)

    if stage >= 2:
        cv = Carver()
        xn_sb = cv.c16(NC * T)
        sims = cv.c16(NT * 512)
        sxm = cv.c16(32)
        _mark16 = cv.o16
        qfst = [cv.c16(T) for _ in range(8)]
        keysq = cv.c16(8 * 64)
        keysT = cv.c16(8 * 64)
        wq_buf = cv.c16(8 * 128)
        eyall = cv.c16(NT * 256)
        c_v8x = cv.c32(8); c_v8y = cv.c32(8); c_osc = cv.c32(64)
        c_v8 = cv.c32(8); c_negm = cv.c32(1); c_ex8 = cv.c32(8)
        c_z = cv.c32(1); c_lnz = cv.c32(1); c_targ = cv.c32(1)
        stXN2 = mkstream("stXN2")
        stK = mkstream("stK")
        stWQ = mkstream("stWQ")
        t_xn = dma("sync", stXN2, xn_sb.rearrange("p (c t) -> p c t", c=NC),
                   AP(xn_dram.tensor, 0, [[NC * T, 128], [T, NC], [1, T]]))
        for ph in range(8):
            pp, hh = ph // 4, ph % 4
            src = AP(keys_in.tensor, (hh * 128 + pp) * NK,
                     [[2 * NK, NK], [1, NK]])
            dma("gpsimd", stK, keysq[0:64, ph * 64:(ph + 1) * 64], src)
        tK = p.cur(stK)
        p.wait("tensor", stK, tK)
        tKT = None
        for ph in range(8):
            tKT = p.op("tensor", lambda e, ph=ph: e.transpose(
                pb16[0][0:64, ph * 64:(ph + 1) * 64],
                keysq[0:64, ph * 64:(ph + 1) * 64],
                identity[0:64, 0:64]), inc=sPE if ph == 7 else None)
        p.wait("vector", sPE, tKT)
        tKE = p.op("vector", lambda e: e.tensor_copy(
            out=keysT[0:64, 0:512], in_=pb16[0][0:64, 0:512]))
        p.wait("tensor", sV, tKE)
        p.wait("tensor", stXN2, t_xn)
        for qc in range(4):
            wsrc = AP(Wfq.tensor, qc * 128,
                      [[512, 128], [128 * 512, NC], [1, 128]])
            tW = dma("gpsimd", stWQ,
                     wq_buf.rearrange("p (k f) -> p k f", k=NC), wsrc)
            p.wait("tensor", stWQ, tW)
            tQ4 = None
            for ph4 in range(4):
                tQ4 = mm_chain(pf[ph4][:, :],
                               [(wq_buf[:, ts(k)],
                                 xn_sb[:, k * T + ph4 * 512:
                                       k * T + ph4 * 512 + 512])
                                for k in range(NC)],
                               inc=sPE if ph4 == 3 else None)
            p.wait("scalar", sPE, tQ4)
            tEq = None
            for hf in range(2):
                for ph4 in range(4):
                    tEq = p.op("scalar", lambda e, qc=qc, hf=hf, ph4=ph4:
                               e.activation(
                                   out=qfst[2 * qc + hf][0:64,
                                       ph4 * 512:ph4 * 512 + 512],
                                   in_=pf[ph4][hf * 64:hf * 64 + 64, :],
                                   func=AF.Copy, scale=1.0))
            p.wait("tensor", sA, tEq)
            p.wait("gpsimd", sPE, tQ4)
        v8all = cv.c32(NT * FH * 8)
        negmall = cv.c32(NT * FH)
        vsh = cv.c32(NT * FH * 8)
        zsall = cv.c32(NT * FH)
        lnzall = cv.c32(NT * FH)
        for i in range(NT):
            tSm = None
            for ph in range(8):
                tSm = mm(pf[4 + (i % 2)][:, ph * 64:(ph + 1) * 64],
                         qfst[ph][0:64, ts(i)],
                         keysT[0:64, ph * 64:(ph + 1) * 64],
                         True, True, inc=sPE if ph == 7 else None)
            p.wait("scalar", sPE, tSm)
            tSe = p.op("scalar", lambda e, i=i: e.activation(
                out=sims[:, i * 512:(i + 1) * 512],
                in_=pf[4 + (i % 2)][:, :], func=AF.Copy, scale=1.0))
            p.wait("tensor", sA, tSe)
            p.wait("vector", sA, tSe)
            base = i * 512
            for h in range(4):
                col = i * FH + h
                sx_ap = sims[:, base + h * 64:base + h * 64 + 64]
                sy_ap = sims[:, base + 256 + h * 64:base + 256 + h * 64 + 64]
                p.op("vector", lambda e, a=sx_ap: e.max(out=c_v8x, in_=a))
                p.op("vector", lambda e, a=sy_ap: e.max(out=c_v8y, in_=a))
                p.op("vector", lambda e: e.tensor_tensor(
                    out=c_osc.rearrange("p (i j) -> p i j", i=8),
                    in0=AP(c_v8x.tensor, c_v8x.offset,
                           [[AR32, 128], [1, 8], [0, 8]]),
                    in1=AP(c_v8y.tensor, c_v8y.offset,
                           [[AR32, 128], [0, 8], [1, 8]]),
                    op=ALU.add))
                p.op("vector", lambda e, col=col: e.max(
                    out=v8all[:, col * 8:col * 8 + 8], in_=c_osc))
        for i in range(NT):
            p.op("scalar", lambda e, i=i: e.activation(
                out=eyall[:, i * 256:(i + 1) * 256],
                in_=sims[:, i * 512 + 256:i * 512 + 512], func=AF.Exp))
        p.op("vector", lambda e: e.tensor_scalar(
            out=negmall, in0=AP(v8all.tensor, v8all.offset,
                                [[AR32, 128], [8, 64]]),
            scalar1=-1.0, scalar2=None, op0=ALU.mult))
        p.op("vector", lambda e: e.tensor_scalar(
            out=thp_cols[:, 0:64],
            in0=AP(v8all.tensor, v8all.offset + 7, [[AR32, 128], [8, 64]]),
            scalar1=EPS_SEL, scalar2=None, op0=ALU.subtract))
        eshb = cv.c32(NT * FH * 8)
        tVsh = p.op("vector", lambda e: e.tensor_tensor(
            out=vsh.rearrange("p (c j) -> p c j", c=64),
            in0=v8all.rearrange("p (c j) -> p c j", c=64),
            in1=AP(negmall.tensor, negmall.offset,
                   [[AR32, 128], [1, 64], [0, 8]]),
            op=ALU.add))
        p.wait("scalar", sV, tVsh)
        tEsh = p.op("scalar", lambda e: e.activation(
            out=eshb, in_=vsh, func=AF.Exp))
        p.wait("vector", sA, tEsh)
        tZs = p.op("vector", lambda e: e.reduce_sum(
            out=zsall.rearrange("p (c u) -> p c u", c=64),
            in_=eshb.rearrange("p (c j) -> p c j", c=64), axis=AX.X))
        p.wait("scalar", sV, tZs)
        tLn = p.op("scalar", lambda e: e.activation(
            out=lnzall, in_=zsall, func=AF.Ln))
        p.wait("vector", sA, tLn)
        p.op("vector", lambda e: e.tensor_tensor(
            out=mzl_cols[:, 0:64], in0=negmall, in1=lnzall,
            op=ALU.subtract))
        tTa = p.op("vector", lambda e: e.tensor_tensor(
            out=vsh[:, 0:64], in0=thp_cols[:, 0:64], in1=mzl_cols[:, 0:64],
            op=ALU.add))
        p.wait("scalar", sV, tTa)
        tTc = p.op("scalar", lambda e: e.activation(
            out=tcol_cols[:, 0:64], in_=vsh[:, 0:64], func=AF.Exp))
        p.wait("vector", sA, tTc)
        p.barrier(ALLSEMS)

        EG = 1024
        NKX = EG // 64
        cv.o16 = _mark16
        deTg2 = [cv.c16(NC * EG), mixT[:, 0:NC * EG]]
        ueg2 = [cv.c16(8 * 1024), mixT[:, NC * EG:NC * EG + 8 * 1024]]
        swt2 = [cv.c16(EG), cv.c16(EG)]
        cv.o16 += NT * 256
        eexp = cv.c16(EG)
        exe2 = [cv.c16(64), cv.c16(64)]
        emsk2 = [[cv.c16(EG) for _ in range(4)] for _ in range(2)]
        epw = cv.c16(EG)
        epwT = cv.c16(EG)
        stDT = [mkstream("stDT0"), mkstream("stDT1")]
        stUE = [mkstream("stUE0"), mkstream("stUE1")]
        NG = NEXP // EG
        NIT = NG * NT
        tDTh = {}; tUEh = {}
        tW_h = {}; tSW_h = {}; tMsk_h = {}; tEPW_h = {}; tEM_h = {}
        tTe_h = {}; tOut_h = {}; tAcc_h = {}

        def e2_load(gh):
            gp = gh % 2
            if gh >= 2:
                p.wait("gpsimd", sPE, tOut_h[(gh - 1) * NT - 1])
            tDTh[gh] = dma("gpsimd", stDT[gp],
                           deTg2[gp].rearrange("p (k t) -> p k t", k=NC),
                           AP(deT_dram.tensor, gh * EG,
                              [[NEXP, 128], [128 * NEXP, NC], [1, EG]]))
            tUEh[gh] = dma("gpsimd", stUE[gp],
                           ueg2[gp].rearrange("p (k t) -> p k t", k=8),
                           AP(u_emb.tensor, gh * EG * D,
                              [[D, 128], [128 * D, 8], [1, D]]))

        def e2_W(it):
            gh, i = it // NT, it % NT
            b = it % 2
            gp = gh % 2
            if i == 0:
                p.wait("tensor", stDT[gp], tDTh[gh])
                p.wait("tensor", stUE[gp], tUEh[gh])
            if it >= 2:
                p.wait("tensor", sV, tEPW_h[it - 2])
            t = None
            for pc in range(2):
                t = mm_chain(
                    pf[2 * b + pc][:, :],
                    [(xn_sb[:, k * T + i * 128:k * T + i * 128 + 128],
                      deTg2[gp][:, k * EG + pc * 512:k * EG + pc * 512 + 512])
                     for k in range(NC)],
                    inc=sPE if pc == 1 else None)
            tW_h[it] = t

        def e2_chain(it):
            gh, i = it // NT, it % NT
            b = it % 2
            if it >= 2:
                p.wait("scalar", sV, tMsk_h[it - 2])
            tE4 = None
            for h in range(4):
                col = i * FH + h
                sxs_ap = sims[:, i * 512 + h * 64 + NKX * gh:
                              i * 512 + h * 64 + NKX * gh + NKX]
                tE4 = p.op("scalar", lambda e, a=sxs_ap, c=col, h=h, b=b:
                           e.activation(
                               out=exe2[b][:, h * 16:h * 16 + 16],
                               in_=a, func=AF.Exp,
                               bias=mzl_cols[:, c:c + 1]))
            p.wait("scalar", sPE, tW_h[it])
            if it >= 2:
                p.wait("scalar", sV, tEPW_h[it - 2])
            tSW = None
            for pc in range(2):
                tSW = p.op("scalar", lambda e, pc=pc, b=b: e.activation(
                    out=swt2[b][:, pc * 512:pc * 512 + 512],
                    in_=pf[2 * b + pc][:, :], func=AF.Silu))
            tSW_h[it] = tSW
            p.wait("vector", sA, tE4)
            tMsk = None
            for h in range(4):
                col = i * FH + h
                ey_off = eyall.offset + i * 256 + h * 64
                tTT = p.op("vector", lambda e, h=h, b=b, eo=ey_off:
                           e.tensor_tensor(
                               out=eexp.rearrange("p (x y) -> p x y", x=NKX),
                               in0=AP(exe2[b].tensor,
                                      exe2[b].offset + h * 16,
                                      [[AR16, 128], [1, NKX], [0, 64]]),
                               in1=AP(eyall.tensor, eo,
                                      [[AR16, 128], [0, NKX], [1, 64]]),
                               op=ALU.mult))
                tMsk = p.op("vector", lambda e, h=h, c=col, b=b:
                            e.scalar_tensor_tensor(
                                out=emsk2[b][h], in0=eexp,
                                scalar=tcol_cols[:, c:c + 1],
                                in1=eexp, op0=ALU.is_ge, op1=ALU.mult))
            tMsk_h[it] = tMsk

        def e2_EM(it):
            gh, i = it // NT, it % NT
            b = it % 2
            p.wait("tensor", sA, tSW_h[it])
            p.wait("tensor", sV, tMsk_h[it])
            tEM = None
            for pc in range(2):
                for h in range(4):
                    tEM = mm(pf[2 * b + pc][:, :], identity[:],
                             emsk2[b][h][:, pc * 512:pc * 512 + 512],
                             h == 0, h == 3,
                             inc=sPE if (pc == 1 and h == 3) else None)
            tEM_h[it] = tEM

        def e2_tailA(it):
            b = it % 2
            tEM = tEM_h[it]
            p.wait("vector", sPE, tEM)
            tEPW = None
            for pc in range(2):
                tEPW = p.op("vector", lambda e, pc=pc, b=b:
                            e.scalar_tensor_tensor(
                                out=epw[:, pc * 512:pc * 512 + 512],
                                in0=pf[2 * b + pc][:, :], scalar=1.0,
                                in1=swt2[b][:, pc * 512:pc * 512 + 512],
                                op0=ALU.mult, op1=ALU.mult))
            tEPW_h[it] = tEPW
            p.wait("tensor", sV, tEPW)
            if it >= 1:
                p.wait("tensor", sV, tTe_h[it - 1])
            tt = None
            for c in range(NC):
                tt = p.op("tensor", lambda e, c=c: e.transpose(
                    pb16[0][:, ts(c)],
                    epw[:, c * 128:c * 128 + 128],
                    identity[:]), inc=sPE if c == NC - 1 else None)
            p.wait("vector", sPE, tt)
            tTe = p.op("vector", lambda e: e.tensor_copy(
                out=epwT[:, :], in_=pb16[0][:, :]))
            tTe_h[it] = tTe

        def e2_tailB(it):
            gh, i = it // NT, it % NT
            gp = gh % 2
            p.wait("tensor", sV, tTe_h[it])
            if it >= 1:
                p.wait("tensor", sV, tAcc_h[it - 1])
            tOut = None
            for dh in range(2):
                for cc in range(8):
                    tOut = mm(pf[4 + dh][:, :],
                              epwT[:, cc * 128:cc * 128 + 128],
                              ueg2[gp][:, cc * 1024 + dh * 512:
                                       cc * 1024 + dh * 512 + 512],
                              cc == 0, cc == 7,
                              inc=sPE if (dh == 1 and cc == 7) else None)
            tOut_h[it] = tOut
            p.wait("vector", sPE, tOut)
            tAcc = None
            for dh in range(2):
                dst = expacc[:, i * D + dh * 512:i * D + dh * 512 + 512]
                if gh == 0:
                    tAcc = p.op("vector", lambda e, dst=dst, dh=dh:
                                e.tensor_copy(out=dst,
                                              in_=pf[4 + dh][:, :]))
                else:
                    tAcc = p.op("vector", lambda e, dst=dst, dh=dh:
                                e.tensor_tensor(out=dst,
                                                in0=pf[4 + dh][:, :],
                                                in1=dst, op=ALU.add))
            tAcc_h[it] = tAcc

        e2_load(0)
        e2_load(1)
        e2_W(0)
        e2_chain(0)
        for it in range(NIT):
            e2_EM(it)
            if it + 1 < NIT:
                e2_W(it + 1)
            e2_tailA(it)
            if it + 1 < NIT:
                e2_chain(it + 1)
            e2_tailB(it)
            if (it + 1) % NT == 0:
                ghn = (it + 1) // NT + 1
                if ghn < NG:
                    e2_load(ghn)
        p.barrier(ALLSEMS)

    if stage >= 3:
        SC = 0.125
        NEGB = -4096.0
        cv = Carver()
        v_aug = cv.c16(16 * 1056)
        kcT = cv.c16(16 * 128)
        vc_aug = cv.c16(16 * 66)
        awg_sb = cv.c16(24)
        bigexp_sb = cv.c16(16 * 128)
        cok_sb = cv.c16(NB)
        causal_sb = cv.c16(64)
        band_sb = cv.c16(256)
        wck_sb = cv.c16(16 * 64)
        wcv_sb = cv.c16(16 * 64)
        _markA = cv.o16

        stA = mkstream("stA")
        stWA = [mkstream("stWA0"), mkstream("stWA1")]
        stSpQ = [mkstream("stSpQ0"), mkstream("stSpQ1")]
        stSpK = [mkstream("stSpK0"), mkstream("stSpK1")]
        stSpV = [mkstream("stSpV0"), mkstream("stSpV1")]
        stXN3 = mkstream("stXN3")
        stQH = mkstream("stQH")
        stKH = mkstream("stKH")
        stQG = mkstream("stQG")
        stBG = mkstream("stBG")
        stDBG = mkstream("stDBG")
        stDBG2 = mkstream("stDBG2")
        stCok = mkstream("stCok")

        xn_at = cv.c16(NC * T)
        wq_at = [cv.c16(8 * 128) for _ in range(2)]
        pstage = [cv.c16(2048) for _ in range(2)]
        vfm = cv.c16(2048)
        wg3_sb = cv.c16(4)
        ameanT_sb = cv.c16(D)

        dma("gpsimd", stA, bigexp_sb.rearrange("p (a b) -> p a b", a=16),
            AP(bigexp.tensor, 0, [[128, 128], [128 * 128, 16], [1, 128]]))
        dma("gpsimd", stA, causal_sb[0:64, :], causal64[:, :])
        dma("gpsimd", stA, band_sb[:, :], bandmask[1, :, :])
        dma("gpsimd", stA, wg3_sb[0:64, 0:3], Wg_att[:, :])
        dma("gpsimd", stA, ameanT_sb[0:64, :], ameanT[:, :])
        wck_src = AP(Wck_d.tensor, 0, [[HD, 64], [64 * HD, 16], [1, HD]])
        dma("gpsimd", stA, wck_sb[0:64, :].rearrange("p (j c) -> p j c", j=16),
            wck_src)
        wcv_src = AP(Wcv_d.tensor, 0, [[HD, 64], [64 * HD, 16], [1, HD]])
        tA0 = dma("gpsimd", stA,
                  wcv_sb[0:64, :].rearrange("p (j c) -> p j c", j=16), wcv_src)
        t_xn3 = dma("sync", stXN3, xn_at.rearrange("p (c t) -> p c t", c=NC),
                    AP(xn_dram.tensor, 0, [[NC * T, 128], [T, NC], [1, T]]))
        p.wait("tensor", stXN3, t_xn3)

        def proj(wdram, spill_dram, stream_sl, kind):
            for c in range(NC):
                wb = wq_at[c % 2]
                if c >= 2:
                    p.wait("gpsimd", sPE, proj_pe.get(c - 2, 0))
                else:
                    p.wait("gpsimd", sPE, p.cur(sPE))
                tW = dma("gpsimd", stWA[c % 2],
                         wb.rearrange("p (k f) -> p k f", k=NC),
                         AP(wdram.tensor, c * 128,
                            [[D, 128], [128 * D, NC], [1, 128]]))
                p.wait("tensor", stWA[c % 2], tW)
                tMM = None
                for pc in range(4):
                    tMM = mm_chain(
                        pf[pc][:, :],
                        [(wb[:, ts(k)],
                          xn_at[:, k * T + pc * 512:k * T + pc * 512 + 512])
                         for k in range(NC)],
                        inc=sPE if pc == 3 else None)
                proj_pe[c] = tMM
                p.wait("scalar", sPE, tMM)
                if kind == "spill":
                    dst_sb = pstage[c % 2]
                    if (c % 2) in spill_guard:
                        gs, gv = spill_guard[c % 2]
                        p.wait("scalar", gs, gv)
                    tEv = None
                    for pc in range(4):
                        tEv = p.op("scalar", lambda e, pc=pc, d=dst_sb:
                                   e.activation(
                                       out=d[:, pc * 512:pc * 512 + 512],
                                       in_=pf[pc][:, :], func=AF.Copy,
                                       scale=1.0))
                    p.wait("sync", sA, tEv)
                    proj_sp[c] = dma("sync", stream_sl[c % 2],
                                     spill_dram[:, c * T:(c + 1) * T], dst_sb)
                    spill_guard[c % 2] = (stream_sl[c % 2], proj_sp[c])
                    p.wait("tensor", sA, tEv)
                else:
                    tEv = None
                    for pc in range(4):
                        tEv = p.op("scalar", lambda e, pc=pc: e.activation(
                            out=vfm[:, pc * 512:pc * 512 + 512],
                            in_=pf[pc][:, :], func=AF.Copy, scale=1.0))
                    p.wait("tensor", sA, tEv)
                    p.wait("sync", sA, tEv)
                    if c >= 2:
                        p.wait("sync", sV, vau_hist.get(c - 2, 0))
                    proj_sp[c] = dma("sync", stream_sl[c % 2],
                                     spill_dram[:, c * T:(c + 1) * T], vfm)
                    for half8 in range(2):
                        tt = None
                        for kc8 in range(8):
                            kc = half8 * 8 + kc8
                            tt = p.op("tensor", lambda e, kc=kc, kc8=kc8,
                                      half8=half8: e.transpose(
                                          pb16[half8][:, ts(kc8)],
                                          vfm[:, kc * 128:kc * 128 + 128],
                                          identity[:]),
                                      inc=sPE if kc8 == 7 else None)
                        p.wait("vector", sPE, tt)
                        dstv = AP(v_aug.tensor,
                                  v_aug.offset + half8 * 8 * 1056 + c * 2 * 66,
                                  [[AR16, 128], [1056, 8], [66, 2], [1, 64]])
                        tVe = p.op("vector", lambda e, dstv=dstv, half8=half8:
                                   e.tensor_copy(
                                       out=dstv,
                                       in_=pb16[half8].rearrange(
                                           "p (a h f) -> p a h f", a=8, h=2)))
                        p.wait("tensor", sV, tVe)
                    vau_hist[c] = tVe
                    p.wait("scalar", stream_sl[c % 2], proj_sp[c])
                    p.wait("scalar", sPE, proj_pe[c])

        proj_pe = {}
        proj_sp = {}
        spill_guard = {}
        proj(Wq_d, q_dram, stSpQ, "spill")
        proj_pe = {}
        proj_sp = {}
        proj(Wk_d, k_dram, stSpK, "spill")
        proj_pe = {}
        proj_sp = {}
        vau_hist = {}
        proj(Wv_d, v_dram, stSpV, "v")
        tOnes = p.op("vector", lambda e: e.memset(
            AP(v_aug.tensor, v_aug.offset + 64,
               [[AR16, 128], [1056, 16], [66, 16]]), 1.0))

        p.barrier(ALLSEMS)
        bgatt_sb = cv.c32(4)
        glog = cv.c32(4)
        gsc = cv.c32(8)
        qg_buf = cv.c16(8 * 128)
        tBg = dma("sync", stBG, bgatt_sb[:, 0:3], bg_att_rep[:, :])
        tAW = None
        for c in range(NC):
            tAW = mm(pf[0][:, c * 3:c * 3 + 3],
                     ameanT_sb[0:64, ts(c)], wg3_sb[0:64, 0:3],
                     True, True, inc=sPE if c == NC - 1 else None)
        p.wait("scalar", sPE, tAW)
        tAWe = p.op("scalar", lambda e: e.activation(
            out=awg_sb[:, 0:24], in_=pf[0][:, 0:24], func=AF.Copy, scale=1.0))
        p.wait("tensor", sA, tAWe)
        if debug >= 2:
            p.wait("sync", sA, tAWe)
            dma("sync", stDBG, awg_dram[:, 0:24], awg_sb[:, 0:24])
            p.wait("sync", stBG, tBg)
            dma("sync", stDBG, bg_dram[:, 0:3], bgatt_sb[:, 0:3])
        gm_hist = 0
        for i in range(NT):
            p.wait("sync", sPE, gm_hist)
            tQG = dma("sync", stQG, qg_buf.rearrange("p (c f) -> p c f", c=NC),
                      AP(q_dram.tensor, i * 128,
                         [[NC * T, 128], [T, NC], [1, 128]]))
            p.wait("tensor", stQG, tQG)
            tGm = None
            for c in range(NC):
                tGm = mm(pf[1][:, 0:3], qg_buf[:, ts(c)],
                         awg_sb[:, c * 3:c * 3 + 3], c == 0, c == NC - 1,
                         inc=sPE if c == NC - 1 else None)
            gm_hist = tGm
            p.wait("vector", sPE, tGm)
            p.wait("vector", stBG, tBg)
            p.op("vector", lambda e: e.tensor_tensor(
                out=glog[:, 0:3], in0=pf[1][:, 0:3], in1=bgatt_sb[:, 0:3],
                op=ALU.add))
            p.op("vector", lambda e: e.reduce_max(
                out=gsc[:, 0:1], in_=glog[:, 0:3], axis=AX.X))
            tGn = p.op("vector", lambda e: e.tensor_scalar(
                out=gsc[:, 1:2], in0=gsc[:, 0:1], scalar1=-1.0, scalar2=None,
                op0=ALU.mult))
            p.wait("scalar", sV, tGn)
            tGe = p.op("scalar", lambda e: e.activation(
                out=gsc[:, 2:5], in_=glog[:, 0:3], func=AF.Exp,
                bias=gsc[:, 1:2], accum_out=gsc[:, 5:6]))
            p.wait("vector", sA, tGe)
            p.op("vector", lambda e: e.reciprocal(
                out=gsc[:, 6:7], in_=gsc[:, 5:6]))
            tGf = p.op("vector", lambda e, i=i: e.tensor_scalar(
                out=gates_tm[:, i * 4:i * 4 + 3], in0=gsc[:, 2:5],
                scalar1=gsc[:, 6:7], scalar2=None, op0=ALU.mult))
            if debug >= 2:
                if i % 2 == 0:
                    tGf = p.op("vector", lambda e, i=i: e.tensor_copy(
                        out=gates_tm[:, i * 4 + 3:i * 4 + 4],
                        in_=pf[1][:, 0:1]))
                else:
                    tGf = p.op("vector", lambda e, i=i: e.tensor_copy(
                        out=gates_tm[:, i * 4 + 3:i * 4 + 4],
                        in_=bgatt_sb[:, 0:1]))
            p.wait("tensor", sV, tGf)
            if debug >= 2 and i == 0:
                p.wait("sync", sV, p.cur(sV))
                p.wait("sync", sA, p.cur(sA))
                dma("sync", stDBG, bg_dram[:, 4:8], glog[:, 0:4],
                    allow_slow_non_contiguous=True)
                dma("sync", stDBG, bg_dram[:, 8:16], gsc[:, 0:8],
                    allow_slow_non_contiguous=True)
                p.wait("vector", stDBG, p.cur(stDBG))
        if debug >= 2:
            p.wait("sync", sV, p.cur(sV))
            dma("sync", stDBG, bg_dram[:, 3:4], bgatt_sb[:, 0:1],
                allow_slow_non_contiguous=True)
            dma("sync", stDBG, gprobe[:, 0:4], gates_tm[:, 0:4])

        p.barrier(ALLSEMS)
        cv.o16 = _markA
        cv.o32 = 0
        qh = cv.c16(T)
        kh = cv.c16(T)
        khblk = cv.c16(T)
        vct = cv.c16(128)
        kc_hist = 0
        vc_hist = 0
        for h in range(H):
            p.wait("sync", sPE, kc_hist)
            tKH = dma("sync", stKH, kh[0:64, :],
                      k_dram[(h % 2) * 64:(h % 2) * 64 + 64,
                             (h // 2) * T:(h // 2) * T + T])
            p.wait("vector", stKH, tKH)
            p.wait("vector", sPE, kc_hist)
            tGk = p.op("vector", lambda e: e.tensor_copy(
                out=khblk[0:64, :].rearrange("p (j n) -> p j n", j=16),
                in_=AP(kh.tensor, kh.offset, [[AR16, 64], [1, 16], [16, 128]])))
            p.wait("tensor", sV, tGk)
            p.wait("tensor", stA, tA0)
            tKc = None
            for j in range(16):
                tKc = mm(pf[0][0:64, 0:128],
                         wck_sb[0:64, j * 64:j * 64 + 64],
                         khblk[0:64, j * 128:j * 128 + 128],
                         j == 0, j == 15, inc=sPE if j == 15 else None)
            p.wait("scalar", sPE, tKc)
            tKce = p.op("scalar", lambda e, h=h: e.activation(
                out=kcT[0:64, h * 128:h * 128 + 128], in_=pf[0][0:64, 0:128],
                func=AF.Copy, scale=1.0))
            p.wait("tensor", sA, tKce)
            kc_hist = tKc
            p.wait("sync", sPE, vc_hist)
            tVH = dma("sync", stQH, qh[0:64, :],
                      v_dram[(h % 2) * 64:(h % 2) * 64 + 64,
                             (h // 2) * T:(h // 2) * T + T])
            p.wait("tensor", stQH, tVH)
            tVc = None
            for j in range(16):
                tVc = mm(pf[1][0:64, 0:128],
                         wcv_sb[0:64, j * 64:j * 64 + 64],
                         AP(qh.tensor, qh.offset + j, [[AR16, 64], [16, 128]]),
                         j == 0, j == 15, inc=sPE if j == 15 else None)
            vc_hist = tVc
            p.wait("scalar", sPE, tVc)
            tVe = p.op("scalar", lambda e: e.activation(
                out=vct[0:64, :], in_=pf[1][0:64, 0:128], func=AF.Copy,
                scale=1.0))
            p.wait("tensor", sA, tVe)
            tVt = p.op("tensor", lambda e: e.transpose(
                pb16[0][:, 0:64], vct[0:64, :], identity[0:64, 0:64]),
                inc=sPE)
            p.wait("vector", sPE, tVt)
            tVce = p.op("vector", lambda e, h=h: e.tensor_copy(
                out=vc_aug[:, h * 66:h * 66 + 64], in_=pb16[0][:, 0:64]))
            p.wait("tensor", sV, tVce)
        tOne2 = p.op("vector", lambda e: e.memset(
            AP(vc_aug.tensor, vc_aug.offset + 64,
               [[AR16, 128], [66, 16]]), 1.0))
        p.barrier(ALLSEMS)
        if debug >= 2:
            dma("sync", stDBG, gprobe[:, 4:8], gates_tm[:, 0:4])
            p.wait("sync", stDBG, p.cur(stDBG))

        qe2 = [cv.c16(768), cv.c16(768)]
        kh2 = [kh, cv.c16(T)]
        cok_sb = cv.c16(NT * NB)
        cokT_sb = cv.c16(T)
        mtile2 = [cv.c16(128), cv.c16(128)]
        mT_sl = cv.c16(512)
        ecT_sl = cv.c16(512)
        elb = cv.c16(5 * 256)
        euT = cv.c16(16 * 512)
        mixq = cv.c16(4 * 1024)
        a_th4 = cv.c32(1)
        a_v8 = cv.c32(8)
        outc = cv.c32(200)
        z3s = cv.c32(8)
        tmp64 = cv.c32(64)
        if debug >= 2:
            dma("sync", stDBG, gprobe3[:, 30:32], bigexp_sb[:, 0:2])
        tCk = dma("gpsimd", stCok, cok_sb.rearrange("p (i n) -> p i n", i=NT),
                  AP(cok_tm.tensor, 0, [[NB, 128], [128 * NB, NT], [1, NB]]))
        tCkT = dma("gpsimd", stCok, cokT_sb[:, :], cokT_in[:, :])
        p.wait("tensor", stCok, tCkT)
        stKA = [mkstream("stKA0"), mkstream("stKA1")]
        stQA = [mkstream("stQA0"), mkstream("stQA1")]
        NGH = 4 * H
        tKH_h = {}; tQH_h = {}; tSlab_h = {}; tOc_h = {}
        tEc_h = {}; tMe_h = {}; tMix_h = {}
        tMTe_hist = []
        slab_ctr = [0]
        slab_ev = {}
        slab_mm = {}

        def a2_load(g):
            if g >= NGH:
                return
            tq, h = g // H, g % H
            gp = g % 2
            t0 = tq * 512
            if g >= 2:
                p.wait("sync", sPE, tSlab_h[g - 2])
            tKH_h[g] = dma("sync", stKA[gp], kh2[gp][0:64, :],
                           k_dram[(h % 2) * 64:(h % 2) * 64 + 64,
                                  (h // 2) * T:(h // 2) * T + T])
            qlo = max(t0 - 128, 0)
            qhi = min(t0 + 640, T)
            tQH_h[g] = dma("sync", stQA[gp],
                           qe2[gp][0:64, 128 - (t0 - qlo):
                                   128 - (t0 - qlo) + (qhi - qlo)],
                           q_dram[(h % 2) * 64:(h % 2) * 64 + 64,
                                  (h // 2) * T + qlo:(h // 2) * T + qhi])

        def a2_head(g):
            tq, h = g // H, g % H
            hp = g % 2
            t0 = tq * 512
            qe = qe2[hp]
            kh_ = kh2[hp]
            p.wait("tensor", stKA[hp], tKH_h[g])
            p.wait("tensor", stQA[hp], tQH_h[g])
            if g >= 1:
                p.wait("tensor", sV, tOc_h[g - 1][3])
            tSb_h = {}
            tMb = {}
            tMt = {}

            def emit_pair(ii):
                i = tq * 4 + ii
                bank = pf[2 + ii % 2]
                if ii >= 2:
                    p.wait("tensor", sV, tMb[ii - 2])
                mm(bank[:, 0:128],
                   qe[0:64, 128 + ii * 128:128 + ii * 128 + 128],
                   kcT[0:64, h * 128:h * 128 + 128], True, False)
                tSb_h[ii] = mm(bank[:, 0:128],
                               identity[:], cok_sb[:, i * NB:i * NB + NB],
                               False, True, inc=sPE)

            def emit_chain(ii):
                i = tq * 4 + ii
                bank = pf[2 + ii % 2]
                p.wait("vector", sPE, tSb_h[ii])
                p.op("vector", lambda e, bank=bank: e.max(
                    out=a_v8, in_=bank[:, 0:128]))
                p.op("vector", lambda e: e.tensor_scalar(
                    out=a_th4, in0=a_v8[:, 3:4], scalar1=1e-5,
                    scalar2=None, op0=ALU.subtract))
                if ii >= 2:
                    p.wait("vector", sPE, tMt[ii - 2])
                tMb[ii] = p.op("vector", lambda e, ii=ii, bank=bank:
                               e.tensor_scalar(
                                   out=mtile2[ii % 2], in0=bank[:, 0:128],
                                   scalar1=a_th4, scalar2=None,
                                   op0=ALU.is_ge))
                if i == 0:
                    p.op("vector", lambda e: e.memset(
                        mtile2[0][0:64, 0:4], 1.0))
                    tMb[ii] = p.op("vector", lambda e: e.memset(
                        mtile2[0][0:64, 4:128], 0.0))

            def emit_tr(j):
                p.wait("tensor", sV, tMb[j])
                if j == 0 and g >= 1:
                    p.wait("tensor", sV, tMe_h[g - 1])
                if j == 0 and h == 0 and tq > 0:
                    for tk in tMTe_hist[-2:]:
                        p.wait("tensor", sV, tk)
                tMt[j] = p.op("tensor", lambda e, j=j: e.transpose(
                    pb16[0][:, j * 128:j * 128 + 128], mtile2[j % 2],
                    identity[:]), inc=sPE)

            emit_pair(0)
            emit_pair(1)
            emit_chain(0)
            emit_chain(1)
            emit_tr(0)
            emit_pair(2)
            emit_chain(2)
            emit_tr(1)
            emit_pair(3)
            emit_chain(3)
            emit_tr(2)
            emit_tr(3)
            p.wait("vector", sPE, tMt[3])
            tMe = p.op("vector", lambda e: e.tensor_copy(
                out=mT_sl, in_=pb16[0][:, 0:512]))
            tMe_h[g] = tMe
            bB = pf[4 + g % 2]
            if g >= 2:
                p.wait("tensor", sA, tEc_h[g - 2])
            tSbT = mm(bB[:, :], kcT[0:64, h * 128:h * 128 + 128],
                      qe[0:64, 128:640], True, False)
            tSbT = mm(bB[:, :], identity[:],
                      cokT_sb[:, t0:t0 + 512], False, True, inc=sPE)
            p.wait("scalar", sPE, tSbT)
            tEc = p.op("scalar", lambda e, bB=bB: e.activation(
                out=ecT_sl, in_=bB[:, :], func=AF.Exp, scale=SC))
            tEc_h[g] = tEc
            p.wait("tensor", sV, tMe)
            slabs = [("eu", kc) for kc in range(4 * tq + 4)]
            slabs += [("lb", kc) for kc in range(max(0, 4 * tq - 1),
                                                 4 * tq + 4)]
            tCorner = None
            tEvLast = None
            tSlab = None
            for kind, kc in slabs:
                sc_i = slab_ctr[0]
                slab_ctr[0] += 1
                bank = pf[sc_i % 2]
                if sc_i >= 2:
                    p.wait("tensor", sA, slab_ev[sc_i - 2])
                if kind == "eu":
                    mm(bank[:, :], kh_[0:64, kc * 128:kc * 128 + 128],
                       qe[0:64, 128:640], True, False)
                    tSlab = mm(bank[:, :],
                               bigexp_sb[:, kc * 128:kc * 128 + 128],
                               mT_sl, False, True, inc=sPE)
                    p.wait("scalar", sPE, tSlab)
                    tEv2 = p.op("scalar", lambda e, kc=kc, bank=bank:
                                e.activation(
                                    out=euT[:, kc * 512:kc * 512 + 512],
                                    in_=bank[:, :], func=AF.Exp, scale=SC,
                                    bias=constcol[:, 0:1]))
                    if kc == 0 and tq == 0:
                        p.wait("vector", sA, tEv2)
                        tCorner = p.op("vector", lambda e: e.tensor_tensor(
                            out=euT[0:64, 0:64], in0=euT[0:64, 0:64],
                            in1=causal_sb[0:64, 0:64], op=ALU.mult))
                        p.wait("scalar", sV, tCorner)
                else:
                    wlo = kc * 128
                    whi = min(kc * 128 + 256, T)
                    wid = whi - wlo
                    qcol = 128 + (wlo - t0)
                    slot = kc % 5
                    mm(bank[:, 0:wid], kh_[0:64, kc * 128:kc * 128 + 128],
                       qe[0:64, qcol:qcol + wid], True, False)
                    tSlab = mm(bank[:, 0:wid], identity[:],
                               band_sb[:, 0:wid], False, True, inc=sPE)
                    p.wait("scalar", sPE, tSlab)
                    tEv2 = p.op("scalar", lambda e, slot=slot, wid=wid,
                                bank=bank: e.activation(
                                    out=elb[:, slot * 256:slot * 256 + wid],
                                    in_=bank[:, 0:wid], func=AF.Exp,
                                    scale=SC))
                slab_ev[sc_i] = tEv2
                tEvLast = tEv2
            tSlab_h[g] = tSlab
            a2_load(g + 2)
            p.wait("tensor", sA, tEvLast)
            if tCorner is not None:
                p.wait("tensor", sV, tCorner)
            tOc_h[g] = {}
            for ii in range(4):
                i = tq * 4 + ii
                bS = pf[2 + ii % 2]
                bC = pf[4 + ii % 2]
                if ii >= 2:
                    p.wait("tensor", sV, tOc_h[g][ii - 2])
                tSel = None
                for kc in range(i + 1):
                    tSel = mm(bS[:, 0:65],
                              euT[:, kc * 512 + ii * 128:
                                  kc * 512 + ii * 128 + 128],
                              v_aug[:, kc * 1056 + h * 66:
                                    kc * 1056 + h * 66 + 65],
                              kc == 0, kc == i, inc=sPE if kc == i else None)
                tCmp = mm(bC[:, 0:65],
                          ecT_sl[:, ii * 128:ii * 128 + 128],
                          vc_aug[:, h * 66:h * 66 + 65],
                          True, True, inc=sPE)
                tLoc = mm(bC[:, 128:193],
                          elb[:, (i % 5) * 256:(i % 5) * 256 + 128],
                          v_aug[:, i * 1056 + h * 66:
                                i * 1056 + h * 66 + 65],
                          True, i == 0, inc=sPE if i == 0 else None)
                if i > 0:
                    tLoc = mm(bC[:, 128:193],
                              elb[:, ((i - 1) % 5) * 256 + 128:
                                  ((i - 1) % 5) * 256 + 256],
                              v_aug[:, (i - 1) * 1056 + h * 66:
                                    (i - 1) * 1056 + h * 66 + 65],
                              False, True, inc=sPE)
                p.wait("vector", sPE, tLoc)
                p.op("vector", lambda e, bC=bC: e.tensor_copy(
                    out=outc[:, 0:65], in_=bC[:, 0:65]))
                p.op("vector", lambda e, bS=bS: e.tensor_copy(
                    out=outc[:, 65:130], in_=bS[:, 0:65]))
                tOc = p.op("vector", lambda e, bC=bC: e.tensor_copy(
                    out=outc[:, 130:195], in_=bC[:, 128:193]))
                zview = AP(outc.tensor, outc.offset + 64,
                           [[AR32, 128], [65, 3]])
                p.op("vector", lambda e, z=zview: e.tensor_scalar(
                    out=z3s[:, 0:3], in0=z, scalar1=1e-30, scalar2=None,
                    op0=ALU.add))
                p.op("vector", lambda e: e.reciprocal(
                    out=z3s[:, 3:6], in_=z3s[:, 0:3]))
                p.op("vector", lambda e, i=i: e.tensor_tensor(
                    out=z3s[:, 0:3], in0=z3s[:, 3:6],
                    in1=gates_tm[:, i * 4:i * 4 + 3], op=ALU.mult))
                p.op("vector", lambda e: e.tensor_scalar(
                    out=tmp64, in0=outc[:, 0:64], scalar1=z3s[:, 0:1],
                    scalar2=None, op0=ALU.mult))
                p.op("vector", lambda e: e.scalar_tensor_tensor(
                    out=tmp64, in0=outc[:, 65:129], scalar=z3s[:, 1:2],
                    in1=tmp64, op0=ALU.mult, op1=ALU.add))
                tMix = p.op("vector", lambda e, ii=ii, h=h:
                            e.scalar_tensor_tensor(
                                out=mixq[:, ii * 1024 + h * 64:
                                         ii * 1024 + h * 64 + 64],
                                in0=outc[:, 130:194], scalar=z3s[:, 2:3],
                                in1=tmp64, op0=ALU.mult, op1=ALU.add))
                tOc_h[g][ii] = tOc
                tMix_h[g] = tMix

        a2_load(0)
        a2_load(1)
        for g in range(NGH):
            a2_head(g)
            if g % H == H - 1:
                tq = g // H
                p.wait("tensor", sV, tMix_h[g])
                tMTe_q = {}
                for ii in range(4):
                    i = tq * 4 + ii
                    if ii >= 2:
                        p.wait("tensor", sV, tMTe_q[ii - 2])
                    tTr = None
                    for c in range(NC):
                        tTr = p.op("tensor", lambda e, ii=ii, c=c:
                                   e.transpose(
                                       pb16[ii % 2][:, ts(c)],
                                       mixq[:, ii * 1024 + c * 128:
                                            ii * 1024 + c * 128 + 128],
                                       identity[:]),
                                   inc=sPE if c == NC - 1 else None)
                    p.wait("vector", sPE, tTr)
                    dstm = AP(mixT[:].tensor, i * 128,
                              [[NC * T, 128], [T, NC], [1, 128]])
                    tMTe = p.op("vector", lambda e, dstm=dstm, ii=ii:
                                e.tensor_copy(
                                    out=dstm, in_=pb16[ii % 2].rearrange(
                                        "p (c f) -> p c f", c=NC)))
                    tMTe_q[ii] = tMTe
                    tMTe_hist.append(tMTe)
        p.barrier(ALLSEMS)

    cv = Carver()
    hT = cv.c16(NFC * 1024)
    _markM = cv.o16
    xn_half = cv.c16(NC * 1024)
    wbuf = [cv.c16(8 * 128) for _ in range(4)]
    gsil = cv.c16(1024)
    cv.o16 = _markM
    wdres2 = [cv.c16(16 * 512), cv.c16(16 * 512)]
    wores = cv.c16(NC * 512)
    xq_f32 = [cv.c32(512) for _ in range(2)]
    out_sb = [cv.c32(512) for _ in range(4)]

    stXnH = mkstream("stXnH")
    stWo2 = mkstream("stWo2")
    stWg = [mkstream("stWg0"), mkstream("stWg1")]
    stWu = [mkstream("stWu0"), mkstream("stWu1")]
    stWd = mkstream("stWd")
    stXQ = [mkstream("stXQ0"), mkstream("stXQ1")]
    stOut = [mkstream(f"stOut{i}") for i in range(4)]
    store_hist = {}
    add_hist = {}
    for half in range(2):
        p.barrier(ALLSEMS)
        xn_src = AP(xn_dram.tensor, half * 1024,
                    [[NC * T, 128], [T, NC], [1, 1024]])
        t_xnh = dma("sync", stXnH, xn_half.rearrange("p (c t) -> p c t", c=NC),
                    xn_src)
        pe_hist = {}
        ht_hist = {}
        for fc in range(NFC):
            bgw = wbuf[2 * (fc % 2)]
            buw = wbuf[2 * (fc % 2) + 1]
            if fc >= 2:
                p.wait("gpsimd", sPE, pe_hist[fc - 2])
            wsrc_g = AP(Wfg.tensor, fc * 128,
                        [[DCD, 128], [128 * DCD, NC], [1, 128]])
            wsrc_u = AP(Wfu.tensor, fc * 128,
                        [[DCD, 128], [128 * DCD, NC], [1, 128]])
            tWg = dma("gpsimd", stWg[fc % 2],
                      bgw.rearrange("p (k f) -> p k f", k=NC), wsrc_g)
            tWu = dma("gpsimd", stWu[fc % 2],
                      buw.rearrange("p (k f) -> p k f", k=NC), wsrc_u)
            p.wait("tensor", stWg[fc % 2], tWg)
            p.wait("tensor", stWu[fc % 2], tWu)
            if fc == 0:
                p.wait("tensor", stXnH, t_xnh)
                p.wait("tensor", stC, t_const)
            for ph in range(2):
                mm_chain(pf[0 + ph][:, :],
                         [(bgw[:, ts(k)],
                           xn_half[:, k * 1024 + ph * 512:
                                   k * 1024 + ph * 512 + 512])
                          for k in range(NC)])
            tU = None
            for ph in range(2):
                tU = mm_chain(pf[2 + ph][:, :],
                              [(buw[:, ts(k)],
                                xn_half[:, k * 1024 + ph * 512:
                                        k * 1024 + ph * 512 + 512])
                               for k in range(NC)],
                              inc=sPE if ph == 1 else None)
            pe_hist[fc] = tU
            p.wait("scalar", sPE, tU)
            if fc >= 1:
                p.wait("scalar", sV, ht_hist[fc - 1])
            tSg = None
            for ph in range(2):
                tSg = p.op("scalar", lambda e, ph=ph, fc=fc: e.activation(
                    out=gsil[:, ph * 512:ph * 512 + 512], in_=pf[ph][:, :],
                    func=AF.Silu, bias=biasg_sb[:, fc:fc + 1]),
                    inc=sA if ph == 1 else None)
            p.wait("vector", sA, tSg)
            tHT = None
            for ph in range(2):
                tHT = p.op("vector", lambda e, ph=ph, fc=fc:
                           e.scalar_tensor_tensor(
                               out=hT[:, fc * 1024 + ph * 512:
                                      fc * 1024 + ph * 512 + 512],
                               in0=pf[2 + ph][:, :],
                               scalar=biasu_sb[:, fc:fc + 1],
                               in1=gsil[:, ph * 512:ph * 512 + 512],
                               op0=ALU.add, op1=ALU.mult),
                           inc=sV if ph == 1 else None)
            ht_hist[fc] = tHT
            p.wait("tensor", sA, tSg)
            p.wait("tensor", sV, tHT)

        if half == 0:
            t_wdc = {}
            wdA_last = {}
            wdB_last = {}
            stWdc = [mkstream("stWdc0"), mkstream("stWdc1")]

        def load_wd(gdh):
            dh_ = gdh % 2
            p.wait("gpsimd", sV, ht_hist[NFC - 1])
            for c in range(2):
                if gdh >= 1:
                    tok = wdA_last[gdh - 1] if c == 0 else wdB_last[gdh - 1]
                    p.wait("gpsimd", sPE, tok)
                wd_src = AP(Wfd.tensor, c * 16 * 128 * D + dh_ * 512,
                            [[D, 128], [128 * D, 16], [1, 512]])
                t_wdc[(gdh, c)] = dma("gpsimd", stWdc[c],
                                      wdres2[c].rearrange(
                                          "p (k f) -> p k f", k=16), wd_src)

        for dh in range(2):
            gdh = half * 2 + dh
            dcols = slice(dh * 512, dh * 512 + 512)
            if dh == 0:
                load_wd(gdh)
            p.wait("tensor", stWdc[0], t_wdc[(gdh, 0)])
            if stage >= 3:
                wo_src = AP(Wo_d.tensor, dh * 512,
                            [[D, 128], [128 * D, NC], [1, 512]])
                if gdh >= 1:
                    p.wait("gpsimd", sPE, wdB_last[gdh - 1])
                t_wo = dma("gpsimd", stWo2,
                           wores.rearrange("p (k f) -> p k f", k=NC), wo_src)
            for tile in range(8):
                gt = half * 8 + tile
                pq = pf[4 + tile % 2]
                tA = None
                for fc in range(16):
                    tA = mm(pq[:, 0:512],
                            hT[:, fc * 1024 + tile * 128:
                               fc * 1024 + tile * 128 + 128],
                            wdres2[0][:, fc * 512:fc * 512 + 512],
                            fc == 0, False,
                            inc=sPE if fc == 15 else None)
                if tile == 7:
                    wdA_last[gdh] = tA
                if tile == 0:
                    p.wait("tensor", stWdc[1], t_wdc[(gdh, 1)])
                    if stage >= 3:
                        p.wait("tensor", stWo2, t_wo)
                tMM = None
                for fc in range(16, NFC):
                    last = (stage < 3 and fc == NFC - 1)
                    tMM = mm(pq[:, 0:512],
                             hT[:, fc * 1024 + tile * 128:
                                fc * 1024 + tile * 128 + 128],
                             wdres2[1][:, (fc - 16) * 512:
                                       (fc - 16) * 512 + 512],
                             False, last, inc=sPE if last else None)
                if stage >= 3:
                    for c in range(NC):
                        tMM = mm(pq[:, 0:512],
                                 mixT[:, c * T + gt * 128:
                                      c * T + gt * 128 + 128],
                                 wores[:, c * 512:c * 512 + 512],
                                 False, c == NC - 1,
                                 inc=sPE if c == NC - 1 else None)
                if tile == 7:
                    wdB_last[gdh] = tMM
                    if dh == 0:
                        load_wd(gdh + 1)
                xb = xq_f32[tile % 2]
                p.wait("sync", sV, add_hist.get(tile % 2, 0))
                tX = dma("sync", stXQ[tile % 2], xb, x_in[ts(gt), dcols])
                obi = tile % 2 + 2 * (dh % 2)
                ob = out_sb[obi]
                p.wait("vector", sPE, tMM)
                p.wait("vector", stXQ[tile % 2], tX)
                p.wait("vector", stOut[obi], store_hist.get(obi, 0))
                tAdd = p.op("vector", lambda e, pq=pq, xb=xb, ob=ob:
                            e.tensor_tensor(out=ob, in0=pq[:, 0:512], in1=xb,
                                            op=ALU.add), inc=sV)
                add_hist[tile % 2] = tAdd
                if stage >= 2:
                    p.op("vector", lambda e, ob=ob, gt=gt, dh=dh:
                         e.tensor_tensor(
                             out=ob, in0=ob,
                             in1=expacc[:, gt * D + dh * 512:
                                        gt * D + dh * 512 + 512],
                             op=ALU.add))
                bdap = bdrep_sb[:, dcols]
                tO = p.op("vector", lambda e, ob=ob, bdap=bdap:
                          e.tensor_tensor(out=ob, in0=ob, in1=bdap,
                                          op=ALU.add), inc=sV)
                p.wait("tensor", sV, tO)
                p.wait("sync", sV, tO)
                store_hist[obi] = dma("sync", stOut[obi],
                                      out_t[ts(gt), dcols], ob)

    for st in (stOut + [stSp[0], stSp[1]]):
        p.wait("sync", st, p.cur(st))
    p.emit()
    return nc



def _shard_inputs(inputs, core, stage=STAGE):
    d = {}
    d["x"] = np.ascontiguousarray(np.asarray(inputs["L0_attn_x"])[core])
    d["Wfg"] = np.asarray(inputs["L0_ffn_Wg"])
    d["Wfu"] = np.asarray(inputs["L0_ffn_Wu"])
    d["bfg_c"] = np.ascontiguousarray(
        np.asarray(inputs["L0_ffn_bg"]).reshape(NFC, 128).T)
    d["bfu_c"] = np.ascontiguousarray(
        np.asarray(inputs["L0_ffn_bu"]).reshape(NFC, 128).T)
    d["Wfd"] = np.asarray(inputs["L0_ffn_Wd"])
    d["bd_rep"] = np.ascontiguousarray(
        np.broadcast_to(np.asarray(inputs["L0_ffn_bd"])[None, :], (128, D)))
    d["ident_bf"] = np.eye(128, dtype=ml_dtypes.bfloat16)
    if stage >= 2:
        d["Wfq"] = np.asarray(inputs["L0_ffn_Wq"])
        d["keys"] = np.ascontiguousarray(
            np.asarray(inputs["L0_ffn_keys"]).reshape(FH * NK * 2, NK))
        d["down_embed"] = np.asarray(inputs["L0_ffn_down_embed"])
        d["up_embed"] = np.asarray(inputs["L0_ffn_up_embed"])
    if stage >= 3:
        d["Wq"] = np.asarray(inputs["L0_attn_Wq"])
        d["Wk"] = np.asarray(inputs["L0_attn_Wk"])
        d["Wv"] = np.asarray(inputs["L0_attn_Wv"])
        d["Wo"] = np.asarray(inputs["L0_attn_Wo"])
        d["Wck"] = np.asarray(inputs["L0_attn_Wck"])
        d["Wcv"] = np.asarray(inputs["L0_attn_Wcv"])
        d["Wg_att"] = np.asarray(inputs["L0_attn_Wg"])
        d["bg_att_rep"] = np.ascontiguousarray(np.broadcast_to(
            np.asarray(inputs["L0_attn_bg"])[None, :], (128, 3)))
        pos = np.arange(T)
        blk = np.arange(NB)
        cok = np.where(pos[:, None] >= (blk[None, :] + 1) * G, 0.0, -BIGM)
        d["cok_tm"] = cok.astype(ml_dtypes.bfloat16)
        koff = np.arange(128)
        toff = np.arange(256)
        dtk = toff[None, :] - koff[:, None]
        bm = np.where((dtk >= 0) & (dtk <= 32), 0.0, -BIGM).astype(np.float32)
        bmv = np.stack([bm, bm], axis=0)
        d["bandmask"] = bmv.astype(ml_dtypes.bfloat16)
        d["causal64"] = np.triu(np.ones((64, 64), np.float32)).astype(
            ml_dtypes.bfloat16)
        be = np.zeros((16, 128, 128), np.float32)
        for c in range(16):
            for kk in range(128):
                be[c, 8 * c + kk // 16, kk] = BIGM
        d["bigexp"] = be.astype(ml_dtypes.bfloat16)
        am = np.zeros((D, HD), np.float32)
        for h in range(H):
            am[h * HD:(h + 1) * HD, :] = np.eye(HD) / 16.0
        d["ameanT"] = np.ascontiguousarray(am.T).astype(ml_dtypes.bfloat16)
        d["cokT"] = np.ascontiguousarray(
            np.asarray(d["cok_tm"], np.float32).T).astype(ml_dtypes.bfloat16)
    return d


def kernel(**inputs):
    n = 8
    nc = build_nc(STAGE)
    in_maps = [_shard_inputs(inputs, c) for c in range(n)]
    res = run_bass_kernel_spmd(nc, in_maps, core_ids=list(range(n)))
    out = np.stack([np.asarray(res.results[c]["out"]) for c in range(n)],
                   axis=0)
    return out.astype(np.float32)
